# version 11
# baseline (speedup 1.0000x reference)
import contextlib
import os

import numpy as np
import concourse.bass as bass
import concourse.mybir as mybir
from concourse.bass_utils import run_bass_kernel_spmd

F32 = mybir.dt.float32
BF16 = mybir.dt.bfloat16
AF = mybir.ActivationFunctionType
ALU = mybir.AluOpType
AXX = mybir.AxisListType.X

D = 1024
NDC = 8
HEADS = 16
NPAIR = 8
INNER = 4096
NIC = 32
NST = 16
MOE_I = 512
NEXP = 16
RGROUPS = [[0, 1], [2, 3], [4, 5], [6, 7]]
RMAGIC = 3 * 2.0 ** 22
EPS = 1e-5

ENGS = ("tensor", "vector", "scalar", "gpsimd", "sync")


class Plan:

    SELF_SERIAL = ("vector", "scalar", "gpsimd")

    def __init__(self):
        self.stream = {e: [] for e in ENGS}
        self.cnt = {}
        self.dirty = {e: None for e in ENGS}
        self.dma_sems = set()

    def do(self, eng, fn, dma=None, cc=None):
        if cc:
            step = {"fn": fn, "incs": [(cc, 1)], "sem": cc}
            self.cnt[cc] = self.cnt.get(cc, 0) + 1
            self.dma_sems.add(cc)
        elif dma:
            sem = dma
            step = {"fn": fn, "incs": [(sem, 16)], "sem": sem}
            self.cnt[sem] = self.cnt.get(sem, 0) + 16
            self.dma_sems.add(sem)
        else:
            if eng in self.SELF_SERIAL and self.cnt.get(eng, 0) > 0:
                self.stream[eng].append({"wait": (eng, self.cnt[eng])})
            step = {"fn": fn, "incs": [(eng, 1)], "sem": eng}
            self.cnt[eng] = self.cnt.get(eng, 0) + 1
        step["tok"] = (step["sem"], self.cnt[step["sem"]])
        self.stream[eng].append(step)
        self.dirty[eng] = step
        return step

    def mark(self, eng, sem=None):
        step = self.dirty[eng]
        assert step is not None, f"mark({eng}) with no dirty step"
        return step["tok"]

    def tok(self, sem):
        return (sem, self.cnt.get(sem, 0))

    def wait(self, eng, token):
        if token is None:
            return
        sem, val = token
        if val <= 0:
            return
        self.stream[eng].append({"wait": (sem, val)})

    def barrier(self):
        toks = []
        for e in ENGS:
            if self.dirty[e] is not None:
                toks.append(self.dirty[e]["tok"])
                self.dirty[e] = None
        for d in sorted(self.dma_sems):
            if self.cnt.get(d, 0) > 0:
                toks.append(self.tok(d))
        for e in ENGS:
            for t in toks:
                self.wait(e, t)

    def check(self):
        pos = {e: 0 for e in ENGS}
        sem = {}
        progressed = True
        while progressed:
            progressed = False
            for e in ENGS:
                items = self.stream[e]
                while pos[e] < len(items):
                    it = items[pos[e]]
                    if "wait" in it:
                        sm, v = it["wait"]
                        if sem.get(sm, 0) < v:
                            break
                    else:
                        for (sm, k) in it["incs"]:
                            sem[sm] = sem.get(sm, 0) + k
                    pos[e] += 1
                    progressed = True
        stuck = {e: (pos[e], len(self.stream[e])) for e in ENGS if pos[e] < len(self.stream[e])}
        if stuck:
            det = {}
            for e in stuck:
                it = self.stream[e][pos[e]]
                det[e] = ("waits", it["wait"], "have", sem.get(it["wait"][0], 0)) if "wait" in it else "??"
            raise RuntimeError(f"PLAN DEADLOCK: {stuck} {det}")

    def replay(self, blk, sems):
        for e in ENGS:
            items = self.stream[e]

            def run(eng, items=items):
                prev_wait = None
                for it in items:
                    if "wait" in it:
                        if it["wait"] == prev_wait:
                            continue
                        prev_wait = it["wait"]
                        sm, v = it["wait"]
                        eng.wait_ge(sems[sm], v)
                    else:
                        instr = it["fn"](eng)
                        for (sm, k) in it["incs"]:
                            instr.then_inc(sems[sm], k)
            getattr(blk, e)(run)


class Arena:
    def __init__(self, n, name):
        self.n, self.name = n, name
        self.off = 0
        self.marks = []
        self.peak = 0

    def al(self, cols, what=""):
        o = self.off
        self.off += cols
        self.peak = max(self.peak, self.off)
        assert self.off <= self.n, f"{self.name} overflow at {what}: {self.off}>{self.n}"
        return o

    def push(self):
        self.marks.append(self.off)

    def pop(self):
        self.off = self.marks.pop()


class ColView:

    def __init__(self, t, off):
        self.t, self.off = t, off

    def __getitem__(self, key):
        p, c = key
        return self.t[p, c.start + self.off: c.stop + self.off]

    @property
    def dtype(self):
        return self.t[:, 0:1].dtype


DBG = {}


def build(TC=2048, upto=2, taps=()):
    NT = max(1, TC // 512)
    TCH = TC // NT
    NSUB = TC // 128
    nc = bass.Bass("TRN2", target_bir_lowering=False, debug=False)

    io = {}

    def EI(name, shape):
        io[name] = nc.dram_tensor(name, list(shape), F32, kind="ExternalInput")

    EI("x_fm", (D, TC))
    EI("wqkv", (D, 3 * D)); EI("rfblk", (NPAIR, 128, 128))
    EI("wout", (D, D)); EI("bout", (128, NDC))
    EI("wg_t", (D, INNER)); EI("wu_t", (D, INNER)); EI("wd_t", (INNER, D))
    EI("lng", (128, NDC)); EI("lnb", (128, NDC))
    EI("w1", (D, D)); EI("b1", (128, NDC))
    EI("v1", (D, D)); EI("bv", (128, NDC))
    EI("w2", (D, D)); EI("b2", (128, NDC))
    EI("wdt", (D, D)); EI("bdt", (128, NDC))
    EI("convw", (128, NDC * 3)); EI("alog", (128, NDC * NST)); EI("dskip", (128, NDC))
    EI("wb", (D, NST)); EI("wc", (D, NST))
    EI("gatew_t", (D, NEXP)); EI("ebias", (1, NEXP))
    EI("gp_t", (NEXP, D, MOE_I)); EI("up_t", (NEXP, D, MOE_I)); EI("dp_t", (NEXP, MOE_I, D))
    EI("sg_t", (D, MOE_I)); EI("su_t", (D, MOE_I)); EI("sd_t", (MOE_I, D))
    EI("meta", (128, 4))

    out_fm = nc.dram_tensor("out_fm", [D, TC], F32, kind="ExternalOutput")
    tapd = {}
    for tname, shape, _ in taps:
        tapd[tname] = nc.dram_tensor("tap_" + tname, list(shape), F32, kind="ExternalOutput")

    cc_kv_in = nc.dram_tensor("cc_kv_in", [64, HEADS * 65], F32)
    cc_kv_out = nc.dram_tensor("cc_kv_out", [64, HEADS * 65], F32)
    cc_edge_in = nc.dram_tensor("cc_edge_in", [128, NDC * 2], F32)
    cc_edge_out = nc.dram_tensor("cc_edge_out", [2 * 128, NDC * 2], F32)
    cc_dtA_in = nc.dram_tensor("cc_dtA_in", [2, D, TC // 2], BF16)
    cc_dtA_out = nc.dram_tensor("cc_dtA_out", [2, 2, D, TC // 2], BF16)
    cc_dtB_in = nc.dram_tensor("cc_dtB_in", [2, D, TC // 2], BF16)
    cc_dtB_out = nc.dram_tensor("cc_dtB_out", [2, 2, D, TC // 2], BF16)
    cc_bc_in = nc.dram_tensor("cc_bc_in", [2, NST, TC], BF16)
    cc_bc_out = nc.dram_tensor("cc_bc_out", [2, 2, NST, TC], BF16)
    cc_yA_in = nc.dram_tensor("cc_yA_in", [512, TC], BF16)
    cc_yA_out = nc.dram_tensor("cc_yA_out", [D, TC], BF16)
    cc_yB_in = nc.dram_tensor("cc_yB_in", [512, TC], BF16)
    cc_yB_out = nc.dram_tensor("cc_yB_out", [D, TC], BF16)
    wq_g = nc.dram_tensor("wq_g", [D, INNER], BF16)
    rb_scr = nc.dram_tensor("rb_scr", [64, 128], F32)
    wq_u = nc.dram_tensor("wq_u", [D, INNER], BF16)
    wq_d = nc.dram_tensor("wq_d", [INNER, D], BF16)
    wsc_scr = nc.dram_tensor("wsc_scr", [NEXP, TC], BF16)

    P = Plan()

    with contextlib.ExitStack() as stx:
        def SB(name, shape, dtype):
            return stx.enter_context(nc.sbuf_tensor(name, list(shape), dtype))

        def PSUM(name, shape, dtype=F32):
            return stx.enter_context(nc.psum_tensor(name, list(shape), dtype))

        x_sb = SB("x_sb", (128, NDC * TC), F32)
        ABN = max(TC * 28 + 1024, 36000) if TC <= 1024 else 55296
        AFN = 8064 if TC >= 2048 else TC * 4 + 3072
        aB = SB("aB", (128, ABN), BF16)
        aF = SB("aF", (128, AFN), F32)
        cst = SB("cst", (128, 244), F32)
        ones_b = SB("ones_b", (128, 2), BF16)
        rowones_f = SB("rowones_f", (1, 128), F32)
        selp = SB("selp", (16, 128 * NPAIR), BF16)
        identb = SB("identb", (128, 128), BF16)
        identf = SB("identf", (128, 128), F32)
        identf_ap = [None]
        dbgc = SB("dbgc", (1, 16), F32)
        P.do("vector", lambda e: e.memset(dbgc[:, :], 0.0))

        pb = [PSUM(f"pb{i}", (128, 512), F32) for i in range(8)]
        pT = pb[7][:, :].bitcast(BF16)


        def dma(queue, dst, src, st="ld0"):
            return P.do(queue, lambda e, d=dst, s=src: e.dma_start(out=d, in_=s), dma=st)

        def mm(dst, lhsT, rhs, start, stop):
            return P.do("tensor", lambda e, d=dst, l=lhsT, r=rhs, st=start, sp=stop:
                        e.matmul(d, l, r, start=st, stop=sp, skip_group_check=True))

        def act(eng, dst, src, func, bias=0.0, scale=1.0):
            return P.do(eng, lambda e, d=dst, s=src, f=func, b=bias, sc=scale:
                        e.activation(d, s, f, bias=b, scale=sc))

        def tt(eng, dst, a, b, op):
            return P.do(eng, lambda e, d=dst, x=a, y=b, o=op: e.tensor_tensor(d, x, y, o))

        def ts(eng, dst, a, s1, s2, op0, op1=ALU.bypass):
            return P.do(eng, lambda e, d=dst, x=a, u=s1, v=s2, o0=op0, o1=op1:
                        e.tensor_scalar(d, x, u, v, o0, o1))

        def stt(eng, dst, in0, scal, in1, op0, op1):
            return P.do(eng, lambda e, d=dst, a=in0, s=scal, b=in1, o0=op0, o1=op1:
                        e.scalar_tensor_tensor(d, a, s, b, o0, o1))

        def cp(eng, dst, src):
            return P.do(eng, lambda e, d=dst, s=src: e.tensor_copy(d, s))

        def recip(eng, dst, src):
            return P.do(eng, lambda e, d=dst, s=src: e.reciprocal(d, s))

        def transpose(dst_psum, src, ident):
            return P.do("tensor", lambda e, d=dst_psum, s=src, i=ident: e.transpose(d, s, i))

        bank_free = [None] * 8
        bank_ready = [None] * 8

        def mm_bank(bank, mms):
            if bank_free[bank] is not None:
                P.wait("tensor", bank_free[bank])
                bank_free[bank] = None
            n = len(mms)
            for i, (l, r, dap) in enumerate(mms):
                mm(dap if dap is not None else pb[bank][:, 0:TCH], l, r, i == 0, i == n - 1)
            bank_ready[bank] = P.mark("tensor")
            return bank_ready[bank]

        def evac(bank, eng, emit):
            P.wait(eng, bank_ready[bank])
            emit()
            bank_free[bank] = P.mark(eng)

        arB = Arena(ABN, "aB")
        arF = Arena(AFN, "aF")

        dma("sync", x_sb[:, :].rearrange("p (dc t) -> p dc t", dc=NDC),
            io["x_fm"].rearrange("(dc p) t -> p dc t", p=128))
        CST = {}
        _off = [0]
        for nm, w in [("bout", NDC), ("b1", NDC), ("bv", NDC), ("b2", NDC), ("bdt", NDC),
                      ("lng", NDC), ("lnb", NDC), ("convw", NDC * 3), ("dskip", NDC),
                      ("alog", NDC * NST), ("meta", 4)]:
            CST[nm] = _off[0]
            dma("sync", cst[:, _off[0]:_off[0] + w], io[nm][:, :])
            _off[0] += w

        def cstc(name, i=0):
            return cst[:, CST[name] + i: CST[name] + i + 1]

        P.do("vector", lambda e: e.memset(ones_b[:, :], 1.0))
        P.do("vector", lambda e: e.memset(rowones_f[:, :], 1.0))
        _fc = [0]
        FCBASE = 228

        def fconst(v):
            col = FCBASE + _fc[0]
            _fc[0] += 1
            assert col < 256
            P.do("vector", lambda e, c=col, vv=v: e.memset(cst[:, c:c + 1], vv))
            nc.const_aps.register(F32, float(v), cst[:, col:col + 1])
            return float(v)

        for _v in (EPS, EPS * EPS, 1e-6, 1.0, RMAGIC):
            fconst(_v)
        SKIP_CONSTS = os.environ.get("KSKIP", "")
        if "id" not in SKIP_CONSTS:
            P.do("gpsimd", lambda e: e.memset(identb[:, :], 0.0))
        if "id" not in SKIP_CONSTS:
            P.wait("gpsimd", P.mark("gpsimd"))
            P.do("gpsimd", lambda e: e.affine_select(
                out=identb[:, :], in_=identb[:, :], compare_op=ALU.not_equal,
                fill=1.0, base=0, pattern=[[-1, 128]], channel_multiplier=1))
            P.do("gpsimd", lambda e: e.memset(identf[:, :], 0.0))
            P.wait("gpsimd", P.mark("gpsimd"))
            P.do("gpsimd", lambda e: e.affine_select(
                out=identf[:, :], in_=identf[:, :], compare_op=ALU.not_equal,
                fill=1.0, base=0, pattern=[[-1, 128]], channel_multiplier=1))
        if "selp" not in SKIP_CONSTS:
            P.do("gpsimd", lambda e: e.iota(aF[0:16, 0:128], pattern=[[1, 128]], base=0,
                                            channel_multiplier=0, allow_small_or_imprecise_dtypes=True))
            P.do("gpsimd", lambda e: e.iota(aF[0:16, 128:256], pattern=[[0, 128]], base=0,
                                            channel_multiplier=1, allow_small_or_imprecise_dtypes=True))
        P.barrier()
        if "selp" not in SKIP_CONSTS:
            ts("vector", aF[0:16, 0:128], aF[0:16, 0:128], 64.0, None, ALU.is_ge)
            tt("vector", aF[0:16, 128:256], aF[0:16, 128:256], aF[0:16, 0:128], ALU.subtract)
            for hp in range(NPAIR):
                ts("vector", selp[:, hp * 128:(hp + 1) * 128], aF[0:16, 128:256],
                   float(2 * hp), None, ALU.is_equal)
        if "aexp" not in SKIP_CONSTS:
            a0 = CST["alog"]
            act("scalar", cst[:, a0:a0 + NDC * NST], cst[:, a0:a0 + NDC * NST], AF.Exp)
            P.wait("vector", P.mark("scalar"))
            ts("vector", cst[:, a0:a0 + NDC * NST], cst[:, a0:a0 + NDC * NST], -1.0, None, ALU.mult)
        identf_ap[0] = identf[:, :]
        P.barrier()

        def layer_norm(dst_bf, inner_gb=False):
            arB.push(); arF.push()
            sq = [arB.al(TCH, "ln_sq"), arB.al(TCH, "ln_sq2")]
            xb = [arB.al(TCH, "ln_xb"), arB.al(TCH, "ln_xb2")]
            rb = arF.al(TC, "rstd_b")
            mb = arF.al(TC, "murstd_b")
            tmpc = arF.al(TCH, "ln_tmp")
            ro = arF.al(5 * 64, "ln_rows")
            arF.off -= 5 * 64
            ro = arF.al(5 * TCH, "ln_rows")
            mu = aF[0:1, ro:ro + TCH]
            va = aF[0:1, ro + TCH:ro + 2 * TCH]
            r3 = aF[0:1, ro + 2 * TCH:ro + 3 * TCH]
            mr = aF[0:1, ro + 3 * TCH:ro + 4 * TCH]
            sc1 = aF[0:1, ro + 4 * TCH:ro + 5 * TCH]
            mm_toks = {}
            for tc in range(NT):
                if bank_free[5] is not None:
                    P.wait("tensor", bank_free[5]); bank_free[5] = None
                if bank_free[6] is not None:
                    P.wait("tensor", bank_free[6]); bank_free[6] = None
                for dc in range(NDC):
                    par = dc % 2
                    xs = x_sb[:, dc * TC + tc * TCH: dc * TC + (tc + 1) * TCH]
                    prev = mm_toks.get(par)
                    if prev is not None:
                        P.wait("vector", prev); P.wait("scalar", prev)
                    cp("vector", aB[:, xb[par]:xb[par] + TCH], xs)
                    tv = P.mark("vector")
                    act("scalar", aB[:, sq[par]:sq[par] + TCH], xs, AF.Square)
                    tsc = P.mark("scalar")
                    P.wait("tensor", tv); P.wait("tensor", tsc)
                    mm(pb[5][0:1, 0:TCH], ones_b[:, 0:1], aB[:, xb[par]:xb[par] + TCH], dc == 0, dc == NDC - 1)
                    mm(pb[6][0:1, 0:TCH], ones_b[:, 0:1], aB[:, sq[par]:sq[par] + TCH], dc == 0, dc == NDC - 1)
                    mm_toks[par] = P.mark("tensor")
                tmm = P.tok("tensor")
                P.wait("vector", tmm)
                ts("vector", mu, pb[5][0:1, 0:TCH], 1.0 / D, None, ALU.mult)
                ts("vector", va, pb[6][0:1, 0:TCH], 1.0 / D, None, ALU.mult)
                tfree = P.mark("vector")
                bank_free[5] = tfree; bank_free[6] = tfree
                tt("vector", sc1, mu, mu, ALU.mult)
                tt("vector", va, va, sc1, ALU.subtract)
                tva = P.mark("vector")
                P.wait("scalar", tva)
                if inner_gb:
                    act("scalar", r3, va, AF.Ln, bias=EPS * EPS, scale=1.0 + EPS)
                else:
                    act("scalar", r3, va, AF.Ln, bias=EPS)
                act("scalar", r3, r3, AF.Exp, scale=-0.5)
                trs = P.mark("scalar")
                P.wait("vector", trs)
                tt("vector", mr, mu, r3, ALU.mult)
                tstat = P.mark("vector")
                P.wait("tensor", tstat)
                for (row, dsto) in ((r3, rb), (mr, mb)):
                    mm_bank(4, [(rowones_f[:, :], row, None)])

                    def _e(tc=tc, dsto=dsto):
                        cp("vector", aF[:, dsto + tc * TCH: dsto + (tc + 1) * TCH], pb[4][:, 0:TCH])
                    evac(4, "vector", _e)
            for dc in range(NDC):
                for tc in range(NT):
                    lo = dc * TC + tc * TCH
                    hi = lo + TCH
                    tt("vector", aF[:, tmpc:tmpc + TCH], x_sb[:, lo:hi],
                       aF[:, rb + tc * TCH: rb + (tc + 1) * TCH], ALU.mult)
                    if inner_gb:
                        tt("vector", aF[:, tmpc:tmpc + TCH], aF[:, tmpc:tmpc + TCH],
                           aF[:, mb + tc * TCH: mb + (tc + 1) * TCH], ALU.subtract)
                        ts("vector", dst_bf[:, lo:hi], aF[:, tmpc:tmpc + TCH],
                           cstc("lng", dc), cstc("lnb", dc), ALU.mult, ALU.add)
                    else:
                        tt("vector", dst_bf[:, lo:hi], aF[:, tmpc:tmpc + TCH],
                           aF[:, mb + tc * TCH: mb + (tc + 1) * TCH], ALU.subtract)
            arB.pop(); arF.pop()

        def l0_attn():
            arB.push(); arF.push()
            xn_o = arB.al(NDC * TC, "xn1")
            xn = ColView(aB, xn_o)
            layer_norm(xn)
            P.wait("tensor", P.tok("vector"))
            phi_q = arB.al(NPAIR * TC, "phi_q")
            phi_kg = arB.al(4 * TC, "phi_kg")
            rfb = arB.al(NPAIR * 128, "rfblk")
            wst = [arB.al(2 * NDC * 128, "wqk0"), arB.al(2 * NDC * 128, "wqk1")]
            qk = arB.al(TC, "qk")
            vv = arB.al(512, "vT")
            pkT = arB.al(512, "phikT")
            wvm = arB.al(NDC * 512, "wv_moving")
            pairkv = qk
            kvcols = arF.al(1040, "kvcols")
            nrmtm_o = arF.al(NSUB * 16, "nrmtm")
            nrmtm = aF[:, nrmtm_o:nrmtm_o + NSUB * 16]
            nrow_o = arB.al(TC, "nrow")
            nrow = aB[0:16, nrow_o:nrow_o + TC]
            dma("gpsimd", aB[:, rfb:rfb + NPAIR * 128].rearrange("p (h c) -> p h c", h=NPAIR),
                io["rfblk"].rearrange("h p c -> p h c"), st="lrf")
            trf = P.tok("lrf")

            wst_guard = [None, None]
            qk_guard = None
            wv_guard = None
            for grp in range(2):
                for lp in range(4):
                    hp = grp * 4 + lp
                    slot = lp % 2
                    ws = wst[slot]
                    if wst_guard[slot] is not None:
                        P.wait("gpsimd", wst_guard[slot])
                    for part in range(2):
                        col0 = part * D + hp * 128
                        dma("gpsimd",
                            aB[:, ws + part * NDC * 128: ws + (part + 1) * NDC * 128]
                            .rearrange("p (dc c) -> p dc c", dc=NDC),
                            io["wqkv"].rearrange("(dc p) c -> p dc c", p=128)[:, :, col0:col0 + 128],
                            st=f"lw{slot}")
                    tw = P.tok(f"lw{slot}")
                    P.wait("tensor", tw)
                    if hp == 0:
                        P.wait("tensor", trf)
                    for part, pdst, po in ((0, phi_q, hp * TC), (1, phi_kg, lp * TC)):
                        if qk_guard is not None:
                            P.wait("vector", qk_guard)
                        for tc in range(NT):
                            bank = tc % 4
                            mm_bank(bank, [(aB[:, ws + (part * NDC + dc) * 128: ws + (part * NDC + dc) * 128 + 128],
                                            xn[:, dc * TC + tc * TCH: dc * TC + (tc + 1) * TCH], None)
                                           for dc in range(NDC)])

                            def _e(bank=bank, tc=tc):
                                cp("vector", aB[:, qk + tc * TCH: qk + (tc + 1) * TCH],
                                   pb[bank][:, 0:TCH])
                            evac(bank, "vector", _e)
                        tqk = P.tok("vector")
                        P.wait("tensor", tqk)
                        for tc in range(NT):
                            bank = 4 + tc % 2
                            mm_bank(bank, [(aB[:, rfb + hp * 128: rfb + (hp + 1) * 128],
                                            aB[:, qk + tc * TCH: qk + (tc + 1) * TCH],
                                            None)])

                            def _e(bank=bank, pdst=pdst, po=po, tc=tc):
                                act("scalar", aB[:, pdst + po + tc * TCH: pdst + po + (tc + 1) * TCH],
                                    pb[bank][:, 0:TCH], AF.Relu)
                            evac(bank, "scalar", _e)
                        qk_guard = P.tok("tensor")
                    wst_guard[slot] = P.tok("tensor")

                if wv_guard is not None:
                    P.wait("gpsimd", wv_guard)
                dma("gpsimd", aB[:, wvm:wvm + NDC * 512].rearrange("p (dc c) -> p dc c", dc=NDC),
                    io["wqkv"].rearrange("(dc p) c -> p dc c", p=128)
                    [:, :, 2 * D + grp * 512:2 * D + (grp + 1) * 512], st="lwv")
                twv = P.tok("lwv")
                P.wait("tensor", twv)
                tphi = P.tok("scalar")
                P.wait("tensor", tphi)
                for b_ in range(2):
                    if bank_free[b_] is not None:
                        P.wait("tensor", bank_free[b_]); bank_free[b_] = None
                tkv = None
                for sub in range(NSUB):
                    t0 = sub * 128
                    if tkv is not None:
                        P.wait("vector", tkv)
                    mm_bank(4, [(xn[:, dc * TC + t0: dc * TC + t0 + 128],
                                 aB[:, wvm + dc * 512: wvm + (dc + 1) * 512],
                                 pb[4][:, 0:512]) for dc in range(NDC)])

                    def _ev():
                        cp("vector", aB[:, vv:vv + 512], pb[4][:, 0:512])
                    evac(4, "vector", _ev)
                    if bank_free[7] is not None:
                        P.wait("tensor", bank_free[7]); bank_free[7] = None
                    for lp in range(4):
                        transpose(pT[:, lp * 128:(lp + 1) * 128],
                                  aB[:, phi_kg + lp * TC + t0: phi_kg + lp * TC + t0 + 128],
                                  identb[:, :])
                    bank_ready[7] = P.mark("tensor")

                    def _ek():
                        cp("vector", aB[:, pkT:pkT + 512], pT[:, 0:512])
                    evac(7, "vector", _ek)
                    tvv = P.tok("vector")
                    P.wait("tensor", tvv)
                    for lh in range(8):
                        bank = lh // 6
                        c0 = (lh % 6) * 65
                        lpp, ab = lh // 2, lh % 2
                        first = (sub == 0 and lh % 6 == 0)
                        statap = aB[:, pkT + lpp * 128 + ab * 64: pkT + lpp * 128 + ab * 64 + 64]
                        mm(pb[bank][0:64, c0:c0 + 64], statap, aB[:, vv + lh * 64: vv + lh * 64 + 64], first, False)
                        mm(pb[bank][0:64, c0 + 64:c0 + 65], statap, ones_b[:, 0:1], False, False)
                    tkv = P.mark("tensor")
                    bank_ready[0] = tkv
                    bank_ready[1] = tkv
                P.wait("vector", tkv)
                for bank in range(2):
                    nh = 6 if bank == 0 else 2
                    base_h = grp * 8 + bank * 6
                    cp("vector", aF[0:64, kvcols + base_h * 65: kvcols + (base_h + nh) * 65],
                       pb[bank][0:64, 0:nh * 65])
                tkvev = P.mark("vector")
                for bank in range(2):
                    bank_free[bank] = tkvev
                    bank_ready[bank] = None
                wv_guard = P.tok("tensor")

            tkvev = P.tok("vector")
            P.wait("sync", tkvev)
            dma("sync", cc_kv_in[:, :], aF[0:64, kvcols:kvcols + 1040], st="dcc")
            P.wait("gpsimd", P.tok("dcc"))
            P.do("gpsimd", lambda e: e.collective_compute(
                "AllReduce", ALU.add, replica_groups=RGROUPS,
                ins=[cc_kv_in.ap()], outs=[cc_kv_out.ap()]), cc="cck")
            P.wait("gpsimd", P.tok("cck"))
            dma("gpsimd", aF[0:64, kvcols:kvcols + 1040], cc_kv_out[:, :], st="dcc2")
            tkvr = P.tok("dcc2")

            P.wait("vector", tkvr)
            P.do("vector", lambda e: e.memset(aB[:, pairkv:pairkv + NPAIR * 130], 0.0))
            for hp in range(NPAIR):
                o = pairkv + hp * 130
                ha, hb = 2 * hp, 2 * hp + 1
                cp("vector", aB[0:64, o:o + 64], aF[0:64, kvcols + ha * 65: kvcols + ha * 65 + 64])
                cp("vector", aB[64:128, o + 64:o + 128], aF[0:64, kvcols + hb * 65: kvcols + hb * 65 + 64])
                cp("vector", aB[0:64, o + 128:o + 129], aF[0:64, kvcols + ha * 65 + 64: kvcols + ha * 65 + 65])
                cp("vector", aB[64:128, o + 129:o + 130], aF[0:64, kvcols + hb * 65 + 64: kvcols + hb * 65 + 65])
            tpair = P.mark("vector")
            P.wait("tensor", tpair)

            for g0 in range(0, NSUB, 8):
                if bank_free[6] is not None:
                    P.wait("tensor", bank_free[6]); bank_free[6] = None
                gn = min(8, NSUB - g0)
                for si in range(gn):
                    sub = g0 + si
                    for hp in range(NPAIR):
                        mm(pb[6][0:128, si * 16 + hp * 2: si * 16 + hp * 2 + 2],
                           aB[:, phi_q + hp * TC + sub * 128: phi_q + hp * TC + sub * 128 + 128],
                           aB[:, pairkv + hp * 130 + 128: pairkv + hp * 130 + 130],
                           si == 0 and hp == 0, False)
                bank_ready[6] = P.mark("tensor")

                def _e(g0=g0, gn=gn):
                    cp("vector", aF[:, nrmtm_o + g0 * 16:nrmtm_o + (g0 + gn) * 16], pb[6][:, 0:gn * 16])
                evac(6, "vector", _e)
            ts("vector", aF[:, nrmtm_o:nrmtm_o + NSUB * 16], aF[:, nrmtm_o:nrmtm_o + NSUB * 16], 1e-6, None, ALU.add)
            recip("vector", aF[:, nrmtm_o:nrmtm_o + NSUB * 16], aF[:, nrmtm_o:nrmtm_o + NSUB * 16])
            tnr = P.mark("vector")
            P.wait("tensor", tnr)
            for sub in range(NSUB):
                bank = sub % 2
                if bank_free[bank] is not None:
                    P.wait("tensor", bank_free[bank]); bank_free[bank] = None
                transpose(pb[bank][0:16, 0:128], aF[:, nrmtm_o + sub * 16:nrmtm_o + (sub + 1) * 16],
                          identf[:, :])
                bank_ready[bank] = P.mark("tensor")

                def _e(bank=bank, sub=sub):
                    cp("vector", aB[0:16, nrow_o + sub * 128:nrow_o + (sub + 1) * 128],
                       pb[bank][0:16, 0:128])
                evac(bank, "vector", _e)

            atn = xn_o
            nb = arF.al(TCH, "nrm_b")
            tnrow = P.tok("vector")
            P.wait("tensor", tnrow)
            for hp in range(NPAIR):
                for tc in range(NT):
                    bank = (hp * NT + tc) % 2
                    mm_bank(bank, [(aB[:, pairkv + hp * 130: pairkv + hp * 130 + 128],
                                    aB[:, phi_q + hp * TC + tc * TCH: phi_q + hp * TC + (tc + 1) * TCH],
                                    pb[bank][:, 0:TCH])])
                    bank2 = 2 + (hp * NT + tc) % 2
                    mm_bank(bank2, [(selp[:, hp * 128:(hp + 1) * 128],
                                     nrow[:, tc * TCH:(tc + 1) * TCH], pb[bank2][:, 0:TCH])])

                    def _e2(bank2=bank2):
                        cp("vector", aF[:, nb:nb + TCH], pb[bank2][:, 0:TCH])
                    evac(bank2, "vector", _e2)

                    def _e(bank=bank, hp=hp, tc=tc):
                        tt("vector", aB[:, atn + hp * TC + tc * TCH: atn + hp * TC + (tc + 1) * TCH],
                           pb[bank][:, 0:TCH], aF[:, nb:nb + TCH], ALU.mult)
                    evac(bank, "vector", _e)
            tatn = P.tok("vector")
            P.wait("tensor", tatn)
            woguard = [None, None]
            for oc in range(NDC):
                slot = oc % 2
                ws = wst[slot]
                if woguard[slot] is not None:
                    P.wait("gpsimd", woguard[slot])
                P.wait("gpsimd", wst_guard[slot])
                wst_guard[slot] = None
                for dc in range(NDC):
                    dma("gpsimd", aB[:, ws + dc * 128: ws + dc * 128 + 128],
                        io["wout"][dc * 128:(dc + 1) * 128, oc * 128:(oc + 1) * 128], st=f"lw{slot}")
                two = P.tok(f"lw{slot}")
                P.wait("tensor", two)
                for tc in range(NT):
                    bank = 4 + (oc * NT + tc) % 3
                    mm_bank(bank, [(aB[:, ws + dc * 128: ws + dc * 128 + 128],
                                    aB[:, atn + dc * TC + tc * TCH: atn + dc * TC + (tc + 1) * TCH],
                                    pb[bank][:, 0:TCH])
                                   for dc in range(NDC)])

                    def _e(bank=bank, oc=oc, tc=tc):
                        stt("vector", x_sb[:, oc * TC + tc * TCH: oc * TC + (tc + 1) * TCH],
                            pb[bank][:, 0:TCH], cstc("bout", oc),
                            x_sb[:, oc * TC + tc * TCH: oc * TC + (tc + 1) * TCH],
                            ALU.add, ALU.add)
                    evac(bank, "vector", _e)
                woguard[slot] = P.tok("tensor")
            DBG.update(atn=atn, phi_q=phi_q, pairkv=pairkv, kvcols=kvcols)
            arB.pop(); arF.pop()

        def rowmax_bcast(am_off, ncols, sb_off, usb_off, smax, usc):
            nsub = ncols // 128
            arB.push(); arF.push()
            rtm = arB.al(nsub, "rowmax_tm")
            P.wait("tensor", P.tok("vector"))
            for sb_i in range(nsub):
                if bank_free[7] is not None:
                    P.wait("tensor", bank_free[7]); bank_free[7] = None
                transpose(pT[:, 0:128], aB[:, am_off + sb_i * 128: am_off + (sb_i + 1) * 128],
                          identb[:, :])
                bank_ready[7] = P.mark("tensor")

                def _e(sb_i=sb_i):
                    P.do("vector", lambda e, sb_i=sb_i: e.reduce_max(
                        aB[:, rtm + sb_i: rtm + sb_i + 1], pT[:, 0:128], AXX))
                evac(7, "vector", _e)
            srow = arF.al(2 * nsub, "srow_tm")
            ts("vector", aF[:, srow:srow + nsub], aB[:, rtm:rtm + nsub], 1e-5, None, ALU.max)
            recip("vector", aF[:, srow:srow + nsub], aF[:, srow:srow + nsub])
            ts("vector", aF[:, srow:srow + nsub], aF[:, srow:srow + nsub], smax, None, ALU.mult)
            ts("vector", aF[:, srow + nsub:srow + 2 * nsub], aB[:, rtm:rtm + nsub],
               usc, None, ALU.mult)
            tsc_ = P.mark("vector")
            P.wait("tensor", tsc_)
            bank = 6
            if bank_free[bank] is not None:
                P.wait("tensor", bank_free[bank]); bank_free[bank] = None
            transpose(pb[bank][0:2 * nsub, 0:128], aF[:, srow:srow + 2 * nsub], identf_ap[0])
            bank_ready[bank] = P.mark("tensor")
            rows = arF.al(128, "rm_rows")

            def _e3():
                cp("vector", aF[0:2 * nsub, rows:rows + 128], pb[6][0:2 * nsub, 0:128])
            evac(bank, "vector", _e3)
            trow = P.mark("vector")
            P.wait("sync", trow)
            P.wait("sync", P.tok("rbs"))
            dma("sync", rb_scr[0:2 * nsub, :], aF[0:2 * nsub, rows:rows + 128], st="rbs")
            P.wait("sync", P.tok("rbs"))
            src_s = rb_scr[0:nsub, :].unsqueeze(0).broadcast_to((128, nsub, 128))
            src_u = rb_scr[nsub:2 * nsub, :].unsqueeze(0).broadcast_to((128, nsub, 128))
            dma("sync", aF[:, sb_off:sb_off + ncols].rearrange("p (a b) -> p a b", a=nsub),
                src_s, st="rbs")
            dma("sync", aF[:, usb_off:usb_off + ncols].rearrange("p (a b) -> p a b", a=nsub),
                src_u, st="rbs")
            arB.pop(); arF.pop()
            return P.tok("rbs")

        def weight_quant():
            arF.push(); arB.push()
            wbuf = arF.al(2 * 2048, "wq_buf")
            wbB = arB.al(4 * 4096, "wq_bufB")
            accc = arF.al(12, "wq_acc")
            cells = arF.al(8, "wq_cells")
            stg = [arB.al(2048, f"wq_stg{i}") for i in range(6)]
            onef = cst[:, FCBASE + 3:FCBASE + 4]

            def wslot(par):
                if par < 2:
                    return aF[:, wbuf + par * 2048: wbuf + (par + 1) * 2048]
                return aB[:, wbB + (par - 2) * 4096: wbB + (par - 1) * 4096].bitcast(F32)

            NB = 6
            guards = [None] * NB
            sguards = [None] * NB
            mats = [("wg_t", wq_g, NDC, INNER, "swg"), ("wu_t", wq_u, NDC, INNER, "swu"),
                    ("wd_t", wq_d, INNER // 128, D, "swd")]
            for mi, (src, dst, nrow_c, w, sname) in enumerate(mats):
                ncolh = w // 2048 if w > 2048 else 1
                colw = min(w, 2048)
                P.do("vector", lambda e: e.memset(aF[:, accc:accc + 1], 0.0))
                tidx = 0
                for rc in range(nrow_c):
                    for ch in range(ncolh):
                        par = tidx % NB
                        if guards[par] is not None:
                            P.wait("gpsimd", guards[par])
                        if sguards[par % 6] is not None:
                            P.wait("gpsimd", sguards[par % 6])
                        dma("gpsimd", aB[:, stg[par % 6]:stg[par % 6] + colw],
                            io[src][rc * 128:(rc + 1) * 128, ch * 2048: ch * 2048 + colw],
                            st=f"lp{par}")
                        P.wait("vector", P.tok(f"lp{par}"))
                        P.do("vector", lambda e, par=par, colw=colw: e.reduce_sum(
                            aF[:, accc + 1 + par: accc + 2 + par],
                            aB[:, stg[par % 6]:stg[par % 6] + colw], AXX,
                            apply_absolute_value=True))
                        tt("vector", aF[:, accc:accc + 1], aF[:, accc:accc + 1],
                           aF[:, accc + 1 + par: accc + 2 + par], ALU.add)
                        guards[par] = P.mark("vector")
                        tidx += 1
                tacc = P.mark("vector")
                P.wait("tensor", tacc)
                if bank_free[6] is not None:
                    P.wait("tensor", bank_free[6]); bank_free[6] = None
                mm(pb[6][0:1, 0:1], onef, aF[:, accc:accc + 1], True, True)
                bank_ready[6] = P.mark("tensor")

                def _m():
                    cp("vector", dbgc[0:1, 8 + mi:9 + mi], pb[6][0:1, 0:1])
                    ts("vector", aF[0:1, cells:cells + 1], pb[6][0:1, 0:1],
                       1.0 / (nrow_c * 128 * w), None, ALU.mult)
                    ts("vector", aF[0:1, cells:cells + 1], aF[0:1, cells:cells + 1],
                       1e-5, None, ALU.max)
                    recip("vector", aF[0:1, cells + 1:cells + 2], aF[0:1, cells:cells + 1])
                evac(6, "vector", _m)
                tcells = P.mark("vector")
                P.wait("tensor", tcells)
                for (cidx, dst_c) in ((1, 2), (0, 4)):
                    if bank_free[6] is not None:
                        P.wait("tensor", bank_free[6]); bank_free[6] = None
                    mm(pb[6][:, 0:1], rowones_f[:, :],
                       aF[0:1, cells + cidx:cells + cidx + 1], True, True)
                    bank_ready[6] = P.mark("tensor")

                    def _b(dst_c=dst_c):
                        cp("vector", aF[:, cells + dst_c:cells + dst_c + 1], pb[6][:, 0:1])
                    evac(6, "vector", _b)
                CST[sname] = None
                DBG[sname + "_s"] = cells + 2
                DBG[sname + "_m"] = cells + 4
                mw_col = FCBASE - 6 + mi
                cp("vector", cst[:, mw_col:mw_col + 1], aF[:, cells + 4:cells + 5])
                CST["mw" + str(mi)] = mw_col
                tidx = 0
                for rc in range(nrow_c):
                    for ch in range(ncolh):
                        par = tidx % NB
                        if guards[par] is not None:
                            P.wait("sync", guards[par])
                        dma("sync", wslot(par)[:, 0:colw],
                            io[src][rc * 128:(rc + 1) * 128, ch * 2048: ch * 2048 + colw],
                            st=f"lq{par}")
                        P.wait("scalar", P.tok(f"lq{par}"))
                        bo = wslot(par)
                        P.wait("scalar", P.tok("vector"))
                        act("scalar", bo[:, 0:colw], bo[:, 0:colw], AF.Identity,
                            bias=RMAGIC, scale=aF[:, cells + 2:cells + 3])
                        P.wait("vector", P.mark("scalar"))
                        ts("vector", bo[:, 0:colw], bo[:, 0:colw],
                           RMAGIC, 1.0, ALU.subtract, ALU.min)
                        so = stg[par]
                        if sguards[par] is not None:
                            P.wait("vector", sguards[par])
                        P.do("vector", lambda e, bo=bo, colw=colw, so=so: e.tensor_scalar(
                            aB[:, so:so + colw], bo[:, 0:colw],
                            -1.0, None, ALU.max))
                        tq = P.mark("vector")
                        guards[par] = tq
                        P.wait("scalar", tq)
                        dma("scalar", dst[rc * 128:(rc + 1) * 128, ch * 2048: ch * 2048 + colw],
                            aB[:, so:so + colw], st=f"wqs{par}")
                        sguards[par] = P.tok(f"wqs{par}")
                        tidx += 1
            tqs = tuple(P.tok(f"wqs{i}") for i in range(6))
            arF.pop(); arB.pop()
            return tqs

        def l0_ffn(tq_scratch):
            arB.push(); arF.push()
            xn_o = arB.al(NDC * TC, "xn2")
            xn = ColView(aB, xn_o)
            layer_norm(xn)
            P.wait("tensor", P.tok("vector"))
            NPASS = 4 if TC >= 2048 else 2
            TP = TC // NPASS
            NTP = max(1, TP // 512)
            TPH = TP // NTP
            sxb = arF.al(TC, "sx_b")
            usxb = arF.al(TC, "usx_b")
            tmpf = arF.al(TC, "ffn_tmp")
            arB.push()
            am = arB.al(TC, "am")
            P.wait("scalar", P.tok("vector"))
            abst = arB.al(TC, "abs_tmp")
            for dc in range(NDC):
                act("scalar", aB[:, abst:abst + TC], xn[:, dc * TC:(dc + 1) * TC], AF.Abs)
                tabs = P.mark("scalar")
                P.wait("vector", tabs)
                if dc == 0:
                    cp("vector", aB[:, am:am + TC], aB[:, abst:abst + TC])
                else:
                    tt("vector", aB[:, am:am + TC], aB[:, am:am + TC],
                       aB[:, abst:abst + TC], ALU.max)
                tmx = P.mark("vector")
                P.wait("scalar", tmx)
            trb = rowmax_bcast(am, TC, sxb, usxb, 127.0, 1.0 / 127.0)
            P.wait("vector", trb)
            for dc in range(NDC):
                sl = slice(dc * TC, (dc + 1) * TC)
                tt("vector", aF[:, tmpf:tmpf + TC], xn[:, sl], aF[:, sxb:sxb + TC], ALU.mult)
                ts("vector", aF[:, tmpf:tmpf + TC], aF[:, tmpf:tmpf + TC],
                   RMAGIC, RMAGIC, ALU.add, ALU.subtract)
                tt("vector", xn[:, sl], aF[:, tmpf:tmpf + TC], aF[:, usxb:usxb + TC], ALU.mult)
            txdq = P.mark("vector")
            P.wait("tensor", txdq)
            for t_ in tq_scratch:
                P.wait("gpsimd", t_)
            arB.pop()
            if os.environ.get("KFFN", "") == "1":
                arB.pop(); arF.pop(); return

            mid = arB.al(NIC * TP, "mid")
            wsl_g = [arB.al(4096, "wqg0"), arB.al(4096, "wqg1")]
            wsl_u = [arB.al(4096, "wqu0"), arB.al(4096, "wqu1")]
            wsl_d = [arB.al(2048, "wqd0"), arB.al(2048, "wqd1")]
            amM = arB.al(TP, "amM")
            absM = arB.al(TP, "absM")
            smb = arF.al(TP, "sm_b")
            usmb = arF.al(TP, "usm_b")
            gu_guard = [None, None]

            def gu_load(grp, par):
                if gu_guard[par] is not None:
                    P.wait("gpsimd", gu_guard[par])
                    gu_guard[par] = None
                dma("gpsimd", aB[:, wsl_g[par]:wsl_g[par] + 4096]
                    .rearrange("p (dc c) -> p dc c", dc=NDC),
                    wq_g.rearrange("(dc p) c -> p dc c", p=128)[:, :, grp * 512:(grp + 1) * 512],
                    st=f"lg{par}")
                dma("gpsimd", aB[:, wsl_u[par]:wsl_u[par] + 4096]
                    .rearrange("p (dc c) -> p dc c", dc=NDC),
                    wq_u.rearrange("(dc p) c -> p dc c", p=128)[:, :, grp * 512:(grp + 1) * 512],
                    st=f"lu{par}")
                return (P.tok(f"lg{par}"), P.tok(f"lu{par}"))

            d_guard = [None, None]

            def d_load(och, grp, par):
                if d_guard[par] is not None:
                    P.wait("gpsimd", d_guard[par])
                    d_guard[par] = None
                dma("gpsimd", aB[:, wsl_d[par]:wsl_d[par] + 2048]
                    .rearrange("p (ic c) -> p ic c", ic=4),
                    wq_d.rearrange("(ic p) c -> p ic c", p=128)
                    [:, grp * 4:(grp + 1) * 4, och * 512:(och + 1) * 512], st=f"ldd{par}")
                return P.tok(f"ldd{par}")

            tdown_prev = None
            gu_t = {(0, 0): gu_load(0, 0), (0, 1): gu_load(1, 1)}
            for p in range(NPASS):
                t0 = p * TP
                if tdown_prev is not None:
                    P.wait("scalar", tdown_prev)
                    P.wait("vector", tdown_prev)
                for grp in range(8):
                    par = grp % 2
                    tg_, tu_ = gu_t[(p, grp)]
                    P.wait("tensor", tg_); P.wait("tensor", tu_)
                    for ici in range(4):
                        ic = grp * 4 + ici
                        for tch in range(NTP):
                            bg = (2 * (ici * NTP + tch)) % 4
                            bu = bg + 1
                            mm_bank(bg, [(aB[:, wsl_g[par] + dc * 512 + ici * 128:
                                                wsl_g[par] + dc * 512 + ici * 128 + 128],
                                          xn[:, dc * TC + t0 + tch * TPH: dc * TC + t0 + (tch + 1) * TPH],
                                          pb[bg][:, 0:TPH]) for dc in range(NDC)])

                            def _eg(bg=bg, ic=ic, tch=tch):
                                act("scalar", aB[:, mid + ic * TP + tch * TPH: mid + ic * TP + (tch + 1) * TPH],
                                    pb[bg][:, 0:TPH], AF.Silu, scale=cst[:, CST["mw0"]:CST["mw0"] + 1])
                            evac(bg, "scalar", _eg)
                            mm_bank(bu, [(aB[:, wsl_u[par] + dc * 512 + ici * 128:
                                                wsl_u[par] + dc * 512 + ici * 128 + 128],
                                          xn[:, dc * TC + t0 + tch * TPH: dc * TC + t0 + (tch + 1) * TPH],
                                          pb[bu][:, 0:TPH]) for dc in range(NDC)])

                            def _eu(bu=bu, ic=ic, tch=tch, p=p):
                                P.wait("vector", P.tok("scalar"))
                                o = mid + ic * TP + tch * TPH
                                P.do("vector", lambda e, bu=bu, o=o: e.scalar_tensor_tensor(
                                    aB[:, o:o + TPH], pb[bu][:, 0:TPH],
                                    cst[:, CST["mw1"]:CST["mw1"] + 1],
                                    aB[:, o:o + TPH], ALU.mult, ALU.mult))
                                tmid = P.mark("vector")
                                P.wait("scalar", tmid)
                                act("scalar", aB[:, absM + tch * TPH: absM + (tch + 1) * TPH],
                                    aB[:, o:o + TPH], AF.Abs)
                                tam = P.mark("scalar")
                                P.wait("vector", tam)
                                if ic == 0:
                                    cp("vector", aB[:, amM + tch * TPH: amM + (tch + 1) * TPH],
                                       aB[:, absM + tch * TPH: absM + (tch + 1) * TPH])
                                else:
                                    P.do("vector", lambda e, tch=tch: e.tensor_tensor(
                                        aB[:, amM + tch * TPH: amM + (tch + 1) * TPH],
                                        aB[:, amM + tch * TPH: amM + (tch + 1) * TPH],
                                        aB[:, absM + tch * TPH: absM + (tch + 1) * TPH], ALU.max))
                            evac(bu, "vector", _eu)
                    gu_guard[par] = P.tok("tensor")
                    if grp + 2 < 8:
                        gu_t[(p, grp + 2)] = gu_load(grp + 2, par)
                    elif p + 1 < NPASS:
                        gu_t[(p + 1, grp + 2 - 8)] = gu_load(grp + 2 - 8, par)
                if os.environ.get("KFFN", "") == "2":
                    continue
                trb2 = rowmax_bcast(amM, TP, smb, usmb, 127.0, 1.0 / 127.0)
                P.wait("vector", trb2)
                g_rt = {}
                for ic in range(NIC):
                    o = mid + ic * TP
                    tpo = tmpf + (ic % 2) * TP
                    if ic - 2 in g_rt:
                        P.wait("vector", g_rt[ic - 2])
                    tt("vector", aF[:, tpo:tpo + TP], aB[:, o:o + TP],
                       aF[:, smb:smb + TP], ALU.mult)
                    P.wait("gpsimd", P.mark("vector"))
                    P.do("gpsimd", lambda e, o=o, tpo=tpo: e.tensor_scalar(
                        aB[:, o:o + TP], aF[:, tpo:tpo + TP], RMAGIC, RMAGIC,
                        ALU.add, ALU.subtract))
                    g_rt[ic] = P.mark("gpsimd")
                tmq = P.mark("gpsimd")
                P.wait("tensor", tmq)
                if os.environ.get("KFFN", "") == "3":
                    continue
                d_t = {(0, 0): d_load(0, 0, 0), (0, 1): d_load(0, 1, 1)}
                for och in range(2):
                    for b_ in range(4, 8):
                        if bank_free[b_] is not None:
                            P.wait("tensor", bank_free[b_]); bank_free[b_] = None
                    for grp in range(8):
                        par = grp % 2
                        P.wait("tensor", d_t[(och, grp)])
                        for ici in range(4):
                            ic = grp * 4 + ici
                            for tch in range(NTP):
                                for j in range(4):
                                    bank = 4 + j
                                    mm(pb[bank][:, 0:TPH],
                                       aB[:, wsl_d[par] + ici * 512 + j * 128: wsl_d[par] + ici * 512 + j * 128 + 128],
                                       aB[:, mid + ic * TP + tch * TPH: mid + ic * TP + (tch + 1) * TPH],
                                       ic == 0, ic == NIC - 1)
                        d_guard[par] = P.tok("tensor")
                        if grp + 2 < 8:
                            d_t[(och, grp + 2)] = d_load(och, grp + 2, par)
                        elif och == 0:
                            d_t[(1, grp + 2 - 8)] = d_load(1, grp + 2 - 8, par)
                    tdown = P.mark("tensor")
                    for tch in range(NTP):
                        for j in range(4):
                            oc = och * 4 + j
                            bank = 4 + j
                            bank_ready[bank] = tdown
                            xsl = slice(oc * TC + t0 + tch * TPH, oc * TC + t0 + (tch + 1) * TPH)

                            def _ed(bank=bank, xsl=xsl, tch=tch):
                                P.do("vector", lambda e, bank=bank, tch=tch: e.tensor_tensor(
                                    aF[:, tmpf:tmpf + TPH], pb[bank][:, 0:TPH],
                                    aF[:, usmb + tch * TPH: usmb + (tch + 1) * TPH], ALU.mult))
                                P.do("vector", lambda e, xsl=xsl: e.scalar_tensor_tensor(
                                    x_sb[:, xsl], aF[:, tmpf:tmpf + TPH],
                                    cst[:, CST["mw2"]:CST["mw2"] + 1],
                                    x_sb[:, xsl], ALU.mult, ALU.add))
                            evac(bank, "vector", _ed)
                tdown_prev = P.tok("tensor")
            DBG.update(ffn_mid=mid, ffn_amM=amM, ffn_smb=smb, ffn_usmb=usmb, ffn_sxb=sxb, ffn_usxb=usxb)
            arB.pop(); arF.pop()

        def l1_attn():
            arB.push(); arF.push()
            QT = TC // 2
            xn_o = arB.al(NDC * TC, "xn3")
            xn = ColView(aB, xn_o)
            layer_norm(xn, inner_gb=True)
            P.wait("tensor", P.tok("vector"))
            u_o = arB.al(NDC * TC, "u")
            v_o = arB.al(NDC * TC, "v")
            arB.push()
            wstL = [arB.al(NDC * 128, "wL0"), arB.al(NDC * 128, "wL1")]
            raw_o = wstL[0]
            stg_o = arB.al(2 * TCH, "l1stage")
            bstg_o = arB.al(TCH, "bcstage")
            edgb = arB.al(32, "edges_b")
            est = arF.al(16, "edge_stage")
            edges = arF.al(32, "edges")
            x1row = arF.al(TC + 2, "x1row")
            dtc = arF.al(TCH, "dtc")
            halo = arF.al(4, "halo")

            for dc in range(NDC):
                cp("vector", aF[:, est + dc * 2: est + dc * 2 + 1], xn[:, dc * TC: dc * TC + 1])
                cp("vector", aF[:, est + dc * 2 + 1: est + dc * 2 + 2],
                   xn[:, dc * TC + TC - 1: dc * TC + TC])
            P.wait("sync", P.mark("vector"))
            dma("sync", cc_edge_in[:, :], aF[:, est:est + 16], st="deg")
            P.wait("gpsimd", P.tok("deg"))
            P.do("gpsimd", lambda e: e.collective_compute(
                "AllGather", ALU.bypass, replica_groups=RGROUPS,
                ins=[cc_edge_in.ap()], outs=[cc_edge_out.ap()]), cc="cce")
            P.wait("gpsimd", P.tok("cce"))
            dma("gpsimd", aF[:, edges:edges + 16], cc_edge_out[0:128, :], st="deg2")
            dma("gpsimd", aF[:, edges + 16:edges + 32], cc_edge_out[128:256, :], st="deg2")
            P.wait("vector", P.tok("deg2"))
            cp("vector", aB[:, edgb:edgb + 32], aF[:, edges:edges + 32])
            tedgb = P.mark("vector")

            wguard = [None, None]

            def wload(name, oc):
                slot = oc % 2
                ws = wstL[slot]
                if wguard[slot] is not None:
                    P.wait("gpsimd", wguard[slot])
                dma("gpsimd", aB[:, ws:ws + NDC * 128].rearrange("p (dc c) -> p dc c", dc=NDC),
                    io[name].rearrange("(dc p) c -> p dc c", p=128)[:, :, oc * 128:(oc + 1) * 128],
                    st=f"lwa{slot}")
                P.wait("tensor", P.tok(f"lwa{slot}"))
                return slot, ws

            for oc in range(NDC):
                slot, ws = wload("w1", oc)
                if oc == 0:
                    P.wait("tensor", tedgb)
                mm_bank(5, [(aB[:, ws + dc * 128: ws + dc * 128 + 128],
                             aB[:, edgb + 2 * dc + 1: edgb + 2 * dc + 2],
                             pb[5][:, 0:1]) for dc in range(NDC)] +
                           [(aB[:, ws + dc * 128: ws + dc * 128 + 128],
                             aB[:, edgb + 16 + 2 * dc: edgb + 16 + 2 * dc + 1],
                             pb[5][:, 1:2]) for dc in range(NDC)])

                def _eh(oc=oc):
                    ts("vector", aF[:, halo:halo + 2], pb[5][:, 0:2], cstc("b1", oc),
                       None, ALU.add)
                    tt("vector", aF[:, halo:halo + 2], aF[:, halo:halo + 2],
                       cst[:, CST["meta"]:CST["meta"] + 2], ALU.mult)
                    cp("vector", aF[:, x1row:x1row + 1], aF[:, halo:halo + 1])
                    cp("vector", aF[:, x1row + TC + 1:x1row + TC + 2],
                       aF[:, halo + 1:halo + 2])
                evac(5, "vector", _eh)
                for tc in range(NT):
                    bank = tc % 4
                    mm_bank(bank, [(aB[:, ws + dc * 128: ws + dc * 128 + 128],
                                    xn[:, dc * TC + tc * TCH: dc * TC + (tc + 1) * TCH],
                                    pb[bank][:, 0:TCH]) for dc in range(NDC)])

                    def _ex(bank=bank, oc=oc, tc=tc):
                        act("scalar", aF[:, x1row + 1 + tc * TCH: x1row + 1 + (tc + 1) * TCH],
                            pb[bank][:, 0:TCH], AF.Identity, bias=cstc("b1", oc))
                    evac(bank, "scalar", _ex)
                P.wait("vector", P.tok("scalar"))
                thalo = P.tok("vector")
                for tc in range(NT):
                    lo = x1row + tc * TCH
                    P.wait("vector", P.tok("scalar"))
                    ts("vector", aF[:, dtc:dtc + TCH], aF[:, lo:lo + TCH],
                       cstc("convw", oc * 3 + 0), None, ALU.mult)
                    stt("vector", aF[:, dtc:dtc + TCH], aF[:, lo + 1:lo + 1 + TCH],
                        cstc("convw", oc * 3 + 1), aF[:, dtc:dtc + TCH], ALU.mult, ALU.add)
                    stt("vector", aF[:, dtc:dtc + TCH], aF[:, lo + 2:lo + 2 + TCH],
                        cstc("convw", oc * 3 + 2), aF[:, dtc:dtc + TCH], ALU.mult, ALU.add)
                    P.wait("scalar", P.mark("vector"))
                    act("scalar", aB[:, u_o + oc * TC + tc * TCH: u_o + oc * TC + (tc + 1) * TCH],
                        aF[:, dtc:dtc + TCH], AF.Silu)
                P.wait("scalar", P.tok("vector"))
                wguard[slot] = P.tok("tensor")

            P.wait("tensor", P.tok("scalar"))
            for oc in range(NDC):
                slot, ws = wload("wdt", oc)
                for tc in range(NT):
                    bank = tc % 4
                    mm_bank(bank, [(aB[:, ws + dc * 128: ws + dc * 128 + 128],
                                    aB[:, u_o + dc * TC + tc * TCH: u_o + dc * TC + (tc + 1) * TCH],
                                    pb[bank][:, 0:TCH]) for dc in range(NDC)])

                    def _ed(bank=bank, oc=oc, tc=tc):
                        P.wait("scalar", P.tok("vector"))
                        act("scalar", aF[:, dtc:dtc + TCH], pb[bank][:, 0:TCH],
                            AF.Exp, bias=cstc("bdt", oc))
                        act("scalar", aF[:, dtc:dtc + TCH], aF[:, dtc:dtc + TCH],
                            AF.Ln, bias=1.0)
                        P.wait("vector", P.mark("scalar"))
                        P.wait("vector", P.tok("sdt0"))
                        P.wait("vector", P.tok("sdt1"))
                        cp("vector", aB[:, stg_o:stg_o + TCH], aF[:, dtc:dtc + TCH])
                        tt("vector", aB[:, stg_o + TCH:stg_o + 2 * TCH], aF[:, dtc:dtc + TCH],
                           aB[:, u_o + oc * TC + tc * TCH: u_o + oc * TC + (tc + 1) * TCH],
                           ALU.mult)
                        P.wait("sync", P.mark("vector"))
                        tmid = TC // 2
                        ro = oc * 128
                        g0, g1 = tc * TCH, (tc + 1) * TCH
                        for (lo_, hi_, cc_h) in ((g0, min(g1, tmid), cc_dtA_in),
                                                 (max(g0, tmid), g1, cc_dtB_in)):
                            if lo_ >= hi_:
                                continue
                            co = lo_ - (0 if cc_h is cc_dtA_in else tmid)
                            so_ = lo_ - g0
                            dma("sync", cc_h[0, ro:ro + 128, co:co + (hi_ - lo_)],
                                aB[:, stg_o + so_:stg_o + so_ + (hi_ - lo_)], st="sdt0")
                            dma("sync", cc_h[1, ro:ro + 128, co:co + (hi_ - lo_)],
                                aB[:, stg_o + TCH + so_:stg_o + TCH + so_ + (hi_ - lo_)],
                                st="sdt1")
                    evac(bank, "scalar", _ed)
                wguard[slot] = P.tok("tensor")

            for bci, wname in ((0, "wb"), (1, "wc")):
                if wguard[0] is not None:
                    P.wait("gpsimd", wguard[0]); wguard[0] = None
                dma("gpsimd", aB[:, wstL[0]:wstL[0] + NDC * 16]
                    .rearrange("p (dc c) -> p dc c", dc=NDC),
                    io[wname].rearrange("(dc p) c -> p dc c", p=128), st="lwa0")
                P.wait("tensor", P.tok("lwa0"))
                for tc in range(NT):
                    bank = 4 + tc % 2
                    mm_bank(bank, [(aB[:, wstL[0] + dc * 16: wstL[0] + dc * 16 + 16],
                                    aB[:, u_o + dc * TC + tc * TCH: u_o + dc * TC + (tc + 1) * TCH],
                                    pb[bank][0:16, 0:TCH]) for dc in range(NDC)])

                    def _eb(bank=bank, bci=bci, tc=tc):
                        P.wait("vector", P.tok("sbc"))
                        cp("vector", aB[0:16, bstg_o:bstg_o + TCH], pb[bank][0:16, 0:TCH])
                        P.wait("sync", P.mark("vector"))
                        dma("sync", cc_bc_in[bci, :, tc * TCH:(tc + 1) * TCH],
                            aB[0:16, bstg_o:bstg_o + TCH], st="sbc")
                    evac(bank, "vector", _eb)
                wguard[0] = P.tok("tensor")

            P.wait("gpsimd", P.tok("sdt0"))
            P.wait("gpsimd", P.tok("sdt1"))
            P.wait("gpsimd", P.tok("sbc"))
            P.do("gpsimd", lambda e: e.collective_compute(
                "AllGather", ALU.bypass, replica_groups=RGROUPS,
                ins=[cc_bc_in.ap()], outs=[cc_bc_out.ap()]), cc="ccg")
            tgathBC = P.tok("ccg")
            P.do("gpsimd", lambda e: e.collective_compute(
                "AllGather", ALU.bypass, replica_groups=RGROUPS,
                ins=[cc_dtA_in.ap()], outs=[cc_dtA_out.ap()]), cc="ccg")
            tgathA = P.tok("ccg")
            P.do("gpsimd", lambda e: e.collective_compute(
                "AllGather", ALU.bypass, replica_groups=RGROUPS,
                ins=[cc_dtB_in.ap()], outs=[cc_dtB_out.ap()]), cc="ccg")
            tgath = P.tok("ccg")

            for oc in range(NDC):
                slot, ws = wload("v1", oc)
                for tc in range(NT):
                    bank = tc % 4
                    mm_bank(bank, [(aB[:, ws + dc * 128: ws + dc * 128 + 128],
                                    xn[:, dc * TC + tc * TCH: dc * TC + (tc + 1) * TCH],
                                    pb[bank][:, 0:TCH]) for dc in range(NDC)])

                    def _ev(bank=bank, oc=oc, tc=tc):
                        act("scalar", aB[:, v_o + oc * TC + tc * TCH: v_o + oc * TC + (tc + 1) * TCH],
                            pb[bank][:, 0:TCH], AF.Silu, bias=cstc("bv", oc))
                    evac(bank, "scalar", _ev)
                wguard[slot] = P.tok("tensor")



            P.wait("gpsimd", tgath)
            dt_o = xn_o
            dtu_o = xn_o + 4 * QT
            y_o = xn_o + 8 * QT
            bcB_o = xn_o + 12 * QT
            bcC_o = xn_o + 13 * QT
            inp_o = xn_o + 14 * QT
            h0_o = xn_o + 15 * QT
            arF.push()
            dAs = [arF.al(QT, "dA0"), arF.al(QT, "dA1")]
            car_o = arF.al(64, "carry")
            bc2f = arF.al(QT, "bc2")
            tm2f = arF.al(QT, "tmp2")
            h2f = arF.al(QT // 2, "h2")
            in2f = arF.al(QT // 2, "inp2")
            aBC2 = aF[:, bc2f:bc2f + QT].bitcast(BF16)
            aTM2 = aF[:, tm2f:tm2f + QT].bitcast(BF16)
            inpS = [aB[:, inp_o:inp_o + QT], aF[:, in2f:in2f + QT // 2].bitcast(BF16)]
            bcB = [aB[:, bcB_o:bcB_o + QT], aBC2[:, 0:QT]]
            bcC = [aB[:, bcC_o:bcC_o + QT], aBC2[:, QT:2 * QT]]
            bcBC = [aB[:, bcB_o:bcB_o + 2 * QT].rearrange("p (b t) -> p b t", b=2),
                    aBC2[:, 0:2 * QT].rearrange("p (b t) -> p b t", b=2)]
            tmp = [aTM2[:, 0:QT], aTM2[:, QT:2 * QT]]
            hb = [aB[:, h0_o:h0_o + QT], aF[:, h2f:h2f + QT // 2].bitcast(BF16)]
            m_lo = cstc("meta", 1)
            m_hi = cstc("meta", 0)
            bc_guard = [None, None]

            def bc_load(q_, n_):
                par = n_ % 2
                s_, hf_ = q_ // 2, q_ % 2
                if bc_guard[par] is not None:
                    P.wait("gpsimd", bc_guard[par])
                    bc_guard[par] = None
                dma("gpsimd", bcBC[par],
                    cc_bc_out[s_, :, n_, hf_ * QT:(hf_ + 1) * QT]
                    .unsqueeze(0).broadcast_to((128, 2, QT)), st=f"lbc{par}")
                return P.tok(f"lbc{par}")

            P.wait("gpsimd", P.tok("scalar"))
            P.wait("gpsimd", tgathBC)
            tbc_next = bc_load(0, 0)
            P.wait("sync", P.tok("scalar"))
            for q in range(4):
                s, hf = q // 2, q % 2
                tsrc = cc_dtA_out if hf == 0 else cc_dtB_out
                tgq = tgathA if hf == 0 else tgath
                P.wait("gpsimd", tgq)
                P.wait("gpsimd", P.tok("scalar"))
                P.wait("gpsimd", P.tok("vector"))
                for mi, dsto in ((0, dt_o), (1, dtu_o)):
                    dma("gpsimd", aB[:, dsto:dsto + 4 * QT]
                        .rearrange("p (dcl t) -> p dcl t", dcl=4),
                        tsrc[s, mi].rearrange("(dcl p) t -> p dcl t", p=128)[:, 0:4, 0:QT],
                        st="ldt")
                t_lo = P.tok("ldt")
                P.wait("sync", tgq)
                P.wait("sync", P.tok("scalar"))
                P.wait("sync", P.tok("vector"))
                blend_t = {}
                idx = 0
                for mi, dsto in ((0, dt_o), (1, dtu_o)):
                    for dcl in range(4):
                        par = idx % 2
                        if idx >= 2:
                            P.wait("sync", blend_t[idx - 2])
                        dma("sync", aB[:, raw_o + par * QT: raw_o + (par + 1) * QT],
                            tsrc[s, mi, 512 + dcl * 128: 512 + (dcl + 1) * 128, 0:QT],
                            st=f"ldh{par}")
                        thi = P.tok(f"ldh{par}")
                        d_ = aB[:, dsto + dcl * QT: dsto + (dcl + 1) * QT]
                        P.wait("vector", t_lo)
                        P.wait("vector", thi)
                        ts("vector", d_, d_, m_lo, None, ALU.mult)
                        stt("vector", d_, aB[:, raw_o + par * QT: raw_o + (par + 1) * QT],
                            m_hi, d_, ALU.mult, ALU.add)
                        blend_t[idx] = P.mark("vector")
                        idx += 1
                P.wait("vector", P.tok("sy"))
                P.wait("vector", P.tok("gpsimd"))
                P.wait("scalar", P.tok("vector"))

                def exp_issue(g):
                    n_, dcl_ = g // 4, g % 4
                    act("scalar", aF[:, dAs[g % 2]:dAs[g % 2] + QT],
                        aB[:, dt_o + dcl_ * QT: dt_o + (dcl_ + 1) * QT],
                        AF.Exp, scale=cstc("alog", dcl_ * NST + n_))
                    return P.mark("scalar")

                texp = exp_issue(0)
                tblends = P.tok("vector")
                gp_tok = {}
                sc_tok = {}
                sv_tok = {}
                bctoks = {0: tbc_next}
                inp_tok = {}

                def inp_issue(g):
                    n_, dcl_ = g // 4, g % 4
                    par_ = n_ % 2
                    if g - 2 in sv_tok:
                        P.wait("gpsimd", sv_tok[g - 2])
                    P.wait("gpsimd", bctoks[n_])
                    P.wait("gpsimd", tblends)
                    tt("gpsimd", inpS[g % 2],
                       aB[:, dtu_o + dcl_ * QT: dtu_o + (dcl_ + 1) * QT],
                       bcB[par_], ALU.mult)
                    return P.mark("gpsimd")

                inp_tok[0] = inp_issue(0)
                for n in range(NST):
                    if n + 1 < NST:
                        tbc_cur, tbc_next = tbc_next, bc_load(q, n + 1)
                        bctoks[n + 1] = tbc_next
                    elif q + 1 < 4:
                        tbc_cur, tbc_next = tbc_next, bc_load(q + 1, 0)
                    else:
                        tbc_cur = tbc_next
                    P.wait("vector", tbc_cur)
                    par = n % 2
                    for dcl in range(4):
                        g = n * 4 + dcl
                        texp_cur = texp
                        if g + 1 < 4 * NST:
                            P.wait("scalar", P.tok("vector"))
                            texp = exp_issue(g + 1)
                            inp_tok[g + 1] = inp_issue(g + 1)
                        hB = hb[g % 2]
                        tmB = tmp[g % 2]
                        P.wait("vector", texp_cur)
                        P.wait("vector", inp_tok[g])
                        if g - 2 in gp_tok:
                            P.wait("vector", gp_tok[g - 2])
                        if g - 2 in sc_tok:
                            P.wait("vector", sc_tok[g - 2])
                        cini = 0.0 if q == 0 else aF[:, car_o + dcl * 16 + n: car_o + dcl * 16 + n + 1]
                        P.do("vector", lambda e, ci=cini, hB=hB, g=g: e.tensor_tensor_scan(
                            hB, aF[:, dAs[g % 2]:dAs[g % 2] + QT],
                            inpS[g % 2], ci, ALU.mult, ALU.add))
                        tv1 = P.mark("vector")
                        sv_tok[g] = tv1
                        P.wait("scalar", tv1)
                        act("scalar", aF[:, car_o + dcl * 16 + n: car_o + dcl * 16 + n + 1],
                            hB[:, QT - 1:QT], AF.Identity)
                        sc_tok[g] = P.mark("scalar")
                        if n == 0:
                            tt("vector", aB[:, y_o + dcl * QT: y_o + (dcl + 1) * QT],
                               hB, bcC[par], ALU.mult)
                        elif g % 2 == 0:
                            tt("vector", tmB, hB, bcC[par], ALU.mult)
                            tv2 = P.mark("vector")
                            P.wait("gpsimd", tv2)
                            tt("gpsimd", aB[:, y_o + dcl * QT: y_o + (dcl + 1) * QT],
                               aB[:, y_o + dcl * QT: y_o + (dcl + 1) * QT],
                               tmB, ALU.add)
                            gp_tok[g] = P.mark("gpsimd")
                        else:
                            tt("vector", tmB, hB, bcC[par], ALU.mult)
                            P.wait("vector", P.tok("gpsimd"))
                            tt("vector", aB[:, y_o + dcl * QT: y_o + (dcl + 1) * QT],
                               aB[:, y_o + dcl * QT: y_o + (dcl + 1) * QT],
                               tmB, ALU.add)
                            gp_tok[g] = P.mark("vector")
                    bc_guard[par] = P.tok("vector")
                P.wait("sync", P.mark("vector"))
                P.wait("sync", P.tok("gpsimd"))
                ydst = cc_yA_in if q < 2 else cc_yB_in
                for dcl in range(4):
                    dma("sync", ydst[dcl * 128:(dcl + 1) * 128, hf * QT:(hf + 1) * QT],
                        aB[:, y_o + dcl * QT: y_o + (dcl + 1) * QT], st="sy")
                if q == 1:
                    P.wait("gpsimd", P.tok("sy"))
                    P.do("gpsimd", lambda e: e.collective_compute(
                        "AllGather", ALU.bypass, replica_groups=RGROUPS,
                        ins=[cc_yA_in.ap()], outs=[cc_yA_out.ap()]), cc="ccy")

            arF.pop()
            P.wait("gpsimd", P.tok("sy"))
            P.do("gpsimd", lambda e: e.collective_compute(
                "AllGather", ALU.bypass, replica_groups=RGROUPS,
                ins=[cc_yB_in.ap()], outs=[cc_yB_out.ap()]), cc="ccy")
            P.wait("gpsimd", P.tok("ccy"))
            P.wait("sync", P.tok("ccy"))
            P.wait("gpsimd", P.tok("vector"))
            P.wait("gpsimd", P.tok("scalar"))
            P.wait("vector", P.tok("scalar"))
            arB.pop()
            xs_o = xn_o
            ya_o = [arB.al(TC, "yA_t0"), arB.al(TC, "yA_t1")]
            yb_o = [arF.al(TC, "yB_t0"), arF.al(TC, "yB_t1")]
            yg = [None, None]

            def yr_load(dc):
                par = dc % 2
                if yg[par] is not None:
                    P.wait("gpsimd", yg[par])
                    P.wait("sync", yg[par])
                dma("sync", aB[:, ya_o[par]:ya_o[par] + TC],
                    cc_yA_out[dc * 128:(dc + 1) * 128, :], st=f"lya{par}")
                dma("gpsimd", aF[:, yb_o[par]:yb_o[par] + TC],
                    cc_yB_out[dc * 128:(dc + 1) * 128, :], st=f"lyr{par}")
                return (P.tok(f"lya{par}"), P.tok(f"lyr{par}"))

            yr_t = {0: yr_load(0), 1: yr_load(1)}
            for dc in range(NDC):
                par = dc % 2
                P.wait("vector", yr_t[dc][0])
                P.wait("vector", yr_t[dc][1])
                xs = aB[:, xs_o + dc * TC: xs_o + (dc + 1) * TC]
                ts("vector", xs, aB[:, ya_o[par]:ya_o[par] + TC], m_lo, None, ALU.mult)
                stt("vector", xs, aF[:, yb_o[par]:yb_o[par] + TC], m_hi, xs, ALU.mult, ALU.add)
                stt("vector", xs, aB[:, u_o + dc * TC: u_o + (dc + 1) * TC],
                    cstc("dskip", dc), xs, ALU.mult, ALU.add)
                tt("vector", xs, xs, aB[:, v_o + dc * TC: v_o + (dc + 1) * TC], ALU.mult)
                yg[par] = P.tok("vector")
                if dc + 2 < NDC:
                    yr_t[dc + 2] = yr_load(dc + 2)
            P.wait("tensor", P.mark("vector"))

            wstL2 = [arB.al(NDC * 128, "w2a"), arB.al(NDC * 128, "w2b")]
            wstL[0], wstL[1] = wstL2[0], wstL2[1]
            for oc in range(NDC):
                slot, ws = wload("w2", oc)
                for tc in range(NT):
                    bank = tc % 4
                    mm_bank(bank, [(aB[:, ws + dc * 128: ws + dc * 128 + 128],
                                    aB[:, xs_o + dc * TC + tc * TCH: xs_o + dc * TC + (tc + 1) * TCH],
                                    pb[bank][:, 0:TCH]) for dc in range(NDC)])

                    def _eo(bank=bank, oc=oc, tc=tc):
                        stt("vector", x_sb[:, oc * TC + tc * TCH: oc * TC + (tc + 1) * TCH],
                            pb[bank][:, 0:TCH], cstc("b2", oc),
                            x_sb[:, oc * TC + tc * TCH: oc * TC + (tc + 1) * TCH],
                            ALU.add, ALU.add)
                    evac(bank, "vector", _eo)
                wguard[slot] = P.tok("tensor")
            DBG.update(l1_xn=xn_o, l1_u=u_o, l1_v=v_o, l1_xs=xs_o)
            arB.pop(); arF.pop()

        def l1_ffn():
            arB.push(); arF.push()
            NG = 4
            PG = NEXP // NG
            xn_o = arB.al(NDC * TC, "xn4")
            xn = ColView(aB, xn_o)
            layer_norm(xn)
            P.wait("tensor", P.tok("vector"))
            wrow_o = arB.al(TC, "Wrow")
            wsl = [arB.al(4096, f"moew{i}") for i in range(4)]
            it_o = arB.al(4 * TC, "inter")
            wbc = [arB.al(TC, "wbc0"), arB.al(TC, "wbc1")]
            tmpb_o = arB.al(TCH, "silu_t")
            wtmb_o = arB.al(16, "W_tmb")
            scr = arF.al(TC, "scores")
            ebr = arF.al(16, "ebias_row")
            g0 = arF.al(96, "gate_scratch")
            c_sc, c_sfc, c_tmp, c_rnd = g0, g0 + 16, g0 + 32, g0 + 48
            c_ab, c_cd, c_mix, c_gsc = g0 + 64, g0 + 68, g0 + 72, g0 + 76
            c_m1, c_n1, c_m2, c_n2 = g0 + 80, g0 + 81, g0 + 82, g0 + 83
            c_t1, c_t2, c_thrg, c_thre = g0 + 84, g0 + 85, g0 + 86, g0 + 87
            c_mx, c_ws, c_selg = g0 + 88, g0 + 89, g0 + 92

            for q_ in ("sync", "gpsimd"):
                P.wait(q_, P.tok("scalar"))
                P.wait(q_, P.tok("vector"))
            dma("sync", aF[:, ebr:ebr + 16],
                io["ebias"][0:1, :].broadcast_to((128, NEXP)), st="lgw")
            dma("gpsimd", aB[:, wsl[0]:wsl[0] + 128].rearrange("p (dc e) -> p dc e", dc=NDC),
                io["gatew_t"].rearrange("(dc p) e -> p dc e", p=128), st="lm0")
            P.wait("tensor", P.tok("lm0"))
            for tc in range(NT):
                bank = 5
                mm_bank(bank, [(aB[:, wsl[0] + dc * 16: wsl[0] + dc * 16 + 16],
                                xn[:, dc * TC + tc * TCH: dc * TC + (tc + 1) * TCH],
                                pb[bank][0:16, 0:TCH]) for dc in range(NDC)])

                def _eg(tc=tc):
                    act("scalar", aF[0:16, scr + tc * TCH: scr + (tc + 1) * TCH],
                        pb[5][0:16, 0:TCH], AF.Sigmoid)
                evac(bank, "scalar", _eg)
            wg_e0_gate = P.tok("tensor")
            P.wait("tensor", P.tok("scalar"))
            P.wait("vector", P.tok("lgw"))

            sfc3 = aF[:, c_sfc:c_sfc + 16].rearrange("p (g e) -> p g e", g=NG)
            tmp3 = aF[:, c_tmp:c_tmp + 16].rearrange("p (g e) -> p g e", g=NG)
            selg3 = aF[:, c_selg:c_selg + 4].rearrange("p (g e) -> p g e", g=NG)
            ab3 = aF[:, c_ab:c_ab + 4].rearrange("p (g e) -> p g e", g=NG)
            cd3 = aF[:, c_cd:c_cd + 4].rearrange("p (g e) -> p g e", g=NG)
            mix3 = aF[:, c_mix:c_mix + 4].rearrange("p (g e) -> p g e", g=NG)
            for j in range(TC // 128):
                jc = slice(j * 128, (j + 1) * 128)
                bank = 6
                if bank_free[bank] is not None:
                    P.wait("tensor", bank_free[bank]); bank_free[bank] = None
                transpose(pb[6][:, 0:16], aF[0:16, scr + j * 128: scr + (j + 1) * 128],
                          identf[0:16, 0:16])
                bank_ready[bank] = P.mark("tensor")

                def _etm():
                    cp("vector", aF[:, c_sc:c_sc + 16], pb[6][:, 0:16])
                evac(bank, "vector", _etm)
                tt("vector", aF[:, c_sfc:c_sfc + 16], aF[:, c_sc:c_sc + 16],
                   aF[:, ebr:ebr + 16], ALU.add)
                tt("vector", ab3[:, :, 0:1], sfc3[:, :, 0:1], sfc3[:, :, 1:2], ALU.add)
                tt("vector", cd3[:, :, 0:1], sfc3[:, :, 2:3], sfc3[:, :, 3:4], ALU.add)
                tt("vector", mix3[:, :, 0:1], sfc3[:, :, 0:1], sfc3[:, :, 1:2], ALU.max)
                tt("vector", selg3[:, :, 0:1], sfc3[:, :, 2:3], sfc3[:, :, 3:4], ALU.max)
                tt("vector", aF[:, c_mix:c_mix + 4], aF[:, c_mix:c_mix + 4],
                   aF[:, c_selg:c_selg + 4], ALU.add)
                tt("vector", aF[:, c_gsc:c_gsc + 4], aF[:, c_ab:c_ab + 4],
                   aF[:, c_cd:c_cd + 4], ALU.max)
                tt("vector", aF[:, c_gsc:c_gsc + 4], aF[:, c_gsc:c_gsc + 4],
                   aF[:, c_mix:c_mix + 4], ALU.max)
                tt("vector", aF[:, c_m1:c_m1 + 1], aF[:, c_gsc:c_gsc + 1],
                   aF[:, c_gsc + 1:c_gsc + 2], ALU.max)
                tt("vector", aF[:, c_n1:c_n1 + 1], aF[:, c_gsc:c_gsc + 1],
                   aF[:, c_gsc + 1:c_gsc + 2], ALU.min)
                tt("vector", aF[:, c_m2:c_m2 + 1], aF[:, c_gsc + 2:c_gsc + 3],
                   aF[:, c_gsc + 3:c_gsc + 4], ALU.max)
                tt("vector", aF[:, c_n2:c_n2 + 1], aF[:, c_gsc + 2:c_gsc + 3],
                   aF[:, c_gsc + 3:c_gsc + 4], ALU.min)
                tt("vector", aF[:, c_t1:c_t1 + 1], aF[:, c_m1:c_m1 + 1],
                   aF[:, c_m2:c_m2 + 1], ALU.min)
                tt("vector", aF[:, c_t2:c_t2 + 1], aF[:, c_n1:c_n1 + 1],
                   aF[:, c_n2:c_n2 + 1], ALU.max)
                tt("vector", aF[:, c_thrg:c_thrg + 1], aF[:, c_t1:c_t1 + 1],
                   aF[:, c_t2:c_t2 + 1], ALU.max)
                ts("vector", aF[:, c_selg:c_selg + 4], aF[:, c_gsc:c_gsc + 4],
                   aF[:, c_thrg:c_thrg + 1], None, ALU.is_ge)
                tt("vector", tmp3[:, :, :], sfc3[:, :, :],
                   selg3[:, :, 0:1].broadcast_to((128, NG, PG)), ALU.mult)
                cp("vector", aF[:, c_rnd:c_rnd + 16], aF[:, c_tmp:c_tmp + 16])
                for k in range(4):
                    P.do("vector", lambda e: e.reduce_max(
                        aF[:, c_mx:c_mx + 1], aF[:, c_rnd:c_rnd + 16], AXX))
                    if k < 3:
                        ts("vector", aF[:, c_ab:c_ab + 16], aF[:, c_rnd:c_rnd + 16],
                           aF[:, c_mx:c_mx + 1], None, ALU.is_equal)
                        stt("vector", aF[:, c_rnd:c_rnd + 16], aF[:, c_ab:c_ab + 16],
                            -1e30, aF[:, c_rnd:c_rnd + 16], ALU.mult, ALU.add)
                ts("vector", aF[:, c_tmp:c_tmp + 16], aF[:, c_tmp:c_tmp + 16],
                   aF[:, c_mx:c_mx + 1], None, ALU.is_ge)
                tt("vector", aF[:, c_tmp:c_tmp + 16], aF[:, c_tmp:c_tmp + 16],
                   aF[:, c_sc:c_sc + 16], ALU.mult)
                P.do("vector", lambda e: e.reduce_sum(
                    aF[:, c_ws:c_ws + 1], aF[:, c_tmp:c_tmp + 16], AXX))
                ts("vector", aF[:, c_ws:c_ws + 1], aF[:, c_ws:c_ws + 1], 1e-20, None, ALU.add)
                recip("vector", aF[:, c_ws:c_ws + 1], aF[:, c_ws:c_ws + 1])
                ts("vector", aF[:, c_ws:c_ws + 1], aF[:, c_ws:c_ws + 1], 2.5, None, ALU.mult)
                ts("vector", aB[:, wtmb_o:wtmb_o + 16], aF[:, c_tmp:c_tmp + 16],
                   aF[:, c_ws:c_ws + 1], None, ALU.mult)
                twm = P.mark("vector")
                P.wait("tensor", twm)
                if bank_free[7] is not None:
                    P.wait("tensor", bank_free[7]); bank_free[7] = None
                transpose(pT[0:16, 0:128], aB[:, wtmb_o:wtmb_o + 16], identb[:, :])
                bank_ready[7] = P.mark("tensor")

                def _ewr(j=j):
                    cp("vector", aB[0:16, wrow_o + j * 128: wrow_o + (j + 1) * 128],
                       pT[0:16, 0:128])
                evac(7, "vector", _ewr)
            P.wait("sync", P.mark("vector"))
            dma("sync", wsc_scr[:, :], aB[0:16, wrow_o:wrow_o + TC], st="swsc")

            wg_e = [wg_e0_gate, None, None, None]
            wbcg = [None, None]
            tdown = None

            def eload(slot, src_ap, nin=NDC):
                if wg_e[slot] is not None:
                    P.wait("gpsimd", wg_e[slot])
                    wg_e[slot] = None
                dma("gpsimd",
                    aB[:, wsl[slot]:wsl[slot] + 4096].rearrange("p (b i) -> p b i", b=nin),
                    src_ap, st=f"lm{slot}")
                return P.tok(f"lm{slot}")

            for e in range(NEXP + 1):
                shared = (e == NEXP)
                sg, su, sd = (3 * e) % 4, (3 * e + 1) % 4, (3 * e + 2) % 4
                if shared:
                    tg = eload(sg, io["sg_t"].rearrange("(dc p) i -> p dc i", p=128))
                    tu = eload(su, io["su_t"].rearrange("(dc p) i -> p dc i", p=128))
                    td = eload(sd, io["sd_t"].rearrange("(ic p) o -> p ic o", p=128), nin=4)
                else:
                    tg = eload(sg, io["gp_t"][e].rearrange("(dc p) i -> p dc i", p=128))
                    tu = eload(su, io["up_t"][e].rearrange("(dc p) i -> p dc i", p=128))
                    td = eload(sd, io["dp_t"][e].rearrange("(ic p) o -> p ic o", p=128), nin=4)
                    if wbcg[e % 2] is not None:
                        P.wait("gpsimd", wbcg[e % 2])
                    P.wait("gpsimd", P.tok("swsc"))
                    dma("gpsimd", aB[:, wbc[e % 2]:wbc[e % 2] + TC],
                        wsc_scr[e:e + 1, :].broadcast_to((128, TC)), st="lwb")
                if tdown is not None:
                    P.wait("vector", tdown)
                P.wait("tensor", tg)
                P.wait("tensor", tu)
                if not shared:
                    P.wait("vector", P.tok("lwb"))

                def emit_downs(tc, tw, sd=sd):
                    P.wait("tensor", tw)
                    P.wait("tensor", td)
                    for oc in range(NDC):
                        bank = 4 + (oc + tc) % 4
                        mm_bank(bank, [(aB[:, wsl[sd] + ic * 1024 + oc * 128:
                                              wsl[sd] + ic * 1024 + oc * 128 + 128],
                                        aB[:, it_o + ic * TC + tc * TCH:
                                              it_o + ic * TC + (tc + 1) * TCH],
                                        pb[bank][:, 0:TCH]) for ic in range(4)])

                        def _ed(bank=bank, oc=oc, tc=tc):
                            xs = x_sb[:, oc * TC + tc * TCH: oc * TC + (tc + 1) * TCH]
                            tt("vector", xs, xs, pb[bank][:, 0:TCH], ALU.add)
                        evac(bank, "vector", _ed)

                pend = None
                for tc in range(NT):
                    for ic in range(4):
                        bg, bu = ic % 2, 2 + ic % 2
                        mm_bank(bg, [(aB[:, wsl[sg] + (dc * 4 + ic) * 128:
                                            wsl[sg] + (dc * 4 + ic) * 128 + 128],
                                      xn[:, dc * TC + tc * TCH: dc * TC + (tc + 1) * TCH],
                                      pb[bg][:, 0:TCH]) for dc in range(NDC)])

                        def _es(bg=bg):
                            P.wait("scalar", P.tok("vector"))
                            act("scalar", aB[:, tmpb_o:tmpb_o + TCH], pb[bg][:, 0:TCH],
                                AF.Silu)
                        evac(bg, "scalar", _es)
                        tsi = P.tok("scalar")
                        mm_bank(bu, [(aB[:, wsl[su] + (dc * 4 + ic) * 128:
                                            wsl[su] + (dc * 4 + ic) * 128 + 128],
                                      xn[:, dc * TC + tc * TCH: dc * TC + (tc + 1) * TCH],
                                      pb[bu][:, 0:TCH]) for dc in range(NDC)])

                        def _ei(bu=bu, ic=ic, tc=tc, tsi=tsi, e=e, shared=shared):
                            P.wait("vector", tsi)
                            dst = aB[:, it_o + ic * TC + tc * TCH: it_o + ic * TC + (tc + 1) * TCH]
                            tt("vector", dst, aB[:, tmpb_o:tmpb_o + TCH],
                               pb[bu][:, 0:TCH], ALU.mult)
                            if not shared:
                                tt("vector", dst, dst,
                                   aB[:, wbc[e % 2] + tc * TCH: wbc[e % 2] + (tc + 1) * TCH],
                                   ALU.mult)
                        evac(bu, "vector", _ei)
                    tw = P.tok("vector")
                    if pend is not None:
                        emit_downs(*pend)
                    pend = (tc, tw)
                emit_downs(*pend)
                tdown = P.tok("tensor")
                if not shared:
                    wbcg[e % 2] = P.tok("vector")
                for sl in (sg, su, sd):
                    wg_e[sl] = P.tok("tensor")
            DBG.update(moe_xn=xn_o, moe_wrow=wrow_o, moe_inter=it_o, moe_scr=scr)
            arB.pop(); arF.pop()

        if upto >= 1:
            l0_attn()
            P.barrier()
        if upto >= 2:
            tq_scratch = weight_quant()
            P.barrier()
            if os.environ.get("KWQONLY", "") != "1":
                l0_ffn(tq_scratch)
                P.barrier()
        if upto >= 3:
            l1_attn()
            P.barrier()
        if upto >= 4:
            l1_ffn()
            P.barrier()

        for tname, shape, fn in taps:
            src = fn(dict(x_sb=x_sb, aB=aB, aF=aF, dbgc=dbgc))
            q = "gpsimd" if src.dtype != F32 else "sync"
            dma(q, tapd[tname][:, :], src, st="dout" if q == "sync" else "dout2")
        for dc in range(NDC):
            dma("sync", out_fm[dc * 128:(dc + 1) * 128, :], x_sb[:, dc * TC:(dc + 1) * TC], st="dout")
        tout = P.tok("dout")
        P.wait("sync", tout)

        P.check()
        sems = {}
        for sname in list(ENGS) + sorted(P.dma_sems):
            sems[sname] = stx.enter_context(nc.semaphore(sname))
        P.replay(stx.enter_context(nc.Block()), sems)

    return nc


def stage_inputs(inputs, TC):
    B = 4
    NSEQ = 2 * TC
    x = np.asarray(inputs["L0_attn_x"], np.float32)

    def t(a):
        return np.ascontiguousarray(np.asarray(a, np.float32).T)

    rf = np.asarray(inputs["L0_attn_RF"], np.float32)
    rfblk = np.zeros((NPAIR, 128, 128), np.float32)
    for hp in range(NPAIR):
        rfblk[hp, 0:64, 0:64] = rf[2 * hp]
        rfblk[hp, 64:128, 64:128] = rf[2 * hp + 1]

    def pcol(v):
        return np.ascontiguousarray(np.asarray(v, np.float32).reshape(NDC, 128).T)

    base = {
        "wqkv": np.ascontiguousarray(np.asarray(inputs["L0_attn_W_qkv"], np.float32)),
        "rfblk": rfblk,
        "wout": np.ascontiguousarray(np.asarray(inputs["L0_attn_W_out"], np.float32)),
        "bout": pcol(inputs["L0_attn_b_out"]),
        "wg_t": t(inputs["L0_ffn_w_gate"]),
        "wu_t": t(inputs["L0_ffn_w_up"]),
        "wd_t": t(inputs["L0_ffn_w_down"]),
        "lng": pcol(inputs["L1_attn_ln_g"]),
        "lnb": pcol(inputs["L1_attn_ln_b"]),
        "w1": np.ascontiguousarray(np.asarray(inputs["L1_attn_w1_w"], np.float32)),
        "b1": pcol(inputs["L1_attn_w1_b"]),
        "v1": np.ascontiguousarray(np.asarray(inputs["L1_attn_v1_w"], np.float32)),
        "bv": pcol(inputs["L1_attn_v1_b"]),
        "w2": np.ascontiguousarray(np.asarray(inputs["L1_attn_w2_w"], np.float32)),
        "b2": pcol(inputs["L1_attn_w2_b"]),
        "wdt": np.ascontiguousarray(np.asarray(inputs["L1_attn_W_dt"], np.float32)),
        "bdt": pcol(inputs["L1_attn_b_dt"]),
        "convw": np.ascontiguousarray(
            np.asarray(inputs["L1_attn_conv_w"], np.float32).reshape(NDC, 128, 3)
            .transpose(1, 0, 2).reshape(128, NDC * 3)),
        "dskip": pcol(inputs["L1_attn_D_skip"]),
        "wb": np.ascontiguousarray(np.asarray(inputs["L1_attn_W_B"], np.float32)),
        "wc": np.ascontiguousarray(np.asarray(inputs["L1_attn_W_C"], np.float32)),
        "gatew_t": t(inputs["L1_ffn_gate_w"]),
        "ebias": np.asarray(inputs["L1_ffn_e_bias"], np.float32).reshape(1, NEXP),
        "gp_t": np.ascontiguousarray(np.asarray(inputs["L1_ffn_gate_proj"], np.float32).transpose(0, 2, 1)),
        "up_t": np.ascontiguousarray(np.asarray(inputs["L1_ffn_up_proj"], np.float32).transpose(0, 2, 1)),
        "dp_t": np.ascontiguousarray(np.asarray(inputs["L1_ffn_down_proj"], np.float32).transpose(0, 2, 1)),
        "sg_t": t(inputs["L1_ffn_shared_gate_w"]),
        "su_t": t(inputs["L1_ffn_shared_up_w"]),
        "sd_t": t(inputs["L1_ffn_shared_down_w"]),
    }
    alog3 = np.asarray(inputs["L1_attn_A_log"], np.float32).reshape(NDC, 128, NST)
    in_maps = []
    for c in range(8):
        b, h = c // 2, c % 2
        m = dict(base)
        m["x_fm"] = np.ascontiguousarray(x[b, h * TC:(h + 1) * TC, :].T)
        meta = np.zeros((128, 4), np.float32)
        meta[:, 0] = 1.0 if h == 1 else 0.0
        meta[:, 1] = 1.0 if h == 0 else 0.0
        m["meta"] = meta
        m["alog"] = np.ascontiguousarray(
            np.roll(alog3, -4 * h, axis=0).transpose(1, 0, 2).reshape(128, NDC * NST))
        in_maps.append(m)
    return in_maps


def assemble(res, TC):
    x = np.zeros((4, 2 * TC, D), np.float32)
    for c in range(8):
        b, h = c // 2, c % 2
        x[b, h * TC:(h + 1) * TC, :] = res.results[c]["out_fm"].T
    return x


def kernel(**inputs):
    TC = 2048
    nc = build(TC, upto=4)
    in_maps = stage_inputs(inputs, TC)
    res = run_bass_kernel_spmd(nc, in_maps, core_ids=list(range(8)))
    return assemble(res, TC)


# version 12
# speedup vs baseline: 1.1468x; 1.1468x over previous
import contextlib
import os

import numpy as np
import concourse.bass as bass
import concourse.mybir as mybir
from concourse.bass_utils import run_bass_kernel_spmd

F32 = mybir.dt.float32
BF16 = mybir.dt.bfloat16
AF = mybir.ActivationFunctionType
ALU = mybir.AluOpType
AXX = mybir.AxisListType.X

D = 1024
NDC = 8
HEADS = 16
NPAIR = 8
INNER = 4096
NIC = 32
NST = 16
MOE_I = 512
NEXP = 16
RGROUPS = [[0, 1], [2, 3], [4, 5], [6, 7]]
RMAGIC = 3 * 2.0 ** 22
EPS = 1e-5

ENGS = ("tensor", "vector", "scalar", "gpsimd", "sync")


class Plan:

    SELF_SERIAL = ("vector", "scalar", "gpsimd")

    def __init__(self):
        self.stream = {e: [] for e in ENGS}
        self.cnt = {}
        self.dirty = {e: None for e in ENGS}
        self.dma_sems = set()

    def do(self, eng, fn, dma=None, cc=None):
        if cc:
            step = {"fn": fn, "incs": [(cc, 1)], "sem": cc}
            self.cnt[cc] = self.cnt.get(cc, 0) + 1
            self.dma_sems.add(cc)
        elif dma:
            sem = dma
            step = {"fn": fn, "incs": [(sem, 16)], "sem": sem}
            self.cnt[sem] = self.cnt.get(sem, 0) + 16
            self.dma_sems.add(sem)
        else:
            if eng in self.SELF_SERIAL and self.cnt.get(eng, 0) > 0:
                self.stream[eng].append({"wait": (eng, self.cnt[eng])})
            step = {"fn": fn, "incs": [(eng, 1)], "sem": eng}
            self.cnt[eng] = self.cnt.get(eng, 0) + 1
        step["tok"] = (step["sem"], self.cnt[step["sem"]])
        self.stream[eng].append(step)
        self.dirty[eng] = step
        return step

    def mark(self, eng, sem=None):
        step = self.dirty[eng]
        assert step is not None, f"mark({eng}) with no dirty step"
        return step["tok"]

    def tok(self, sem):
        return (sem, self.cnt.get(sem, 0))

    def wait(self, eng, token):
        if token is None:
            return
        sem, val = token
        if val <= 0:
            return
        self.stream[eng].append({"wait": (sem, val)})

    def barrier(self):
        toks = []
        for e in ENGS:
            if self.dirty[e] is not None:
                toks.append(self.dirty[e]["tok"])
                self.dirty[e] = None
        for d in sorted(self.dma_sems):
            if self.cnt.get(d, 0) > 0:
                toks.append(self.tok(d))
        for e in ENGS:
            for t in toks:
                self.wait(e, t)

    def check(self):
        pos = {e: 0 for e in ENGS}
        sem = {}
        progressed = True
        while progressed:
            progressed = False
            for e in ENGS:
                items = self.stream[e]
                while pos[e] < len(items):
                    it = items[pos[e]]
                    if "wait" in it:
                        sm, v = it["wait"]
                        if sem.get(sm, 0) < v:
                            break
                    else:
                        for (sm, k) in it["incs"]:
                            sem[sm] = sem.get(sm, 0) + k
                    pos[e] += 1
                    progressed = True
        stuck = {e: (pos[e], len(self.stream[e])) for e in ENGS if pos[e] < len(self.stream[e])}
        if stuck:
            det = {}
            for e in stuck:
                it = self.stream[e][pos[e]]
                det[e] = ("waits", it["wait"], "have", sem.get(it["wait"][0], 0)) if "wait" in it else "??"
            raise RuntimeError(f"PLAN DEADLOCK: {stuck} {det}")

    def replay(self, blk, sems):
        for e in ENGS:
            items = self.stream[e]

            def run(eng, items=items):
                prev_wait = None
                for it in items:
                    if "wait" in it:
                        if it["wait"] == prev_wait:
                            continue
                        prev_wait = it["wait"]
                        sm, v = it["wait"]
                        eng.wait_ge(sems[sm], v)
                    else:
                        instr = it["fn"](eng)
                        for (sm, k) in it["incs"]:
                            instr.then_inc(sems[sm], k)
            getattr(blk, e)(run)


class Arena:
    def __init__(self, n, name):
        self.n, self.name = n, name
        self.off = 0
        self.marks = []
        self.peak = 0

    def al(self, cols, what=""):
        o = self.off
        self.off += cols
        self.peak = max(self.peak, self.off)
        assert self.off <= self.n, f"{self.name} overflow at {what}: {self.off}>{self.n}"
        return o

    def push(self):
        self.marks.append(self.off)

    def pop(self):
        self.off = self.marks.pop()


class ColView:

    def __init__(self, t, off):
        self.t, self.off = t, off

    def __getitem__(self, key):
        p, c = key
        return self.t[p, c.start + self.off: c.stop + self.off]

    @property
    def dtype(self):
        return self.t[:, 0:1].dtype


DBG = {}


def build(TC=2048, upto=2, taps=()):
    NT = max(1, TC // 512)
    TCH = TC // NT
    NSUB = TC // 128
    nc = bass.Bass("TRN2", target_bir_lowering=False, debug=False)

    io = {}

    def EI(name, shape):
        io[name] = nc.dram_tensor(name, list(shape), F32, kind="ExternalInput")

    EI("x_fm", (D, TC))
    EI("wqkv", (D, 3 * D)); EI("rfblk", (NPAIR, 128, 128))
    EI("wout", (D, D)); EI("bout", (128, NDC))
    EI("wg_t", (D, INNER)); EI("wu_t", (D, INNER)); EI("wd_t", (INNER, D))
    EI("lng", (128, NDC)); EI("lnb", (128, NDC))
    EI("w1", (D, D)); EI("b1", (128, NDC))
    EI("v1", (D, D)); EI("bv", (128, NDC))
    EI("w2", (D, D)); EI("b2", (128, NDC))
    EI("wdt", (D, D)); EI("bdt", (128, NDC))
    EI("convw", (128, NDC * 3)); EI("alog", (128, NDC * NST)); EI("dskip", (128, NDC))
    EI("wb", (D, NST)); EI("wc", (D, NST))
    EI("gatew_t", (D, NEXP)); EI("ebias", (1, NEXP))
    EI("gp_t", (NEXP, D, MOE_I)); EI("up_t", (NEXP, D, MOE_I)); EI("dp_t", (NEXP, MOE_I, D))
    EI("sg_t", (D, MOE_I)); EI("su_t", (D, MOE_I)); EI("sd_t", (MOE_I, D))
    EI("meta", (128, 4))

    out_fm = nc.dram_tensor("out_fm", [D, TC], F32, kind="ExternalOutput")
    tapd = {}
    for tname, shape, _ in taps:
        tapd[tname] = nc.dram_tensor("tap_" + tname, list(shape), F32, kind="ExternalOutput")

    cc_kv_in = nc.dram_tensor("cc_kv_in", [64, HEADS * 65], F32)
    cc_kv_out = nc.dram_tensor("cc_kv_out", [64, HEADS * 65], F32)
    cc_edge_in = nc.dram_tensor("cc_edge_in", [128, NDC * 2], F32)
    cc_edge_out = nc.dram_tensor("cc_edge_out", [2 * 128, NDC * 2], F32)
    cc_dtA_in = nc.dram_tensor("cc_dtA_in", [2, D, TC // 2], BF16)
    cc_dtA_out = nc.dram_tensor("cc_dtA_out", [2, 2, D, TC // 2], BF16)
    cc_dtB_in = nc.dram_tensor("cc_dtB_in", [2, D, TC // 2], BF16)
    cc_dtB_out = nc.dram_tensor("cc_dtB_out", [2, 2, D, TC // 2], BF16)
    cc_bc_in = nc.dram_tensor("cc_bc_in", [2, NST, TC], BF16)
    cc_bc_out = nc.dram_tensor("cc_bc_out", [2, 2, NST, TC], BF16)
    cc_yA_in = nc.dram_tensor("cc_yA_in", [512, TC], BF16)
    cc_yA_out = nc.dram_tensor("cc_yA_out", [D, TC], BF16)
    cc_yB_in = nc.dram_tensor("cc_yB_in", [512, TC], BF16)
    cc_yB_out = nc.dram_tensor("cc_yB_out", [D, TC], BF16)
    wq_g = nc.dram_tensor("wq_g", [D, INNER], BF16)
    rb_scr = nc.dram_tensor("rb_scr", [64, 128], F32)
    wq_u = nc.dram_tensor("wq_u", [D, INNER], BF16)
    wq_d = nc.dram_tensor("wq_d", [INNER, D], BF16)
    wsc_scr = nc.dram_tensor("wsc_scr", [NEXP, TC], BF16)

    P = Plan()

    with contextlib.ExitStack() as stx:
        def SB(name, shape, dtype):
            return stx.enter_context(nc.sbuf_tensor(name, list(shape), dtype))

        def PSUM(name, shape, dtype=F32):
            return stx.enter_context(nc.psum_tensor(name, list(shape), dtype))

        x_sb = SB("x_sb", (128, NDC * TC), F32)
        ABN = max(TC * 28 + 1024, 36000) if TC <= 1024 else 55296
        AFN = 8064 if TC >= 2048 else TC * 4 + 3072
        aB = SB("aB", (128, ABN), BF16)
        aF = SB("aF", (128, AFN), F32)
        cst = SB("cst", (128, 244), F32)
        ones_b = SB("ones_b", (128, 2), BF16)
        rowones_f = SB("rowones_f", (1, 128), F32)
        selp = SB("selp", (16, 128 * NPAIR), BF16)
        identb = SB("identb", (128, 128), BF16)
        identf = SB("identf", (128, 128), F32)
        identf_ap = [None]
        dbgc = SB("dbgc", (1, 16), F32)
        P.do("vector", lambda e: e.memset(dbgc[:, :], 0.0))

        pb = [PSUM(f"pb{i}", (128, 512), F32) for i in range(8)]
        pT = pb[7][:, :].bitcast(BF16)


        def dma(queue, dst, src, st="ld0"):
            return P.do(queue, lambda e, d=dst, s=src: e.dma_start(out=d, in_=s), dma=st)

        def mm(dst, lhsT, rhs, start, stop):
            return P.do("tensor", lambda e, d=dst, l=lhsT, r=rhs, st=start, sp=stop:
                        e.matmul(d, l, r, start=st, stop=sp, skip_group_check=True))

        def act(eng, dst, src, func, bias=0.0, scale=1.0):
            return P.do(eng, lambda e, d=dst, s=src, f=func, b=bias, sc=scale:
                        e.activation(d, s, f, bias=b, scale=sc))

        def tt(eng, dst, a, b, op):
            return P.do(eng, lambda e, d=dst, x=a, y=b, o=op: e.tensor_tensor(d, x, y, o))

        def ts(eng, dst, a, s1, s2, op0, op1=ALU.bypass):
            return P.do(eng, lambda e, d=dst, x=a, u=s1, v=s2, o0=op0, o1=op1:
                        e.tensor_scalar(d, x, u, v, o0, o1))

        def stt(eng, dst, in0, scal, in1, op0, op1):
            return P.do(eng, lambda e, d=dst, a=in0, s=scal, b=in1, o0=op0, o1=op1:
                        e.scalar_tensor_tensor(d, a, s, b, o0, o1))

        def cp(eng, dst, src):
            return P.do(eng, lambda e, d=dst, s=src: e.tensor_copy(d, s))

        def recip(eng, dst, src):
            return P.do(eng, lambda e, d=dst, s=src: e.reciprocal(d, s))

        def transpose(dst_psum, src, ident):
            return P.do("tensor", lambda e, d=dst_psum, s=src, i=ident: e.transpose(d, s, i))

        bank_free = [None] * 8
        bank_ready = [None] * 8

        def mm_bank(bank, mms):
            if bank_free[bank] is not None:
                P.wait("tensor", bank_free[bank])
                bank_free[bank] = None
            n = len(mms)
            for i, (l, r, dap) in enumerate(mms):
                mm(dap if dap is not None else pb[bank][:, 0:TCH], l, r, i == 0, i == n - 1)
            bank_ready[bank] = P.mark("tensor")
            return bank_ready[bank]

        def evac(bank, eng, emit):
            P.wait(eng, bank_ready[bank])
            emit()
            bank_free[bank] = P.mark(eng)

        arB = Arena(ABN, "aB")
        arF = Arena(AFN, "aF")

        dma("sync", x_sb[:, :].rearrange("p (dc t) -> p dc t", dc=NDC),
            io["x_fm"].rearrange("(dc p) t -> p dc t", p=128))
        CST = {}
        _off = [0]
        for nm, w in [("bout", NDC), ("b1", NDC), ("bv", NDC), ("b2", NDC), ("bdt", NDC),
                      ("lng", NDC), ("lnb", NDC), ("convw", NDC * 3), ("dskip", NDC),
                      ("alog", NDC * NST), ("meta", 4)]:
            CST[nm] = _off[0]
            dma("sync", cst[:, _off[0]:_off[0] + w], io[nm][:, :])
            _off[0] += w

        def cstc(name, i=0):
            return cst[:, CST[name] + i: CST[name] + i + 1]

        P.do("vector", lambda e: e.memset(ones_b[:, :], 1.0))
        P.do("vector", lambda e: e.memset(rowones_f[:, :], 1.0))
        _fc = [0]
        FCBASE = 228

        def fconst(v):
            col = FCBASE + _fc[0]
            _fc[0] += 1
            assert col < 256
            P.do("vector", lambda e, c=col, vv=v: e.memset(cst[:, c:c + 1], vv))
            nc.const_aps.register(F32, float(v), cst[:, col:col + 1])
            return float(v)

        for _v in (EPS, EPS * EPS, 1e-6, 1.0, RMAGIC):
            fconst(_v)
        SKIP_CONSTS = os.environ.get("KSKIP", "")
        if "id" not in SKIP_CONSTS:
            P.do("gpsimd", lambda e: e.memset(identb[:, :], 0.0))
        if "id" not in SKIP_CONSTS:
            P.wait("gpsimd", P.mark("gpsimd"))
            P.do("gpsimd", lambda e: e.affine_select(
                out=identb[:, :], in_=identb[:, :], compare_op=ALU.not_equal,
                fill=1.0, base=0, pattern=[[-1, 128]], channel_multiplier=1))
            P.do("gpsimd", lambda e: e.memset(identf[:, :], 0.0))
            P.wait("gpsimd", P.mark("gpsimd"))
            P.do("gpsimd", lambda e: e.affine_select(
                out=identf[:, :], in_=identf[:, :], compare_op=ALU.not_equal,
                fill=1.0, base=0, pattern=[[-1, 128]], channel_multiplier=1))
        if "selp" not in SKIP_CONSTS:
            P.do("gpsimd", lambda e: e.iota(aF[0:16, 0:128], pattern=[[1, 128]], base=0,
                                            channel_multiplier=0, allow_small_or_imprecise_dtypes=True))
            P.do("gpsimd", lambda e: e.iota(aF[0:16, 128:256], pattern=[[0, 128]], base=0,
                                            channel_multiplier=1, allow_small_or_imprecise_dtypes=True))
        P.barrier()
        if "selp" not in SKIP_CONSTS:
            ts("vector", aF[0:16, 0:128], aF[0:16, 0:128], 64.0, None, ALU.is_ge)
            tt("vector", aF[0:16, 128:256], aF[0:16, 128:256], aF[0:16, 0:128], ALU.subtract)
            for hp in range(NPAIR):
                ts("vector", selp[:, hp * 128:(hp + 1) * 128], aF[0:16, 128:256],
                   float(2 * hp), None, ALU.is_equal)
        if "aexp" not in SKIP_CONSTS:
            a0 = CST["alog"]
            act("scalar", cst[:, a0:a0 + NDC * NST], cst[:, a0:a0 + NDC * NST], AF.Exp)
            P.wait("vector", P.mark("scalar"))
            ts("vector", cst[:, a0:a0 + NDC * NST], cst[:, a0:a0 + NDC * NST], -1.0, None, ALU.mult)
        identf_ap[0] = identf[:, :]
        P.barrier()

        def layer_norm(dst_bf, inner_gb=False):
            arB.push(); arF.push()
            sq = [arB.al(TCH, "ln_sq"), arB.al(TCH, "ln_sq2")]
            xb = [arB.al(TCH, "ln_xb"), arB.al(TCH, "ln_xb2")]
            rb = arF.al(TC, "rstd_b")
            mb = arF.al(TC, "murstd_b")
            tmpc = arF.al(TCH, "ln_tmp")
            ro = arF.al(5 * 64, "ln_rows")
            arF.off -= 5 * 64
            ro = arF.al(5 * TCH, "ln_rows")
            mu = aF[0:1, ro:ro + TCH]
            va = aF[0:1, ro + TCH:ro + 2 * TCH]
            r3 = aF[0:1, ro + 2 * TCH:ro + 3 * TCH]
            mr = aF[0:1, ro + 3 * TCH:ro + 4 * TCH]
            sc1 = aF[0:1, ro + 4 * TCH:ro + 5 * TCH]
            mm_toks = {}
            for tc in range(NT):
                if bank_free[5] is not None:
                    P.wait("tensor", bank_free[5]); bank_free[5] = None
                if bank_free[6] is not None:
                    P.wait("tensor", bank_free[6]); bank_free[6] = None
                for dc in range(NDC):
                    par = dc % 2
                    xs = x_sb[:, dc * TC + tc * TCH: dc * TC + (tc + 1) * TCH]
                    prev = mm_toks.get(par)
                    if prev is not None:
                        P.wait("vector", prev); P.wait("scalar", prev)
                    cp("vector", aB[:, xb[par]:xb[par] + TCH], xs)
                    tv = P.mark("vector")
                    act("scalar", aB[:, sq[par]:sq[par] + TCH], xs, AF.Square)
                    tsc = P.mark("scalar")
                    P.wait("tensor", tv); P.wait("tensor", tsc)
                    mm(pb[5][0:1, 0:TCH], ones_b[:, 0:1], aB[:, xb[par]:xb[par] + TCH], dc == 0, dc == NDC - 1)
                    mm(pb[6][0:1, 0:TCH], ones_b[:, 0:1], aB[:, sq[par]:sq[par] + TCH], dc == 0, dc == NDC - 1)
                    mm_toks[par] = P.mark("tensor")
                tmm = P.tok("tensor")
                P.wait("vector", tmm)
                ts("vector", mu, pb[5][0:1, 0:TCH], 1.0 / D, None, ALU.mult)
                ts("vector", va, pb[6][0:1, 0:TCH], 1.0 / D, None, ALU.mult)
                tfree = P.mark("vector")
                bank_free[5] = tfree; bank_free[6] = tfree
                tt("vector", sc1, mu, mu, ALU.mult)
                tt("vector", va, va, sc1, ALU.subtract)
                tva = P.mark("vector")
                P.wait("scalar", tva)
                if inner_gb:
                    act("scalar", r3, va, AF.Ln, bias=EPS * EPS, scale=1.0 + EPS)
                else:
                    act("scalar", r3, va, AF.Ln, bias=EPS)
                act("scalar", r3, r3, AF.Exp, scale=-0.5)
                trs = P.mark("scalar")
                P.wait("vector", trs)
                tt("vector", mr, mu, r3, ALU.mult)
                tstat = P.mark("vector")
                P.wait("tensor", tstat)
                for (row, dsto) in ((r3, rb), (mr, mb)):
                    mm_bank(4, [(rowones_f[:, :], row, None)])

                    def _e(tc=tc, dsto=dsto):
                        cp("vector", aF[:, dsto + tc * TCH: dsto + (tc + 1) * TCH], pb[4][:, 0:TCH])
                    evac(4, "vector", _e)
            for dc in range(NDC):
                for tc in range(NT):
                    lo = dc * TC + tc * TCH
                    hi = lo + TCH
                    tt("vector", aF[:, tmpc:tmpc + TCH], x_sb[:, lo:hi],
                       aF[:, rb + tc * TCH: rb + (tc + 1) * TCH], ALU.mult)
                    if inner_gb:
                        tt("vector", aF[:, tmpc:tmpc + TCH], aF[:, tmpc:tmpc + TCH],
                           aF[:, mb + tc * TCH: mb + (tc + 1) * TCH], ALU.subtract)
                        ts("vector", dst_bf[:, lo:hi], aF[:, tmpc:tmpc + TCH],
                           cstc("lng", dc), cstc("lnb", dc), ALU.mult, ALU.add)
                    else:
                        tt("vector", dst_bf[:, lo:hi], aF[:, tmpc:tmpc + TCH],
                           aF[:, mb + tc * TCH: mb + (tc + 1) * TCH], ALU.subtract)
            arB.pop(); arF.pop()

        def l0_attn():
            arB.push(); arF.push()
            xn_o = arB.al(NDC * TC, "xn1")
            xn = ColView(aB, xn_o)
            layer_norm(xn)
            P.wait("tensor", P.tok("vector"))
            phi_q = arB.al(NPAIR * TC, "phi_q")
            phi_kg = arB.al(4 * TC, "phi_kg")
            rfb = arB.al(NPAIR * 128, "rfblk")
            wst = [arB.al(2 * NDC * 128, "wqk0"), arB.al(2 * NDC * 128, "wqk1")]
            qk = arB.al(TC, "qk")
            vv = arB.al(512, "vT")
            pkT = arB.al(512, "phikT")
            wvm = arB.al(NDC * 512, "wv_moving")
            pairkv = qk
            kvcols = arF.al(1040, "kvcols")
            nrmtm_o = arF.al(NSUB * 16, "nrmtm")
            nrmtm = aF[:, nrmtm_o:nrmtm_o + NSUB * 16]
            nrow_o = arB.al(TC, "nrow")
            nrow = aB[0:16, nrow_o:nrow_o + TC]
            dma("gpsimd", aB[:, rfb:rfb + NPAIR * 128].rearrange("p (h c) -> p h c", h=NPAIR),
                io["rfblk"].rearrange("h p c -> p h c"), st="lrf")
            trf = P.tok("lrf")

            wst_guard = [None, None]
            qk_guard = None
            wv_guard = None
            for grp in range(2):
                for lp in range(4):
                    hp = grp * 4 + lp
                    slot = lp % 2
                    ws = wst[slot]
                    if wst_guard[slot] is not None:
                        P.wait("gpsimd", wst_guard[slot])
                    for part in range(2):
                        col0 = part * D + hp * 128
                        dma("gpsimd",
                            aB[:, ws + part * NDC * 128: ws + (part + 1) * NDC * 128]
                            .rearrange("p (dc c) -> p dc c", dc=NDC),
                            io["wqkv"].rearrange("(dc p) c -> p dc c", p=128)[:, :, col0:col0 + 128],
                            st=f"lw{slot}")
                    tw = P.tok(f"lw{slot}")
                    P.wait("tensor", tw)
                    if hp == 0:
                        P.wait("tensor", trf)
                    for part, pdst, po in ((0, phi_q, hp * TC), (1, phi_kg, lp * TC)):
                        if qk_guard is not None:
                            P.wait("vector", qk_guard)
                        for tc in range(NT):
                            bank = tc % 4
                            mm_bank(bank, [(aB[:, ws + (part * NDC + dc) * 128: ws + (part * NDC + dc) * 128 + 128],
                                            xn[:, dc * TC + tc * TCH: dc * TC + (tc + 1) * TCH], None)
                                           for dc in range(NDC)])

                            def _e(bank=bank, tc=tc):
                                cp("vector", aB[:, qk + tc * TCH: qk + (tc + 1) * TCH],
                                   pb[bank][:, 0:TCH])
                            evac(bank, "vector", _e)
                        tqk = P.tok("vector")
                        P.wait("tensor", tqk)
                        for tc in range(NT):
                            bank = 4 + tc % 2
                            mm_bank(bank, [(aB[:, rfb + hp * 128: rfb + (hp + 1) * 128],
                                            aB[:, qk + tc * TCH: qk + (tc + 1) * TCH],
                                            None)])

                            def _e(bank=bank, pdst=pdst, po=po, tc=tc):
                                act("scalar", aB[:, pdst + po + tc * TCH: pdst + po + (tc + 1) * TCH],
                                    pb[bank][:, 0:TCH], AF.Relu)
                            evac(bank, "scalar", _e)
                        qk_guard = P.tok("tensor")
                    wst_guard[slot] = P.tok("tensor")

                if wv_guard is not None:
                    P.wait("gpsimd", wv_guard)
                dma("gpsimd", aB[:, wvm:wvm + NDC * 512].rearrange("p (dc c) -> p dc c", dc=NDC),
                    io["wqkv"].rearrange("(dc p) c -> p dc c", p=128)
                    [:, :, 2 * D + grp * 512:2 * D + (grp + 1) * 512], st="lwv")
                twv = P.tok("lwv")
                P.wait("tensor", twv)
                tphi = P.tok("scalar")
                P.wait("tensor", tphi)
                for b_ in range(2):
                    if bank_free[b_] is not None:
                        P.wait("tensor", bank_free[b_]); bank_free[b_] = None
                tkv = None
                for sub in range(NSUB):
                    t0 = sub * 128
                    if tkv is not None:
                        P.wait("vector", tkv)
                    mm_bank(4, [(xn[:, dc * TC + t0: dc * TC + t0 + 128],
                                 aB[:, wvm + dc * 512: wvm + (dc + 1) * 512],
                                 pb[4][:, 0:512]) for dc in range(NDC)])

                    def _ev():
                        cp("vector", aB[:, vv:vv + 512], pb[4][:, 0:512])
                    evac(4, "vector", _ev)
                    if bank_free[7] is not None:
                        P.wait("tensor", bank_free[7]); bank_free[7] = None
                    for lp in range(4):
                        transpose(pT[:, lp * 128:(lp + 1) * 128],
                                  aB[:, phi_kg + lp * TC + t0: phi_kg + lp * TC + t0 + 128],
                                  identb[:, :])
                    bank_ready[7] = P.mark("tensor")

                    def _ek():
                        cp("vector", aB[:, pkT:pkT + 512], pT[:, 0:512])
                    evac(7, "vector", _ek)
                    tvv = P.tok("vector")
                    P.wait("tensor", tvv)
                    for lh in range(8):
                        bank = lh // 6
                        c0 = (lh % 6) * 65
                        lpp, ab = lh // 2, lh % 2
                        first = (sub == 0 and lh % 6 == 0)
                        statap = aB[:, pkT + lpp * 128 + ab * 64: pkT + lpp * 128 + ab * 64 + 64]
                        mm(pb[bank][0:64, c0:c0 + 64], statap, aB[:, vv + lh * 64: vv + lh * 64 + 64], first, False)
                        mm(pb[bank][0:64, c0 + 64:c0 + 65], statap, ones_b[:, 0:1], False, False)
                    tkv = P.mark("tensor")
                    bank_ready[0] = tkv
                    bank_ready[1] = tkv
                P.wait("vector", tkv)
                for bank in range(2):
                    nh = 6 if bank == 0 else 2
                    base_h = grp * 8 + bank * 6
                    cp("vector", aF[0:64, kvcols + base_h * 65: kvcols + (base_h + nh) * 65],
                       pb[bank][0:64, 0:nh * 65])
                tkvev = P.mark("vector")
                for bank in range(2):
                    bank_free[bank] = tkvev
                    bank_ready[bank] = None
                wv_guard = P.tok("tensor")

            tkvev = P.tok("vector")
            P.wait("sync", tkvev)
            dma("sync", cc_kv_in[:, :], aF[0:64, kvcols:kvcols + 1040], st="dcc")
            P.wait("gpsimd", P.tok("dcc"))
            P.do("gpsimd", lambda e: e.collective_compute(
                "AllReduce", ALU.add, replica_groups=RGROUPS,
                ins=[cc_kv_in.ap()], outs=[cc_kv_out.ap()]), cc="cck")
            P.wait("gpsimd", P.tok("cck"))
            dma("gpsimd", aF[0:64, kvcols:kvcols + 1040], cc_kv_out[:, :], st="dcc2")
            tkvr = P.tok("dcc2")

            P.wait("vector", tkvr)
            P.do("vector", lambda e: e.memset(aB[:, pairkv:pairkv + NPAIR * 130], 0.0))
            for hp in range(NPAIR):
                o = pairkv + hp * 130
                ha, hb = 2 * hp, 2 * hp + 1
                cp("vector", aB[0:64, o:o + 64], aF[0:64, kvcols + ha * 65: kvcols + ha * 65 + 64])
                cp("vector", aB[64:128, o + 64:o + 128], aF[0:64, kvcols + hb * 65: kvcols + hb * 65 + 64])
                cp("vector", aB[0:64, o + 128:o + 129], aF[0:64, kvcols + ha * 65 + 64: kvcols + ha * 65 + 65])
                cp("vector", aB[64:128, o + 129:o + 130], aF[0:64, kvcols + hb * 65 + 64: kvcols + hb * 65 + 65])
            tpair = P.mark("vector")
            P.wait("tensor", tpair)

            for g0 in range(0, NSUB, 8):
                if bank_free[6] is not None:
                    P.wait("tensor", bank_free[6]); bank_free[6] = None
                gn = min(8, NSUB - g0)
                for si in range(gn):
                    sub = g0 + si
                    for hp in range(NPAIR):
                        mm(pb[6][0:128, si * 16 + hp * 2: si * 16 + hp * 2 + 2],
                           aB[:, phi_q + hp * TC + sub * 128: phi_q + hp * TC + sub * 128 + 128],
                           aB[:, pairkv + hp * 130 + 128: pairkv + hp * 130 + 130],
                           si == 0 and hp == 0, False)
                bank_ready[6] = P.mark("tensor")

                def _e(g0=g0, gn=gn):
                    cp("vector", aF[:, nrmtm_o + g0 * 16:nrmtm_o + (g0 + gn) * 16], pb[6][:, 0:gn * 16])
                evac(6, "vector", _e)
            ts("vector", aF[:, nrmtm_o:nrmtm_o + NSUB * 16], aF[:, nrmtm_o:nrmtm_o + NSUB * 16], 1e-6, None, ALU.add)
            recip("vector", aF[:, nrmtm_o:nrmtm_o + NSUB * 16], aF[:, nrmtm_o:nrmtm_o + NSUB * 16])
            tnr = P.mark("vector")
            P.wait("tensor", tnr)
            for sub in range(NSUB):
                bank = sub % 2
                if bank_free[bank] is not None:
                    P.wait("tensor", bank_free[bank]); bank_free[bank] = None
                transpose(pb[bank][0:16, 0:128], aF[:, nrmtm_o + sub * 16:nrmtm_o + (sub + 1) * 16],
                          identf[:, :])
                bank_ready[bank] = P.mark("tensor")

                def _e(bank=bank, sub=sub):
                    cp("vector", aB[0:16, nrow_o + sub * 128:nrow_o + (sub + 1) * 128],
                       pb[bank][0:16, 0:128])
                evac(bank, "vector", _e)

            atn = xn_o
            nb = arF.al(TCH, "nrm_b")
            tnrow = P.tok("vector")
            P.wait("tensor", tnrow)
            for hp in range(NPAIR):
                for tc in range(NT):
                    bank = (hp * NT + tc) % 2
                    mm_bank(bank, [(aB[:, pairkv + hp * 130: pairkv + hp * 130 + 128],
                                    aB[:, phi_q + hp * TC + tc * TCH: phi_q + hp * TC + (tc + 1) * TCH],
                                    pb[bank][:, 0:TCH])])
                    bank2 = 2 + (hp * NT + tc) % 2
                    mm_bank(bank2, [(selp[:, hp * 128:(hp + 1) * 128],
                                     nrow[:, tc * TCH:(tc + 1) * TCH], pb[bank2][:, 0:TCH])])

                    def _e2(bank2=bank2):
                        cp("vector", aF[:, nb:nb + TCH], pb[bank2][:, 0:TCH])
                    evac(bank2, "vector", _e2)

                    def _e(bank=bank, hp=hp, tc=tc):
                        tt("vector", aB[:, atn + hp * TC + tc * TCH: atn + hp * TC + (tc + 1) * TCH],
                           pb[bank][:, 0:TCH], aF[:, nb:nb + TCH], ALU.mult)
                    evac(bank, "vector", _e)
            tatn = P.tok("vector")
            P.wait("tensor", tatn)
            woguard = [None, None]
            for oc in range(NDC):
                slot = oc % 2
                ws = wst[slot]
                if woguard[slot] is not None:
                    P.wait("gpsimd", woguard[slot])
                P.wait("gpsimd", wst_guard[slot])
                wst_guard[slot] = None
                for dc in range(NDC):
                    dma("gpsimd", aB[:, ws + dc * 128: ws + dc * 128 + 128],
                        io["wout"][dc * 128:(dc + 1) * 128, oc * 128:(oc + 1) * 128], st=f"lw{slot}")
                two = P.tok(f"lw{slot}")
                P.wait("tensor", two)
                for tc in range(NT):
                    bank = 4 + (oc * NT + tc) % 3
                    mm_bank(bank, [(aB[:, ws + dc * 128: ws + dc * 128 + 128],
                                    aB[:, atn + dc * TC + tc * TCH: atn + dc * TC + (tc + 1) * TCH],
                                    pb[bank][:, 0:TCH])
                                   for dc in range(NDC)])

                    def _e(bank=bank, oc=oc, tc=tc):
                        stt("vector", x_sb[:, oc * TC + tc * TCH: oc * TC + (tc + 1) * TCH],
                            pb[bank][:, 0:TCH], cstc("bout", oc),
                            x_sb[:, oc * TC + tc * TCH: oc * TC + (tc + 1) * TCH],
                            ALU.add, ALU.add)
                    evac(bank, "vector", _e)
                woguard[slot] = P.tok("tensor")
            DBG.update(atn=atn, phi_q=phi_q, pairkv=pairkv, kvcols=kvcols)
            arB.pop(); arF.pop()

        def rowmax_bcast(am_off, ncols, sb_off, usb_off, smax, usc):
            nsub = ncols // 128
            arB.push(); arF.push()
            rtm = arB.al(nsub, "rowmax_tm")
            P.wait("tensor", P.tok("vector"))
            for sb_i in range(nsub):
                if bank_free[7] is not None:
                    P.wait("tensor", bank_free[7]); bank_free[7] = None
                transpose(pT[:, 0:128], aB[:, am_off + sb_i * 128: am_off + (sb_i + 1) * 128],
                          identb[:, :])
                bank_ready[7] = P.mark("tensor")

                def _e(sb_i=sb_i):
                    P.do("vector", lambda e, sb_i=sb_i: e.reduce_max(
                        aB[:, rtm + sb_i: rtm + sb_i + 1], pT[:, 0:128], AXX))
                evac(7, "vector", _e)
            srow = arF.al(2 * nsub, "srow_tm")
            ts("vector", aF[:, srow:srow + nsub], aB[:, rtm:rtm + nsub], 1e-5, None, ALU.max)
            recip("vector", aF[:, srow:srow + nsub], aF[:, srow:srow + nsub])
            ts("vector", aF[:, srow:srow + nsub], aF[:, srow:srow + nsub], smax, None, ALU.mult)
            ts("vector", aF[:, srow + nsub:srow + 2 * nsub], aB[:, rtm:rtm + nsub],
               usc, None, ALU.mult)
            tsc_ = P.mark("vector")
            P.wait("tensor", tsc_)
            bank = 6
            if bank_free[bank] is not None:
                P.wait("tensor", bank_free[bank]); bank_free[bank] = None
            transpose(pb[bank][0:2 * nsub, 0:128], aF[:, srow:srow + 2 * nsub], identf_ap[0])
            bank_ready[bank] = P.mark("tensor")
            rows = arF.al(128, "rm_rows")

            def _e3():
                cp("vector", aF[0:2 * nsub, rows:rows + 128], pb[6][0:2 * nsub, 0:128])
            evac(bank, "vector", _e3)
            trow = P.mark("vector")
            P.wait("sync", trow)
            P.wait("sync", P.tok("rbs"))
            dma("sync", rb_scr[0:2 * nsub, :], aF[0:2 * nsub, rows:rows + 128], st="rbs")
            P.wait("sync", P.tok("rbs"))
            src_s = rb_scr[0:nsub, :].unsqueeze(0).broadcast_to((128, nsub, 128))
            src_u = rb_scr[nsub:2 * nsub, :].unsqueeze(0).broadcast_to((128, nsub, 128))
            dma("sync", aF[:, sb_off:sb_off + ncols].rearrange("p (a b) -> p a b", a=nsub),
                src_s, st="rbs")
            dma("sync", aF[:, usb_off:usb_off + ncols].rearrange("p (a b) -> p a b", a=nsub),
                src_u, st="rbs")
            arB.pop(); arF.pop()
            return P.tok("rbs")

        def weight_quant():
            arF.push(); arB.push()
            wbuf = arF.al(2 * 2048, "wq_buf")
            wbB = arB.al(4 * 4096, "wq_bufB")
            accc = arF.al(12, "wq_acc")
            cells = arF.al(8, "wq_cells")
            stg = [arB.al(2048, f"wq_stg{i}") for i in range(6)]
            onef = cst[:, FCBASE + 3:FCBASE + 4]

            def wslot(par):
                if par < 2:
                    return aF[:, wbuf + par * 2048: wbuf + (par + 1) * 2048]
                return aB[:, wbB + (par - 2) * 4096: wbB + (par - 1) * 4096].bitcast(F32)

            NB = 6
            guards = [None] * NB
            sguards = [None] * NB
            mats = [("wg_t", wq_g, NDC, INNER, "swg"), ("wu_t", wq_u, NDC, INNER, "swu"),
                    ("wd_t", wq_d, INNER // 128, D, "swd")]
            for mi, (src, dst, nrow_c, w, sname) in enumerate(mats):
                ncolh = w // 2048 if w > 2048 else 1
                colw = min(w, 2048)
                P.do("vector", lambda e: e.memset(aF[:, accc:accc + 1], 0.0))
                tidx = 0
                for rc in range(nrow_c):
                    for ch in range(ncolh):
                        par = tidx % NB
                        if guards[par] is not None:
                            P.wait("gpsimd", guards[par])
                        if sguards[par % 6] is not None:
                            P.wait("gpsimd", sguards[par % 6])
                        dma("gpsimd", aB[:, stg[par % 6]:stg[par % 6] + colw],
                            io[src][rc * 128:(rc + 1) * 128, ch * 2048: ch * 2048 + colw],
                            st=f"lp{par}")
                        P.wait("vector", P.tok(f"lp{par}"))
                        P.do("vector", lambda e, par=par, colw=colw: e.reduce_sum(
                            aF[:, accc + 1 + par: accc + 2 + par],
                            aB[:, stg[par % 6]:stg[par % 6] + colw], AXX,
                            apply_absolute_value=True))
                        tt("vector", aF[:, accc:accc + 1], aF[:, accc:accc + 1],
                           aF[:, accc + 1 + par: accc + 2 + par], ALU.add)
                        guards[par] = P.mark("vector")
                        tidx += 1
                tacc = P.mark("vector")
                P.wait("tensor", tacc)
                if bank_free[6] is not None:
                    P.wait("tensor", bank_free[6]); bank_free[6] = None
                mm(pb[6][0:1, 0:1], onef, aF[:, accc:accc + 1], True, True)
                bank_ready[6] = P.mark("tensor")

                def _m():
                    cp("vector", dbgc[0:1, 8 + mi:9 + mi], pb[6][0:1, 0:1])
                    ts("vector", aF[0:1, cells:cells + 1], pb[6][0:1, 0:1],
                       1.0 / (nrow_c * 128 * w), None, ALU.mult)
                    ts("vector", aF[0:1, cells:cells + 1], aF[0:1, cells:cells + 1],
                       1e-5, None, ALU.max)
                    recip("vector", aF[0:1, cells + 1:cells + 2], aF[0:1, cells:cells + 1])
                evac(6, "vector", _m)
                tcells = P.mark("vector")
                P.wait("tensor", tcells)
                for (cidx, dst_c) in ((1, 2), (0, 4)):
                    if bank_free[6] is not None:
                        P.wait("tensor", bank_free[6]); bank_free[6] = None
                    mm(pb[6][:, 0:1], rowones_f[:, :],
                       aF[0:1, cells + cidx:cells + cidx + 1], True, True)
                    bank_ready[6] = P.mark("tensor")

                    def _b(dst_c=dst_c):
                        cp("vector", aF[:, cells + dst_c:cells + dst_c + 1], pb[6][:, 0:1])
                    evac(6, "vector", _b)
                CST[sname] = None
                DBG[sname + "_s"] = cells + 2
                DBG[sname + "_m"] = cells + 4
                mw_col = FCBASE - 6 + mi
                cp("vector", cst[:, mw_col:mw_col + 1], aF[:, cells + 4:cells + 5])
                CST["mw" + str(mi)] = mw_col
                tidx = 0
                for rc in range(nrow_c):
                    for ch in range(ncolh):
                        par = tidx % NB
                        if guards[par] is not None:
                            P.wait("sync", guards[par])
                        dma("sync", wslot(par)[:, 0:colw],
                            io[src][rc * 128:(rc + 1) * 128, ch * 2048: ch * 2048 + colw],
                            st=f"lq{par}")
                        P.wait("scalar", P.tok(f"lq{par}"))
                        bo = wslot(par)
                        P.wait("scalar", P.tok("vector"))
                        act("scalar", bo[:, 0:colw], bo[:, 0:colw], AF.Identity,
                            bias=RMAGIC, scale=aF[:, cells + 2:cells + 3])
                        P.wait("vector", P.mark("scalar"))
                        ts("vector", bo[:, 0:colw], bo[:, 0:colw],
                           RMAGIC, 1.0, ALU.subtract, ALU.min)
                        so = stg[par]
                        if sguards[par] is not None:
                            P.wait("vector", sguards[par])
                        P.do("vector", lambda e, bo=bo, colw=colw, so=so: e.tensor_scalar(
                            aB[:, so:so + colw], bo[:, 0:colw],
                            -1.0, None, ALU.max))
                        tq = P.mark("vector")
                        guards[par] = tq
                        P.wait("scalar", tq)
                        dma("scalar", dst[rc * 128:(rc + 1) * 128, ch * 2048: ch * 2048 + colw],
                            aB[:, so:so + colw], st=f"wqs{par}")
                        sguards[par] = P.tok(f"wqs{par}")
                        tidx += 1
            tqs = tuple(P.tok(f"wqs{i}") for i in range(6))
            arF.pop(); arB.pop()
            return tqs

        def l0_ffn(tq_scratch):
            arB.push(); arF.push()
            xn_o = arB.al(NDC * TC, "xn2")
            xn = ColView(aB, xn_o)
            layer_norm(xn)
            P.wait("tensor", P.tok("vector"))
            NPASS = 4 if TC >= 2048 else 2
            TP = TC // NPASS
            NTP = max(1, TP // 512)
            TPH = TP // NTP
            sxb = arF.al(TC, "sx_b")
            usxb = arF.al(TC, "usx_b")
            tmpf = arF.al(TC, "ffn_tmp")
            arB.push()
            am = arB.al(TC, "am")
            P.wait("scalar", P.tok("vector"))
            abst = arB.al(TC, "abs_tmp")
            for dc in range(NDC):
                act("scalar", aB[:, abst:abst + TC], xn[:, dc * TC:(dc + 1) * TC], AF.Abs)
                tabs = P.mark("scalar")
                P.wait("vector", tabs)
                if dc == 0:
                    cp("vector", aB[:, am:am + TC], aB[:, abst:abst + TC])
                else:
                    tt("vector", aB[:, am:am + TC], aB[:, am:am + TC],
                       aB[:, abst:abst + TC], ALU.max)
                tmx = P.mark("vector")
                P.wait("scalar", tmx)
            trb = rowmax_bcast(am, TC, sxb, usxb, 127.0, 1.0 / 127.0)
            P.wait("vector", trb)
            for dc in range(NDC):
                sl = slice(dc * TC, (dc + 1) * TC)
                tt("vector", aF[:, tmpf:tmpf + TC], xn[:, sl], aF[:, sxb:sxb + TC], ALU.mult)
                ts("vector", aF[:, tmpf:tmpf + TC], aF[:, tmpf:tmpf + TC],
                   RMAGIC, RMAGIC, ALU.add, ALU.subtract)
                tt("vector", xn[:, sl], aF[:, tmpf:tmpf + TC], aF[:, usxb:usxb + TC], ALU.mult)
            txdq = P.mark("vector")
            P.wait("tensor", txdq)
            for t_ in tq_scratch:
                P.wait("gpsimd", t_)
            arB.pop()
            if os.environ.get("KFFN", "") == "1":
                arB.pop(); arF.pop(); return

            mid = arB.al(NIC * TP, "mid")
            wsl_g = [arB.al(4096, "wqg0"), arB.al(4096, "wqg1")]
            wsl_u = [arB.al(4096, "wqu0"), arB.al(4096, "wqu1")]
            wsl_d = [arB.al(2048, "wqd0"), arB.al(2048, "wqd1")]
            amM = arB.al(TP, "amM")
            absM = arB.al(TP, "absM")
            smb = arF.al(TP, "sm_b")
            usmb = arF.al(TP, "usm_b")
            gu_guard = [None, None]

            def gu_load(grp, par):
                if gu_guard[par] is not None:
                    P.wait("gpsimd", gu_guard[par])
                    gu_guard[par] = None
                dma("gpsimd", aB[:, wsl_g[par]:wsl_g[par] + 4096]
                    .rearrange("p (dc c) -> p dc c", dc=NDC),
                    wq_g.rearrange("(dc p) c -> p dc c", p=128)[:, :, grp * 512:(grp + 1) * 512],
                    st=f"lg{par}")
                dma("gpsimd", aB[:, wsl_u[par]:wsl_u[par] + 4096]
                    .rearrange("p (dc c) -> p dc c", dc=NDC),
                    wq_u.rearrange("(dc p) c -> p dc c", p=128)[:, :, grp * 512:(grp + 1) * 512],
                    st=f"lu{par}")
                return (P.tok(f"lg{par}"), P.tok(f"lu{par}"))

            d_guard = [None, None]

            def d_load(och, grp, par):
                if d_guard[par] is not None:
                    P.wait("gpsimd", d_guard[par])
                    d_guard[par] = None
                dma("gpsimd", aB[:, wsl_d[par]:wsl_d[par] + 2048]
                    .rearrange("p (ic c) -> p ic c", ic=4),
                    wq_d.rearrange("(ic p) c -> p ic c", p=128)
                    [:, grp * 4:(grp + 1) * 4, och * 512:(och + 1) * 512], st=f"ldd{par}")
                return P.tok(f"ldd{par}")

            tdown_prev = None
            gu_t = {(0, 0): gu_load(0, 0), (0, 1): gu_load(1, 1)}
            for p in range(NPASS):
                t0 = p * TP
                if tdown_prev is not None:
                    P.wait("scalar", tdown_prev)
                    P.wait("vector", tdown_prev)
                for grp in range(8):
                    par = grp % 2
                    tg_, tu_ = gu_t[(p, grp)]
                    P.wait("tensor", tg_); P.wait("tensor", tu_)
                    for ici in range(4):
                        ic = grp * 4 + ici
                        for tch in range(NTP):
                            bg = (2 * (ici * NTP + tch)) % 4
                            bu = bg + 1
                            mm_bank(bg, [(aB[:, wsl_g[par] + dc * 512 + ici * 128:
                                                wsl_g[par] + dc * 512 + ici * 128 + 128],
                                          xn[:, dc * TC + t0 + tch * TPH: dc * TC + t0 + (tch + 1) * TPH],
                                          pb[bg][:, 0:TPH]) for dc in range(NDC)])

                            def _eg(bg=bg, ic=ic, tch=tch):
                                act("scalar", aB[:, mid + ic * TP + tch * TPH: mid + ic * TP + (tch + 1) * TPH],
                                    pb[bg][:, 0:TPH], AF.Silu, scale=cst[:, CST["mw0"]:CST["mw0"] + 1])
                            evac(bg, "scalar", _eg)
                            mm_bank(bu, [(aB[:, wsl_u[par] + dc * 512 + ici * 128:
                                                wsl_u[par] + dc * 512 + ici * 128 + 128],
                                          xn[:, dc * TC + t0 + tch * TPH: dc * TC + t0 + (tch + 1) * TPH],
                                          pb[bu][:, 0:TPH]) for dc in range(NDC)])

                            def _eu(bu=bu, ic=ic, tch=tch, p=p):
                                P.wait("vector", P.tok("scalar"))
                                o = mid + ic * TP + tch * TPH
                                P.do("vector", lambda e, bu=bu, o=o: e.scalar_tensor_tensor(
                                    aB[:, o:o + TPH], pb[bu][:, 0:TPH],
                                    cst[:, CST["mw1"]:CST["mw1"] + 1],
                                    aB[:, o:o + TPH], ALU.mult, ALU.mult))
                                tmid = P.mark("vector")
                                P.wait("scalar", tmid)
                                act("scalar", aB[:, absM + tch * TPH: absM + (tch + 1) * TPH],
                                    aB[:, o:o + TPH], AF.Abs)
                                tam = P.mark("scalar")
                                P.wait("vector", tam)
                                if ic == 0:
                                    cp("vector", aB[:, amM + tch * TPH: amM + (tch + 1) * TPH],
                                       aB[:, absM + tch * TPH: absM + (tch + 1) * TPH])
                                else:
                                    P.do("vector", lambda e, tch=tch: e.tensor_tensor(
                                        aB[:, amM + tch * TPH: amM + (tch + 1) * TPH],
                                        aB[:, amM + tch * TPH: amM + (tch + 1) * TPH],
                                        aB[:, absM + tch * TPH: absM + (tch + 1) * TPH], ALU.max))
                            evac(bu, "vector", _eu)
                    gu_guard[par] = P.tok("tensor")
                    if grp + 2 < 8:
                        gu_t[(p, grp + 2)] = gu_load(grp + 2, par)
                    elif p + 1 < NPASS:
                        gu_t[(p + 1, grp + 2 - 8)] = gu_load(grp + 2 - 8, par)
                if os.environ.get("KFFN", "") == "2":
                    continue
                trb2 = rowmax_bcast(amM, TP, smb, usmb, 127.0, 1.0 / 127.0)
                P.wait("vector", trb2)
                for ic in range(NIC):
                    o = mid + ic * TP
                    tt("vector", aF[:, tmpf:tmpf + TP], aB[:, o:o + TP],
                       aF[:, smb:smb + TP], ALU.mult)
                    P.do("vector", lambda e, o=o: e.tensor_scalar(
                        aB[:, o:o + TP], aF[:, tmpf:tmpf + TP], RMAGIC, RMAGIC,
                        ALU.add, ALU.subtract))
                tmq = P.mark("vector")
                P.wait("tensor", tmq)
                if os.environ.get("KFFN", "") == "3":
                    continue
                d_t = {(0, 0): d_load(0, 0, 0), (0, 1): d_load(0, 1, 1)}
                for och in range(2):
                    for b_ in range(4, 8):
                        if bank_free[b_] is not None:
                            P.wait("tensor", bank_free[b_]); bank_free[b_] = None
                    for grp in range(8):
                        par = grp % 2
                        P.wait("tensor", d_t[(och, grp)])
                        for ici in range(4):
                            ic = grp * 4 + ici
                            for tch in range(NTP):
                                for j in range(4):
                                    bank = 4 + j
                                    mm(pb[bank][:, 0:TPH],
                                       aB[:, wsl_d[par] + ici * 512 + j * 128: wsl_d[par] + ici * 512 + j * 128 + 128],
                                       aB[:, mid + ic * TP + tch * TPH: mid + ic * TP + (tch + 1) * TPH],
                                       ic == 0, ic == NIC - 1)
                        d_guard[par] = P.tok("tensor")
                        if grp + 2 < 8:
                            d_t[(och, grp + 2)] = d_load(och, grp + 2, par)
                        elif och == 0:
                            d_t[(1, grp + 2 - 8)] = d_load(1, grp + 2 - 8, par)
                    tdown = P.mark("tensor")
                    for tch in range(NTP):
                        for j in range(4):
                            oc = och * 4 + j
                            bank = 4 + j
                            bank_ready[bank] = tdown
                            xsl = slice(oc * TC + t0 + tch * TPH, oc * TC + t0 + (tch + 1) * TPH)

                            def _ed(bank=bank, xsl=xsl, tch=tch):
                                P.do("vector", lambda e, bank=bank, tch=tch: e.tensor_tensor(
                                    aF[:, tmpf:tmpf + TPH], pb[bank][:, 0:TPH],
                                    aF[:, usmb + tch * TPH: usmb + (tch + 1) * TPH], ALU.mult))
                                P.do("vector", lambda e, xsl=xsl: e.scalar_tensor_tensor(
                                    x_sb[:, xsl], aF[:, tmpf:tmpf + TPH],
                                    cst[:, CST["mw2"]:CST["mw2"] + 1],
                                    x_sb[:, xsl], ALU.mult, ALU.add))
                            evac(bank, "vector", _ed)
                tdown_prev = P.tok("tensor")
            DBG.update(ffn_mid=mid, ffn_amM=amM, ffn_smb=smb, ffn_usmb=usmb, ffn_sxb=sxb, ffn_usxb=usxb)
            arB.pop(); arF.pop()

        def l1_attn():
            arB.push(); arF.push()
            QT = TC // 2
            xn_o = arB.al(NDC * TC, "xn3")
            xn = ColView(aB, xn_o)
            layer_norm(xn, inner_gb=True)
            P.wait("tensor", P.tok("vector"))
            u_o = arB.al(NDC * TC, "u")
            v_o = arB.al(NDC * TC, "v")
            arB.push()
            wstL = [arB.al(NDC * 128, "wL0"), arB.al(NDC * 128, "wL1")]
            raw_o = wstL[0]
            stg_o = arB.al(2 * TCH, "l1stage")
            bstg_o = arB.al(TCH, "bcstage")
            edgb = arB.al(32, "edges_b")
            est = arF.al(16, "edge_stage")
            edges = arF.al(32, "edges")
            x1row = arF.al(TC + 2, "x1row")
            dtc = arF.al(TCH, "dtc")
            halo = arF.al(4, "halo")

            for dc in range(NDC):
                cp("vector", aF[:, est + dc * 2: est + dc * 2 + 1], xn[:, dc * TC: dc * TC + 1])
                cp("vector", aF[:, est + dc * 2 + 1: est + dc * 2 + 2],
                   xn[:, dc * TC + TC - 1: dc * TC + TC])
            P.wait("sync", P.mark("vector"))
            dma("sync", cc_edge_in[:, :], aF[:, est:est + 16], st="deg")
            P.wait("gpsimd", P.tok("deg"))
            P.do("gpsimd", lambda e: e.collective_compute(
                "AllGather", ALU.bypass, replica_groups=RGROUPS,
                ins=[cc_edge_in.ap()], outs=[cc_edge_out.ap()]), cc="cce")
            P.wait("gpsimd", P.tok("cce"))
            dma("gpsimd", aF[:, edges:edges + 16], cc_edge_out[0:128, :], st="deg2")
            dma("gpsimd", aF[:, edges + 16:edges + 32], cc_edge_out[128:256, :], st="deg2")
            P.wait("vector", P.tok("deg2"))
            cp("vector", aB[:, edgb:edgb + 32], aF[:, edges:edges + 32])
            tedgb = P.mark("vector")

            wguard = [None, None]

            def wload(name, oc):
                slot = oc % 2
                ws = wstL[slot]
                if wguard[slot] is not None:
                    P.wait("gpsimd", wguard[slot])
                dma("gpsimd", aB[:, ws:ws + NDC * 128].rearrange("p (dc c) -> p dc c", dc=NDC),
                    io[name].rearrange("(dc p) c -> p dc c", p=128)[:, :, oc * 128:(oc + 1) * 128],
                    st=f"lwa{slot}")
                P.wait("tensor", P.tok(f"lwa{slot}"))
                return slot, ws

            for oc in range(NDC):
                slot, ws = wload("w1", oc)
                if oc == 0:
                    P.wait("tensor", tedgb)
                mm_bank(5, [(aB[:, ws + dc * 128: ws + dc * 128 + 128],
                             aB[:, edgb + 2 * dc + 1: edgb + 2 * dc + 2],
                             pb[5][:, 0:1]) for dc in range(NDC)] +
                           [(aB[:, ws + dc * 128: ws + dc * 128 + 128],
                             aB[:, edgb + 16 + 2 * dc: edgb + 16 + 2 * dc + 1],
                             pb[5][:, 1:2]) for dc in range(NDC)])

                def _eh(oc=oc):
                    ts("vector", aF[:, halo:halo + 2], pb[5][:, 0:2], cstc("b1", oc),
                       None, ALU.add)
                    tt("vector", aF[:, halo:halo + 2], aF[:, halo:halo + 2],
                       cst[:, CST["meta"]:CST["meta"] + 2], ALU.mult)
                    cp("vector", aF[:, x1row:x1row + 1], aF[:, halo:halo + 1])
                    cp("vector", aF[:, x1row + TC + 1:x1row + TC + 2],
                       aF[:, halo + 1:halo + 2])
                evac(5, "vector", _eh)
                for tc in range(NT):
                    bank = tc % 4
                    mm_bank(bank, [(aB[:, ws + dc * 128: ws + dc * 128 + 128],
                                    xn[:, dc * TC + tc * TCH: dc * TC + (tc + 1) * TCH],
                                    pb[bank][:, 0:TCH]) for dc in range(NDC)])

                    def _ex(bank=bank, oc=oc, tc=tc):
                        act("scalar", aF[:, x1row + 1 + tc * TCH: x1row + 1 + (tc + 1) * TCH],
                            pb[bank][:, 0:TCH], AF.Identity, bias=cstc("b1", oc))
                    evac(bank, "scalar", _ex)
                P.wait("vector", P.tok("scalar"))
                thalo = P.tok("vector")
                for tc in range(NT):
                    lo = x1row + tc * TCH
                    P.wait("vector", P.tok("scalar"))
                    ts("vector", aF[:, dtc:dtc + TCH], aF[:, lo:lo + TCH],
                       cstc("convw", oc * 3 + 0), None, ALU.mult)
                    stt("vector", aF[:, dtc:dtc + TCH], aF[:, lo + 1:lo + 1 + TCH],
                        cstc("convw", oc * 3 + 1), aF[:, dtc:dtc + TCH], ALU.mult, ALU.add)
                    stt("vector", aF[:, dtc:dtc + TCH], aF[:, lo + 2:lo + 2 + TCH],
                        cstc("convw", oc * 3 + 2), aF[:, dtc:dtc + TCH], ALU.mult, ALU.add)
                    P.wait("scalar", P.mark("vector"))
                    act("scalar", aB[:, u_o + oc * TC + tc * TCH: u_o + oc * TC + (tc + 1) * TCH],
                        aF[:, dtc:dtc + TCH], AF.Silu)
                P.wait("scalar", P.tok("vector"))
                wguard[slot] = P.tok("tensor")

            P.wait("tensor", P.tok("scalar"))
            for oc in range(NDC):
                slot, ws = wload("wdt", oc)
                for tc in range(NT):
                    bank = tc % 4
                    mm_bank(bank, [(aB[:, ws + dc * 128: ws + dc * 128 + 128],
                                    aB[:, u_o + dc * TC + tc * TCH: u_o + dc * TC + (tc + 1) * TCH],
                                    pb[bank][:, 0:TCH]) for dc in range(NDC)])

                    def _ed(bank=bank, oc=oc, tc=tc):
                        P.wait("scalar", P.tok("vector"))
                        act("scalar", aF[:, dtc:dtc + TCH], pb[bank][:, 0:TCH],
                            AF.Exp, bias=cstc("bdt", oc))
                        act("scalar", aF[:, dtc:dtc + TCH], aF[:, dtc:dtc + TCH],
                            AF.Ln, bias=1.0)
                        P.wait("vector", P.mark("scalar"))
                        P.wait("vector", P.tok("sdt0"))
                        P.wait("vector", P.tok("sdt1"))
                        cp("vector", aB[:, stg_o:stg_o + TCH], aF[:, dtc:dtc + TCH])
                        tt("vector", aB[:, stg_o + TCH:stg_o + 2 * TCH], aF[:, dtc:dtc + TCH],
                           aB[:, u_o + oc * TC + tc * TCH: u_o + oc * TC + (tc + 1) * TCH],
                           ALU.mult)
                        P.wait("sync", P.mark("vector"))
                        tmid = TC // 2
                        ro = oc * 128
                        g0, g1 = tc * TCH, (tc + 1) * TCH
                        for (lo_, hi_, cc_h) in ((g0, min(g1, tmid), cc_dtA_in),
                                                 (max(g0, tmid), g1, cc_dtB_in)):
                            if lo_ >= hi_:
                                continue
                            co = lo_ - (0 if cc_h is cc_dtA_in else tmid)
                            so_ = lo_ - g0
                            dma("sync", cc_h[0, ro:ro + 128, co:co + (hi_ - lo_)],
                                aB[:, stg_o + so_:stg_o + so_ + (hi_ - lo_)], st="sdt0")
                            dma("sync", cc_h[1, ro:ro + 128, co:co + (hi_ - lo_)],
                                aB[:, stg_o + TCH + so_:stg_o + TCH + so_ + (hi_ - lo_)],
                                st="sdt1")
                    evac(bank, "scalar", _ed)
                wguard[slot] = P.tok("tensor")

            for bci, wname in ((0, "wb"), (1, "wc")):
                if wguard[0] is not None:
                    P.wait("gpsimd", wguard[0]); wguard[0] = None
                dma("gpsimd", aB[:, wstL[0]:wstL[0] + NDC * 16]
                    .rearrange("p (dc c) -> p dc c", dc=NDC),
                    io[wname].rearrange("(dc p) c -> p dc c", p=128), st="lwa0")
                P.wait("tensor", P.tok("lwa0"))
                for tc in range(NT):
                    bank = 4 + tc % 2
                    mm_bank(bank, [(aB[:, wstL[0] + dc * 16: wstL[0] + dc * 16 + 16],
                                    aB[:, u_o + dc * TC + tc * TCH: u_o + dc * TC + (tc + 1) * TCH],
                                    pb[bank][0:16, 0:TCH]) for dc in range(NDC)])

                    def _eb(bank=bank, bci=bci, tc=tc):
                        P.wait("vector", P.tok("sbc"))
                        cp("vector", aB[0:16, bstg_o:bstg_o + TCH], pb[bank][0:16, 0:TCH])
                        P.wait("sync", P.mark("vector"))
                        dma("sync", cc_bc_in[bci, :, tc * TCH:(tc + 1) * TCH],
                            aB[0:16, bstg_o:bstg_o + TCH], st="sbc")
                    evac(bank, "vector", _eb)
                wguard[0] = P.tok("tensor")

            P.wait("gpsimd", P.tok("sdt0"))
            P.wait("gpsimd", P.tok("sdt1"))
            P.wait("gpsimd", P.tok("sbc"))
            P.do("gpsimd", lambda e: e.collective_compute(
                "AllGather", ALU.bypass, replica_groups=RGROUPS,
                ins=[cc_bc_in.ap()], outs=[cc_bc_out.ap()]), cc="ccg")
            tgathBC = P.tok("ccg")
            P.do("gpsimd", lambda e: e.collective_compute(
                "AllGather", ALU.bypass, replica_groups=RGROUPS,
                ins=[cc_dtA_in.ap()], outs=[cc_dtA_out.ap()]), cc="ccg")
            tgathA = P.tok("ccg")
            P.do("gpsimd", lambda e: e.collective_compute(
                "AllGather", ALU.bypass, replica_groups=RGROUPS,
                ins=[cc_dtB_in.ap()], outs=[cc_dtB_out.ap()]), cc="ccg")
            tgath = P.tok("ccg")

            for oc in range(NDC):
                slot, ws = wload("v1", oc)
                for tc in range(NT):
                    bank = tc % 4
                    mm_bank(bank, [(aB[:, ws + dc * 128: ws + dc * 128 + 128],
                                    xn[:, dc * TC + tc * TCH: dc * TC + (tc + 1) * TCH],
                                    pb[bank][:, 0:TCH]) for dc in range(NDC)])

                    def _ev(bank=bank, oc=oc, tc=tc):
                        act("scalar", aB[:, v_o + oc * TC + tc * TCH: v_o + oc * TC + (tc + 1) * TCH],
                            pb[bank][:, 0:TCH], AF.Silu, bias=cstc("bv", oc))
                    evac(bank, "scalar", _ev)
                wguard[slot] = P.tok("tensor")



            P.wait("gpsimd", tgath)
            dt_o = xn_o
            dtu_o = xn_o + 4 * QT
            y_o = xn_o + 8 * QT
            bcB_o = xn_o + 12 * QT
            bcC_o = xn_o + 13 * QT
            inp_o = xn_o + 14 * QT
            h0_o = xn_o + 15 * QT
            arF.push()
            dAs = [arF.al(QT, "dA0"), arF.al(QT, "dA1")]
            car_o = arF.al(64, "carry")
            bc2f = arF.al(QT, "bc2")
            tm2f = arF.al(QT, "tmp2")
            h2f = arF.al(QT // 2, "h2")
            in2f = arF.al(QT // 2, "inp2")
            aBC2 = aF[:, bc2f:bc2f + QT].bitcast(BF16)
            aTM2 = aF[:, tm2f:tm2f + QT].bitcast(BF16)
            inpS = [aB[:, inp_o:inp_o + QT], aF[:, in2f:in2f + QT // 2].bitcast(BF16)]
            bcB = [aB[:, bcB_o:bcB_o + QT], aBC2[:, 0:QT]]
            bcC = [aB[:, bcC_o:bcC_o + QT], aBC2[:, QT:2 * QT]]
            bcBC = [aB[:, bcB_o:bcB_o + 2 * QT].rearrange("p (b t) -> p b t", b=2),
                    aBC2[:, 0:2 * QT].rearrange("p (b t) -> p b t", b=2)]
            tmp = [aTM2[:, 0:QT], aTM2[:, QT:2 * QT]]
            hb = [aB[:, h0_o:h0_o + QT], aF[:, h2f:h2f + QT // 2].bitcast(BF16)]
            m_lo = cstc("meta", 1)
            m_hi = cstc("meta", 0)
            bc_guard = [None, None]

            def bc_load(q_, n_):
                par = n_ % 2
                s_, hf_ = q_ // 2, q_ % 2
                if bc_guard[par] is not None:
                    P.wait("gpsimd", bc_guard[par])
                    bc_guard[par] = None
                dma("gpsimd", bcBC[par],
                    cc_bc_out[s_, :, n_, hf_ * QT:(hf_ + 1) * QT]
                    .unsqueeze(0).broadcast_to((128, 2, QT)), st=f"lbc{par}")
                return P.tok(f"lbc{par}")

            P.wait("gpsimd", P.tok("scalar"))
            P.wait("gpsimd", tgathBC)
            tbc_next = bc_load(0, 0)
            P.wait("sync", P.tok("scalar"))
            for q in range(4):
                s, hf = q // 2, q % 2
                tsrc = cc_dtA_out if hf == 0 else cc_dtB_out
                tgq = tgathA if hf == 0 else tgath
                P.wait("gpsimd", tgq)
                P.wait("gpsimd", P.tok("scalar"))
                P.wait("gpsimd", P.tok("vector"))
                for mi, dsto in ((0, dt_o), (1, dtu_o)):
                    dma("gpsimd", aB[:, dsto:dsto + 4 * QT]
                        .rearrange("p (dcl t) -> p dcl t", dcl=4),
                        tsrc[s, mi].rearrange("(dcl p) t -> p dcl t", p=128)[:, 0:4, 0:QT],
                        st="ldt")
                t_lo = P.tok("ldt")
                P.wait("sync", tgq)
                P.wait("sync", P.tok("scalar"))
                P.wait("sync", P.tok("vector"))
                blend_t = {}
                idx = 0
                for mi, dsto in ((0, dt_o), (1, dtu_o)):
                    for dcl in range(4):
                        par = idx % 2
                        if idx >= 2:
                            P.wait("sync", blend_t[idx - 2])
                        dma("sync", aB[:, raw_o + par * QT: raw_o + (par + 1) * QT],
                            tsrc[s, mi, 512 + dcl * 128: 512 + (dcl + 1) * 128, 0:QT],
                            st=f"ldh{par}")
                        thi = P.tok(f"ldh{par}")
                        d_ = aB[:, dsto + dcl * QT: dsto + (dcl + 1) * QT]
                        P.wait("vector", t_lo)
                        P.wait("vector", thi)
                        ts("vector", d_, d_, m_lo, None, ALU.mult)
                        stt("vector", d_, aB[:, raw_o + par * QT: raw_o + (par + 1) * QT],
                            m_hi, d_, ALU.mult, ALU.add)
                        blend_t[idx] = P.mark("vector")
                        idx += 1
                P.wait("vector", P.tok("sy"))
                P.wait("vector", P.tok("gpsimd"))
                P.wait("scalar", P.tok("vector"))

                def exp_issue(g):
                    n_, dcl_ = g // 4, g % 4
                    act("scalar", aF[:, dAs[g % 2]:dAs[g % 2] + QT],
                        aB[:, dt_o + dcl_ * QT: dt_o + (dcl_ + 1) * QT],
                        AF.Exp, scale=cstc("alog", dcl_ * NST + n_))
                    return P.mark("scalar")

                texp = exp_issue(0)
                gp_tok = {}
                sc_tok = {}
                for n in range(NST):
                    if n + 1 < NST:
                        tbc_cur, tbc_next = tbc_next, bc_load(q, n + 1)
                    elif q + 1 < 4:
                        tbc_cur, tbc_next = tbc_next, bc_load(q + 1, 0)
                    else:
                        tbc_cur = tbc_next
                    P.wait("vector", tbc_cur)
                    par = n % 2
                    for dcl in range(4):
                        g = n * 4 + dcl
                        texp_cur = texp
                        if g + 1 < 4 * NST:
                            P.wait("scalar", P.tok("vector"))
                            texp = exp_issue(g + 1)
                        hB = hb[g % 2]
                        tmB = tmp[g % 2]
                        P.wait("vector", texp_cur)
                        if g - 2 in gp_tok:
                            P.wait("vector", gp_tok[g - 2])
                        if g - 2 in sc_tok:
                            P.wait("vector", sc_tok[g - 2])
                        tt("vector", aB[:, inp_o:inp_o + QT],
                           aB[:, dtu_o + dcl * QT: dtu_o + (dcl + 1) * QT],
                           bcB[par], ALU.mult)
                        cini = 0.0 if q == 0 else aF[:, car_o + dcl * 16 + n: car_o + dcl * 16 + n + 1]
                        P.do("vector", lambda e, ci=cini, hB=hB, g=g: e.tensor_tensor_scan(
                            hB, aF[:, dAs[g % 2]:dAs[g % 2] + QT],
                            aB[:, inp_o:inp_o + QT], ci, ALU.mult, ALU.add))
                        tv1 = P.mark("vector")
                        P.wait("scalar", tv1)
                        act("scalar", aF[:, car_o + dcl * 16 + n: car_o + dcl * 16 + n + 1],
                            hB[:, QT - 1:QT], AF.Identity)
                        sc_tok[g] = P.mark("scalar")
                        if n == 0:
                            tt("vector", aB[:, y_o + dcl * QT: y_o + (dcl + 1) * QT],
                               hB, bcC[par], ALU.mult)
                        elif g % 2 == 0:
                            tt("vector", tmB, hB, bcC[par], ALU.mult)
                            tv2 = P.mark("vector")
                            P.wait("gpsimd", tv2)
                            tt("gpsimd", aB[:, y_o + dcl * QT: y_o + (dcl + 1) * QT],
                               aB[:, y_o + dcl * QT: y_o + (dcl + 1) * QT],
                               tmB, ALU.add)
                            gp_tok[g] = P.mark("gpsimd")
                        else:
                            tt("vector", tmB, hB, bcC[par], ALU.mult)
                            P.wait("vector", P.tok("gpsimd"))
                            tt("vector", aB[:, y_o + dcl * QT: y_o + (dcl + 1) * QT],
                               aB[:, y_o + dcl * QT: y_o + (dcl + 1) * QT],
                               tmB, ALU.add)
                            gp_tok[g] = P.mark("vector")
                    bc_guard[par] = P.tok("vector")
                P.wait("sync", P.mark("vector"))
                P.wait("sync", P.tok("gpsimd"))
                ydst = cc_yA_in if q < 2 else cc_yB_in
                for dcl in range(4):
                    dma("sync", ydst[dcl * 128:(dcl + 1) * 128, hf * QT:(hf + 1) * QT],
                        aB[:, y_o + dcl * QT: y_o + (dcl + 1) * QT], st="sy")
                if q == 1:
                    P.wait("gpsimd", P.tok("sy"))
                    P.do("gpsimd", lambda e: e.collective_compute(
                        "AllGather", ALU.bypass, replica_groups=RGROUPS,
                        ins=[cc_yA_in.ap()], outs=[cc_yA_out.ap()]), cc="ccy")

            arF.pop()
            P.wait("gpsimd", P.tok("sy"))
            P.do("gpsimd", lambda e: e.collective_compute(
                "AllGather", ALU.bypass, replica_groups=RGROUPS,
                ins=[cc_yB_in.ap()], outs=[cc_yB_out.ap()]), cc="ccy")
            P.wait("gpsimd", P.tok("ccy"))
            P.wait("sync", P.tok("ccy"))
            P.wait("gpsimd", P.tok("vector"))
            P.wait("gpsimd", P.tok("scalar"))
            P.wait("vector", P.tok("scalar"))
            arB.pop()
            xs_o = xn_o
            ya_o = [arB.al(TC, "yA_t0"), arB.al(TC, "yA_t1")]
            yb_o = [arF.al(TC, "yB_t0"), arF.al(TC, "yB_t1")]
            yg = [None, None]

            def yr_load(dc):
                par = dc % 2
                if yg[par] is not None:
                    P.wait("gpsimd", yg[par])
                    P.wait("sync", yg[par])
                dma("sync", aB[:, ya_o[par]:ya_o[par] + TC],
                    cc_yA_out[dc * 128:(dc + 1) * 128, :], st=f"lya{par}")
                dma("gpsimd", aF[:, yb_o[par]:yb_o[par] + TC],
                    cc_yB_out[dc * 128:(dc + 1) * 128, :], st=f"lyr{par}")
                return (P.tok(f"lya{par}"), P.tok(f"lyr{par}"))

            yr_t = {0: yr_load(0), 1: yr_load(1)}
            for dc in range(NDC):
                par = dc % 2
                P.wait("vector", yr_t[dc][0])
                P.wait("vector", yr_t[dc][1])
                xs = aB[:, xs_o + dc * TC: xs_o + (dc + 1) * TC]
                ts("vector", xs, aB[:, ya_o[par]:ya_o[par] + TC], m_lo, None, ALU.mult)
                stt("vector", xs, aF[:, yb_o[par]:yb_o[par] + TC], m_hi, xs, ALU.mult, ALU.add)
                stt("vector", xs, aB[:, u_o + dc * TC: u_o + (dc + 1) * TC],
                    cstc("dskip", dc), xs, ALU.mult, ALU.add)
                tt("vector", xs, xs, aB[:, v_o + dc * TC: v_o + (dc + 1) * TC], ALU.mult)
                yg[par] = P.tok("vector")
                if dc + 2 < NDC:
                    yr_t[dc + 2] = yr_load(dc + 2)
            P.wait("tensor", P.mark("vector"))

            wstL2 = [arB.al(NDC * 128, "w2a"), arB.al(NDC * 128, "w2b")]
            wstL[0], wstL[1] = wstL2[0], wstL2[1]
            for oc in range(NDC):
                slot, ws = wload("w2", oc)
                for tc in range(NT):
                    bank = tc % 4
                    mm_bank(bank, [(aB[:, ws + dc * 128: ws + dc * 128 + 128],
                                    aB[:, xs_o + dc * TC + tc * TCH: xs_o + dc * TC + (tc + 1) * TCH],
                                    pb[bank][:, 0:TCH]) for dc in range(NDC)])

                    def _eo(bank=bank, oc=oc, tc=tc):
                        stt("vector", x_sb[:, oc * TC + tc * TCH: oc * TC + (tc + 1) * TCH],
                            pb[bank][:, 0:TCH], cstc("b2", oc),
                            x_sb[:, oc * TC + tc * TCH: oc * TC + (tc + 1) * TCH],
                            ALU.add, ALU.add)
                    evac(bank, "vector", _eo)
                wguard[slot] = P.tok("tensor")
            DBG.update(l1_xn=xn_o, l1_u=u_o, l1_v=v_o, l1_xs=xs_o)
            arB.pop(); arF.pop()

        def l1_ffn():
            arB.push(); arF.push()
            NG = 4
            PG = NEXP // NG
            xn_o = arB.al(NDC * TC, "xn4")
            xn = ColView(aB, xn_o)
            layer_norm(xn)
            P.wait("tensor", P.tok("vector"))
            wrow_o = arB.al(TC, "Wrow")
            wsl = [arB.al(4096, f"moew{i}") for i in range(4)]
            it_o = arB.al(4 * TC, "inter")
            wbc = [arB.al(TC, "wbc0"), arB.al(TC, "wbc1")]
            tmpb_o = arB.al(TCH, "silu_t")
            wtmb_o = arB.al(16, "W_tmb")
            scr = arF.al(TC, "scores")
            ebr = arF.al(16, "ebias_row")
            g0 = arF.al(96, "gate_scratch")
            c_sc, c_sfc, c_tmp, c_rnd = g0, g0 + 16, g0 + 32, g0 + 48
            c_ab, c_cd, c_mix, c_gsc = g0 + 64, g0 + 68, g0 + 72, g0 + 76
            c_m1, c_n1, c_m2, c_n2 = g0 + 80, g0 + 81, g0 + 82, g0 + 83
            c_t1, c_t2, c_thrg, c_thre = g0 + 84, g0 + 85, g0 + 86, g0 + 87
            c_mx, c_ws, c_selg = g0 + 88, g0 + 89, g0 + 92

            for q_ in ("sync", "gpsimd"):
                P.wait(q_, P.tok("scalar"))
                P.wait(q_, P.tok("vector"))
            dma("sync", aF[:, ebr:ebr + 16],
                io["ebias"][0:1, :].broadcast_to((128, NEXP)), st="lgw")
            dma("gpsimd", aB[:, wsl[0]:wsl[0] + 128].rearrange("p (dc e) -> p dc e", dc=NDC),
                io["gatew_t"].rearrange("(dc p) e -> p dc e", p=128), st="lm0")
            P.wait("tensor", P.tok("lm0"))
            for tc in range(NT):
                bank = 5
                mm_bank(bank, [(aB[:, wsl[0] + dc * 16: wsl[0] + dc * 16 + 16],
                                xn[:, dc * TC + tc * TCH: dc * TC + (tc + 1) * TCH],
                                pb[bank][0:16, 0:TCH]) for dc in range(NDC)])

                def _eg(tc=tc):
                    act("scalar", aF[0:16, scr + tc * TCH: scr + (tc + 1) * TCH],
                        pb[5][0:16, 0:TCH], AF.Sigmoid)
                evac(bank, "scalar", _eg)
            wg_e0_gate = P.tok("tensor")
            P.wait("tensor", P.tok("scalar"))
            P.wait("vector", P.tok("lgw"))

            sfc3 = aF[:, c_sfc:c_sfc + 16].rearrange("p (g e) -> p g e", g=NG)
            tmp3 = aF[:, c_tmp:c_tmp + 16].rearrange("p (g e) -> p g e", g=NG)
            selg3 = aF[:, c_selg:c_selg + 4].rearrange("p (g e) -> p g e", g=NG)
            ab3 = aF[:, c_ab:c_ab + 4].rearrange("p (g e) -> p g e", g=NG)
            cd3 = aF[:, c_cd:c_cd + 4].rearrange("p (g e) -> p g e", g=NG)
            mix3 = aF[:, c_mix:c_mix + 4].rearrange("p (g e) -> p g e", g=NG)
            for j in range(TC // 128):
                jc = slice(j * 128, (j + 1) * 128)
                bank = 6
                if bank_free[bank] is not None:
                    P.wait("tensor", bank_free[bank]); bank_free[bank] = None
                transpose(pb[6][:, 0:16], aF[0:16, scr + j * 128: scr + (j + 1) * 128],
                          identf[0:16, 0:16])
                bank_ready[bank] = P.mark("tensor")

                def _etm():
                    cp("vector", aF[:, c_sc:c_sc + 16], pb[6][:, 0:16])
                evac(bank, "vector", _etm)
                tt("vector", aF[:, c_sfc:c_sfc + 16], aF[:, c_sc:c_sc + 16],
                   aF[:, ebr:ebr + 16], ALU.add)
                tt("vector", ab3[:, :, 0:1], sfc3[:, :, 0:1], sfc3[:, :, 1:2], ALU.add)
                tt("vector", cd3[:, :, 0:1], sfc3[:, :, 2:3], sfc3[:, :, 3:4], ALU.add)
                tt("vector", mix3[:, :, 0:1], sfc3[:, :, 0:1], sfc3[:, :, 1:2], ALU.max)
                tt("vector", selg3[:, :, 0:1], sfc3[:, :, 2:3], sfc3[:, :, 3:4], ALU.max)
                tt("vector", aF[:, c_mix:c_mix + 4], aF[:, c_mix:c_mix + 4],
                   aF[:, c_selg:c_selg + 4], ALU.add)
                tt("vector", aF[:, c_gsc:c_gsc + 4], aF[:, c_ab:c_ab + 4],
                   aF[:, c_cd:c_cd + 4], ALU.max)
                tt("vector", aF[:, c_gsc:c_gsc + 4], aF[:, c_gsc:c_gsc + 4],
                   aF[:, c_mix:c_mix + 4], ALU.max)
                tt("vector", aF[:, c_m1:c_m1 + 1], aF[:, c_gsc:c_gsc + 1],
                   aF[:, c_gsc + 1:c_gsc + 2], ALU.max)
                tt("vector", aF[:, c_n1:c_n1 + 1], aF[:, c_gsc:c_gsc + 1],
                   aF[:, c_gsc + 1:c_gsc + 2], ALU.min)
                tt("vector", aF[:, c_m2:c_m2 + 1], aF[:, c_gsc + 2:c_gsc + 3],
                   aF[:, c_gsc + 3:c_gsc + 4], ALU.max)
                tt("vector", aF[:, c_n2:c_n2 + 1], aF[:, c_gsc + 2:c_gsc + 3],
                   aF[:, c_gsc + 3:c_gsc + 4], ALU.min)
                tt("vector", aF[:, c_t1:c_t1 + 1], aF[:, c_m1:c_m1 + 1],
                   aF[:, c_m2:c_m2 + 1], ALU.min)
                tt("vector", aF[:, c_t2:c_t2 + 1], aF[:, c_n1:c_n1 + 1],
                   aF[:, c_n2:c_n2 + 1], ALU.max)
                tt("vector", aF[:, c_thrg:c_thrg + 1], aF[:, c_t1:c_t1 + 1],
                   aF[:, c_t2:c_t2 + 1], ALU.max)
                ts("vector", aF[:, c_selg:c_selg + 4], aF[:, c_gsc:c_gsc + 4],
                   aF[:, c_thrg:c_thrg + 1], None, ALU.is_ge)
                tt("vector", tmp3[:, :, :], sfc3[:, :, :],
                   selg3[:, :, 0:1].broadcast_to((128, NG, PG)), ALU.mult)
                cp("vector", aF[:, c_rnd:c_rnd + 16], aF[:, c_tmp:c_tmp + 16])
                for k in range(4):
                    P.do("vector", lambda e: e.reduce_max(
                        aF[:, c_mx:c_mx + 1], aF[:, c_rnd:c_rnd + 16], AXX))
                    if k < 3:
                        ts("vector", aF[:, c_ab:c_ab + 16], aF[:, c_rnd:c_rnd + 16],
                           aF[:, c_mx:c_mx + 1], None, ALU.is_equal)
                        stt("vector", aF[:, c_rnd:c_rnd + 16], aF[:, c_ab:c_ab + 16],
                            -1e30, aF[:, c_rnd:c_rnd + 16], ALU.mult, ALU.add)
                ts("vector", aF[:, c_tmp:c_tmp + 16], aF[:, c_tmp:c_tmp + 16],
                   aF[:, c_mx:c_mx + 1], None, ALU.is_ge)
                tt("vector", aF[:, c_tmp:c_tmp + 16], aF[:, c_tmp:c_tmp + 16],
                   aF[:, c_sc:c_sc + 16], ALU.mult)
                P.do("vector", lambda e: e.reduce_sum(
                    aF[:, c_ws:c_ws + 1], aF[:, c_tmp:c_tmp + 16], AXX))
                ts("vector", aF[:, c_ws:c_ws + 1], aF[:, c_ws:c_ws + 1], 1e-20, None, ALU.add)
                recip("vector", aF[:, c_ws:c_ws + 1], aF[:, c_ws:c_ws + 1])
                ts("vector", aF[:, c_ws:c_ws + 1], aF[:, c_ws:c_ws + 1], 2.5, None, ALU.mult)
                ts("vector", aB[:, wtmb_o:wtmb_o + 16], aF[:, c_tmp:c_tmp + 16],
                   aF[:, c_ws:c_ws + 1], None, ALU.mult)
                twm = P.mark("vector")
                P.wait("tensor", twm)
                if bank_free[7] is not None:
                    P.wait("tensor", bank_free[7]); bank_free[7] = None
                transpose(pT[0:16, 0:128], aB[:, wtmb_o:wtmb_o + 16], identb[:, :])
                bank_ready[7] = P.mark("tensor")

                def _ewr(j=j):
                    cp("vector", aB[0:16, wrow_o + j * 128: wrow_o + (j + 1) * 128],
                       pT[0:16, 0:128])
                evac(7, "vector", _ewr)
            P.wait("sync", P.mark("vector"))
            dma("sync", wsc_scr[:, :], aB[0:16, wrow_o:wrow_o + TC], st="swsc")

            wg_e = [wg_e0_gate, None, None, None]
            wbcg = [None, None]
            tdown = None

            def eload(slot, src_ap, nin=NDC):
                if wg_e[slot] is not None:
                    P.wait("gpsimd", wg_e[slot])
                    wg_e[slot] = None
                dma("gpsimd",
                    aB[:, wsl[slot]:wsl[slot] + 4096].rearrange("p (b i) -> p b i", b=nin),
                    src_ap, st=f"lm{slot}")
                return P.tok(f"lm{slot}")

            for e in range(NEXP + 1):
                shared = (e == NEXP)
                sg, su, sd = (3 * e) % 4, (3 * e + 1) % 4, (3 * e + 2) % 4
                if shared:
                    tg = eload(sg, io["sg_t"].rearrange("(dc p) i -> p dc i", p=128))
                    tu = eload(su, io["su_t"].rearrange("(dc p) i -> p dc i", p=128))
                    td = eload(sd, io["sd_t"].rearrange("(ic p) o -> p ic o", p=128), nin=4)
                else:
                    tg = eload(sg, io["gp_t"][e].rearrange("(dc p) i -> p dc i", p=128))
                    tu = eload(su, io["up_t"][e].rearrange("(dc p) i -> p dc i", p=128))
                    td = eload(sd, io["dp_t"][e].rearrange("(ic p) o -> p ic o", p=128), nin=4)
                    if wbcg[e % 2] is not None:
                        P.wait("gpsimd", wbcg[e % 2])
                    P.wait("gpsimd", P.tok("swsc"))
                    dma("gpsimd", aB[:, wbc[e % 2]:wbc[e % 2] + TC],
                        wsc_scr[e:e + 1, :].broadcast_to((128, TC)), st="lwb")
                if tdown is not None:
                    P.wait("vector", tdown)
                P.wait("tensor", tg)
                P.wait("tensor", tu)
                if not shared:
                    P.wait("vector", P.tok("lwb"))

                def emit_downs(tc, tw, sd=sd):
                    P.wait("tensor", tw)
                    P.wait("tensor", td)
                    for oc in range(NDC):
                        bank = 4 + (oc + tc) % 4
                        mm_bank(bank, [(aB[:, wsl[sd] + ic * 1024 + oc * 128:
                                              wsl[sd] + ic * 1024 + oc * 128 + 128],
                                        aB[:, it_o + ic * TC + tc * TCH:
                                              it_o + ic * TC + (tc + 1) * TCH],
                                        pb[bank][:, 0:TCH]) for ic in range(4)])

                        def _ed(bank=bank, oc=oc, tc=tc):
                            xs = x_sb[:, oc * TC + tc * TCH: oc * TC + (tc + 1) * TCH]
                            tt("vector", xs, xs, pb[bank][:, 0:TCH], ALU.add)
                        evac(bank, "vector", _ed)

                pend = None
                for tc in range(NT):
                    for ic in range(4):
                        bg, bu = ic % 2, 2 + ic % 2
                        mm_bank(bg, [(aB[:, wsl[sg] + (dc * 4 + ic) * 128:
                                            wsl[sg] + (dc * 4 + ic) * 128 + 128],
                                      xn[:, dc * TC + tc * TCH: dc * TC + (tc + 1) * TCH],
                                      pb[bg][:, 0:TCH]) for dc in range(NDC)])

                        def _es(bg=bg):
                            P.wait("scalar", P.tok("vector"))
                            act("scalar", aB[:, tmpb_o:tmpb_o + TCH], pb[bg][:, 0:TCH],
                                AF.Silu)
                        evac(bg, "scalar", _es)
                        tsi = P.tok("scalar")
                        mm_bank(bu, [(aB[:, wsl[su] + (dc * 4 + ic) * 128:
                                            wsl[su] + (dc * 4 + ic) * 128 + 128],
                                      xn[:, dc * TC + tc * TCH: dc * TC + (tc + 1) * TCH],
                                      pb[bu][:, 0:TCH]) for dc in range(NDC)])

                        def _ei(bu=bu, ic=ic, tc=tc, tsi=tsi, e=e, shared=shared):
                            P.wait("vector", tsi)
                            dst = aB[:, it_o + ic * TC + tc * TCH: it_o + ic * TC + (tc + 1) * TCH]
                            tt("vector", dst, aB[:, tmpb_o:tmpb_o + TCH],
                               pb[bu][:, 0:TCH], ALU.mult)
                            if not shared:
                                tt("vector", dst, dst,
                                   aB[:, wbc[e % 2] + tc * TCH: wbc[e % 2] + (tc + 1) * TCH],
                                   ALU.mult)
                        evac(bu, "vector", _ei)
                    tw = P.tok("vector")
                    if pend is not None:
                        emit_downs(*pend)
                    pend = (tc, tw)
                emit_downs(*pend)
                tdown = P.tok("tensor")
                if not shared:
                    wbcg[e % 2] = P.tok("vector")
                for sl in (sg, su, sd):
                    wg_e[sl] = P.tok("tensor")
            DBG.update(moe_xn=xn_o, moe_wrow=wrow_o, moe_inter=it_o, moe_scr=scr)
            arB.pop(); arF.pop()

        if upto >= 1:
            l0_attn()
            P.barrier()
        if upto >= 2:
            tq_scratch = weight_quant()
            P.barrier()
            if os.environ.get("KWQONLY", "") != "1":
                l0_ffn(tq_scratch)
                P.barrier()
        if upto >= 3:
            l1_attn()
            P.barrier()
        if upto >= 4:
            l1_ffn()
            P.barrier()

        for tname, shape, fn in taps:
            src = fn(dict(x_sb=x_sb, aB=aB, aF=aF, dbgc=dbgc))
            q = "gpsimd" if src.dtype != F32 else "sync"
            dma(q, tapd[tname][:, :], src, st="dout" if q == "sync" else "dout2")
        for dc in range(NDC):
            dma("sync", out_fm[dc * 128:(dc + 1) * 128, :], x_sb[:, dc * TC:(dc + 1) * TC], st="dout")
        tout = P.tok("dout")
        P.wait("sync", tout)

        P.check()
        sems = {}
        for sname in list(ENGS) + sorted(P.dma_sems):
            sems[sname] = stx.enter_context(nc.semaphore(sname))
        P.replay(stx.enter_context(nc.Block()), sems)

    return nc


def stage_inputs(inputs, TC):
    B = 4
    NSEQ = 2 * TC
    x = np.asarray(inputs["L0_attn_x"], np.float32)

    def t(a):
        return np.ascontiguousarray(np.asarray(a, np.float32).T)

    rf = np.asarray(inputs["L0_attn_RF"], np.float32)
    rfblk = np.zeros((NPAIR, 128, 128), np.float32)
    for hp in range(NPAIR):
        rfblk[hp, 0:64, 0:64] = rf[2 * hp]
        rfblk[hp, 64:128, 64:128] = rf[2 * hp + 1]

    def pcol(v):
        return np.ascontiguousarray(np.asarray(v, np.float32).reshape(NDC, 128).T)

    base = {
        "wqkv": np.ascontiguousarray(np.asarray(inputs["L0_attn_W_qkv"], np.float32)),
        "rfblk": rfblk,
        "wout": np.ascontiguousarray(np.asarray(inputs["L0_attn_W_out"], np.float32)),
        "bout": pcol(inputs["L0_attn_b_out"]),
        "wg_t": t(inputs["L0_ffn_w_gate"]),
        "wu_t": t(inputs["L0_ffn_w_up"]),
        "wd_t": t(inputs["L0_ffn_w_down"]),
        "lng": pcol(inputs["L1_attn_ln_g"]),
        "lnb": pcol(inputs["L1_attn_ln_b"]),
        "w1": np.ascontiguousarray(np.asarray(inputs["L1_attn_w1_w"], np.float32)),
        "b1": pcol(inputs["L1_attn_w1_b"]),
        "v1": np.ascontiguousarray(np.asarray(inputs["L1_attn_v1_w"], np.float32)),
        "bv": pcol(inputs["L1_attn_v1_b"]),
        "w2": np.ascontiguousarray(np.asarray(inputs["L1_attn_w2_w"], np.float32)),
        "b2": pcol(inputs["L1_attn_w2_b"]),
        "wdt": np.ascontiguousarray(np.asarray(inputs["L1_attn_W_dt"], np.float32)),
        "bdt": pcol(inputs["L1_attn_b_dt"]),
        "convw": np.ascontiguousarray(
            np.asarray(inputs["L1_attn_conv_w"], np.float32).reshape(NDC, 128, 3)
            .transpose(1, 0, 2).reshape(128, NDC * 3)),
        "dskip": pcol(inputs["L1_attn_D_skip"]),
        "wb": np.ascontiguousarray(np.asarray(inputs["L1_attn_W_B"], np.float32)),
        "wc": np.ascontiguousarray(np.asarray(inputs["L1_attn_W_C"], np.float32)),
        "gatew_t": t(inputs["L1_ffn_gate_w"]),
        "ebias": np.asarray(inputs["L1_ffn_e_bias"], np.float32).reshape(1, NEXP),
        "gp_t": np.ascontiguousarray(np.asarray(inputs["L1_ffn_gate_proj"], np.float32).transpose(0, 2, 1)),
        "up_t": np.ascontiguousarray(np.asarray(inputs["L1_ffn_up_proj"], np.float32).transpose(0, 2, 1)),
        "dp_t": np.ascontiguousarray(np.asarray(inputs["L1_ffn_down_proj"], np.float32).transpose(0, 2, 1)),
        "sg_t": t(inputs["L1_ffn_shared_gate_w"]),
        "su_t": t(inputs["L1_ffn_shared_up_w"]),
        "sd_t": t(inputs["L1_ffn_shared_down_w"]),
    }
    alog3 = np.asarray(inputs["L1_attn_A_log"], np.float32).reshape(NDC, 128, NST)
    in_maps = []
    for c in range(8):
        b, h = c // 2, c % 2
        m = dict(base)
        m["x_fm"] = np.ascontiguousarray(x[b, h * TC:(h + 1) * TC, :].T)
        meta = np.zeros((128, 4), np.float32)
        meta[:, 0] = 1.0 if h == 1 else 0.0
        meta[:, 1] = 1.0 if h == 0 else 0.0
        m["meta"] = meta
        m["alog"] = np.ascontiguousarray(
            np.roll(alog3, -4 * h, axis=0).transpose(1, 0, 2).reshape(128, NDC * NST))
        in_maps.append(m)
    return in_maps


def assemble(res, TC):
    x = np.zeros((4, 2 * TC, D), np.float32)
    for c in range(8):
        b, h = c // 2, c % 2
        x[b, h * TC:(h + 1) * TC, :] = res.results[c]["out_fm"].T
    return x


def kernel(**inputs):
    TC = 2048
    nc = build(TC, upto=4)
    in_maps = stage_inputs(inputs, TC)
    res = run_bass_kernel_spmd(nc, in_maps, core_ids=list(range(8)))
    return assemble(res, TC)


# version 13
# speedup vs baseline: 1.1603x; 1.0117x over previous
import contextlib
import os

import numpy as np
import concourse.bass as bass
import concourse.mybir as mybir
from concourse.bass_utils import run_bass_kernel_spmd

F32 = mybir.dt.float32
BF16 = mybir.dt.bfloat16
AF = mybir.ActivationFunctionType
ALU = mybir.AluOpType
AXX = mybir.AxisListType.X

D = 1024
NDC = 8
HEADS = 16
NPAIR = 8
INNER = 4096
NIC = 32
NST = 16
MOE_I = 512
NEXP = 16
RGROUPS = [[0, 1], [2, 3], [4, 5], [6, 7]]
RMAGIC = 3 * 2.0 ** 22
EPS = 1e-5

ENGS = ("tensor", "vector", "scalar", "gpsimd", "sync")


class Plan:

    SELF_SERIAL = ("vector", "scalar", "gpsimd")

    def __init__(self):
        self.stream = {e: [] for e in ENGS}
        self.cnt = {}
        self.dirty = {e: None for e in ENGS}
        self.dma_sems = set()

    def do(self, eng, fn, dma=None, cc=None):
        if cc:
            step = {"fn": fn, "incs": [(cc, 1)], "sem": cc}
            self.cnt[cc] = self.cnt.get(cc, 0) + 1
            self.dma_sems.add(cc)
        elif dma:
            sem = dma
            step = {"fn": fn, "incs": [(sem, 16)], "sem": sem}
            self.cnt[sem] = self.cnt.get(sem, 0) + 16
            self.dma_sems.add(sem)
        else:
            if eng in self.SELF_SERIAL and self.cnt.get(eng, 0) > 0:
                self.stream[eng].append({"wait": (eng, self.cnt[eng])})
            step = {"fn": fn, "incs": [(eng, 1)], "sem": eng}
            self.cnt[eng] = self.cnt.get(eng, 0) + 1
        step["tok"] = (step["sem"], self.cnt[step["sem"]])
        self.stream[eng].append(step)
        self.dirty[eng] = step
        return step

    def mark(self, eng, sem=None):
        step = self.dirty[eng]
        assert step is not None, f"mark({eng}) with no dirty step"
        return step["tok"]

    def tok(self, sem):
        return (sem, self.cnt.get(sem, 0))

    def wait(self, eng, token):
        if token is None:
            return
        sem, val = token
        if val <= 0:
            return
        self.stream[eng].append({"wait": (sem, val)})

    def barrier(self):
        toks = []
        for e in ENGS:
            if self.dirty[e] is not None:
                toks.append(self.dirty[e]["tok"])
                self.dirty[e] = None
        for d in sorted(self.dma_sems):
            if self.cnt.get(d, 0) > 0:
                toks.append(self.tok(d))
        for e in ENGS:
            for t in toks:
                self.wait(e, t)

    def check(self):
        pos = {e: 0 for e in ENGS}
        sem = {}
        progressed = True
        while progressed:
            progressed = False
            for e in ENGS:
                items = self.stream[e]
                while pos[e] < len(items):
                    it = items[pos[e]]
                    if "wait" in it:
                        sm, v = it["wait"]
                        if sem.get(sm, 0) < v:
                            break
                    else:
                        for (sm, k) in it["incs"]:
                            sem[sm] = sem.get(sm, 0) + k
                    pos[e] += 1
                    progressed = True
        stuck = {e: (pos[e], len(self.stream[e])) for e in ENGS if pos[e] < len(self.stream[e])}
        if stuck:
            det = {}
            for e in stuck:
                it = self.stream[e][pos[e]]
                det[e] = ("waits", it["wait"], "have", sem.get(it["wait"][0], 0)) if "wait" in it else "??"
            raise RuntimeError(f"PLAN DEADLOCK: {stuck} {det}")

    def replay(self, blk, sems):
        for e in ENGS:
            items = self.stream[e]

            def run(eng, items=items):
                prev_wait = None
                for it in items:
                    if "wait" in it:
                        if it["wait"] == prev_wait:
                            continue
                        prev_wait = it["wait"]
                        sm, v = it["wait"]
                        eng.wait_ge(sems[sm], v)
                    else:
                        instr = it["fn"](eng)
                        for (sm, k) in it["incs"]:
                            instr.then_inc(sems[sm], k)
            getattr(blk, e)(run)


class Arena:
    def __init__(self, n, name):
        self.n, self.name = n, name
        self.off = 0
        self.marks = []
        self.peak = 0

    def al(self, cols, what=""):
        o = self.off
        self.off += cols
        self.peak = max(self.peak, self.off)
        assert self.off <= self.n, f"{self.name} overflow at {what}: {self.off}>{self.n}"
        return o

    def push(self):
        self.marks.append(self.off)

    def pop(self):
        self.off = self.marks.pop()


class ColView:

    def __init__(self, t, off):
        self.t, self.off = t, off

    def __getitem__(self, key):
        p, c = key
        return self.t[p, c.start + self.off: c.stop + self.off]

    @property
    def dtype(self):
        return self.t[:, 0:1].dtype


DBG = {}


def build(TC=2048, upto=2, taps=()):
    NT = max(1, TC // 512)
    TCH = TC // NT
    NSUB = TC // 128
    nc = bass.Bass("TRN2", target_bir_lowering=False, debug=False)

    io = {}

    def EI(name, shape):
        io[name] = nc.dram_tensor(name, list(shape), F32, kind="ExternalInput")

    EI("x_fm", (D, TC))
    EI("wqkv", (D, 3 * D)); EI("rfblk", (NPAIR, 128, 128))
    EI("wout", (D, D)); EI("bout", (128, NDC))
    EI("wg_t", (D, INNER)); EI("wu_t", (D, INNER)); EI("wd_t", (INNER, D))
    EI("lng", (128, NDC)); EI("lnb", (128, NDC))
    EI("w1", (D, D)); EI("b1", (128, NDC))
    EI("v1", (D, D)); EI("bv", (128, NDC))
    EI("w2", (D, D)); EI("b2", (128, NDC))
    EI("wdt", (D, D)); EI("bdt", (128, NDC))
    EI("convw", (128, NDC * 3)); EI("alog", (128, NDC * NST)); EI("dskip", (128, NDC))
    EI("wb", (D, NST)); EI("wc", (D, NST))
    EI("gatew_t", (D, NEXP)); EI("ebias", (1, NEXP))
    EI("gp_t", (NEXP, D, MOE_I)); EI("up_t", (NEXP, D, MOE_I)); EI("dp_t", (NEXP, MOE_I, D))
    EI("sg_t", (D, MOE_I)); EI("su_t", (D, MOE_I)); EI("sd_t", (MOE_I, D))
    EI("meta", (128, 4))

    out_fm = nc.dram_tensor("out_fm", [D, TC], F32, kind="ExternalOutput")
    tapd = {}
    for tname, shape, _ in taps:
        tapd[tname] = nc.dram_tensor("tap_" + tname, list(shape), F32, kind="ExternalOutput")

    cc_kv_in = nc.dram_tensor("cc_kv_in", [64, HEADS * 65], F32)
    cc_kv_out = nc.dram_tensor("cc_kv_out", [64, HEADS * 65], F32)
    cc_edge_in = nc.dram_tensor("cc_edge_in", [128, NDC * 2], F32)
    cc_edge_out = nc.dram_tensor("cc_edge_out", [2 * 128, NDC * 2], F32)
    cc_dtA_in = nc.dram_tensor("cc_dtA_in", [2, D, TC // 2], BF16)
    cc_dtA_out = nc.dram_tensor("cc_dtA_out", [2, 2, D, TC // 2], BF16)
    cc_dtB_in = nc.dram_tensor("cc_dtB_in", [2, D, TC // 2], BF16)
    cc_dtB_out = nc.dram_tensor("cc_dtB_out", [2, 2, D, TC // 2], BF16)
    cc_bc_in = nc.dram_tensor("cc_bc_in", [2, NST, TC], BF16)
    cc_bc_out = nc.dram_tensor("cc_bc_out", [2, 2, NST, TC], BF16)
    cc_yA_in = nc.dram_tensor("cc_yA_in", [512, TC], BF16)
    cc_yA_out = nc.dram_tensor("cc_yA_out", [D, TC], BF16)
    cc_yB_in = nc.dram_tensor("cc_yB_in", [512, TC], BF16)
    cc_yB_out = nc.dram_tensor("cc_yB_out", [D, TC], BF16)
    wq_g = nc.dram_tensor("wq_g", [D, INNER], BF16)
    rb_scr = nc.dram_tensor("rb_scr", [64, 128], F32)
    wq_u = nc.dram_tensor("wq_u", [D, INNER], BF16)
    wq_d = nc.dram_tensor("wq_d", [INNER, D], BF16)
    wsc_scr = nc.dram_tensor("wsc_scr", [NEXP, TC], BF16)

    P = Plan()

    with contextlib.ExitStack() as stx:
        def SB(name, shape, dtype):
            return stx.enter_context(nc.sbuf_tensor(name, list(shape), dtype))

        def PSUM(name, shape, dtype=F32):
            return stx.enter_context(nc.psum_tensor(name, list(shape), dtype))

        x_sb = SB("x_sb", (128, NDC * TC), F32)
        ABN = max(TC * 28 + 1024, 36000) if TC <= 1024 else 55296
        AFN = 8064 if TC >= 2048 else TC * 4 + 3072
        aB = SB("aB", (128, ABN), BF16)
        aF = SB("aF", (128, AFN), F32)
        cst = SB("cst", (128, 244), F32)
        ones_b = SB("ones_b", (128, 2), BF16)
        rowones_f = SB("rowones_f", (1, 128), F32)
        selp = SB("selp", (16, 128 * NPAIR), BF16)
        identb = SB("identb", (128, 128), BF16)
        identf = SB("identf", (128, 128), F32)
        identf_ap = [None]
        dbgc = SB("dbgc", (1, 16), F32)
        P.do("vector", lambda e: e.memset(dbgc[:, :], 0.0))

        pb = [PSUM(f"pb{i}", (128, 512), F32) for i in range(8)]
        pT = pb[7][:, :].bitcast(BF16)


        def dma(queue, dst, src, st="ld0"):
            return P.do(queue, lambda e, d=dst, s=src: e.dma_start(out=d, in_=s), dma=st)

        def mm(dst, lhsT, rhs, start, stop):
            return P.do("tensor", lambda e, d=dst, l=lhsT, r=rhs, st=start, sp=stop:
                        e.matmul(d, l, r, start=st, stop=sp, skip_group_check=True))

        def act(eng, dst, src, func, bias=0.0, scale=1.0):
            return P.do(eng, lambda e, d=dst, s=src, f=func, b=bias, sc=scale:
                        e.activation(d, s, f, bias=b, scale=sc))

        def tt(eng, dst, a, b, op):
            return P.do(eng, lambda e, d=dst, x=a, y=b, o=op: e.tensor_tensor(d, x, y, o))

        def ts(eng, dst, a, s1, s2, op0, op1=ALU.bypass):
            return P.do(eng, lambda e, d=dst, x=a, u=s1, v=s2, o0=op0, o1=op1:
                        e.tensor_scalar(d, x, u, v, o0, o1))

        def stt(eng, dst, in0, scal, in1, op0, op1):
            return P.do(eng, lambda e, d=dst, a=in0, s=scal, b=in1, o0=op0, o1=op1:
                        e.scalar_tensor_tensor(d, a, s, b, o0, o1))

        def cp(eng, dst, src):
            return P.do(eng, lambda e, d=dst, s=src: e.tensor_copy(d, s))

        def recip(eng, dst, src):
            return P.do(eng, lambda e, d=dst, s=src: e.reciprocal(d, s))

        def transpose(dst_psum, src, ident):
            return P.do("tensor", lambda e, d=dst_psum, s=src, i=ident: e.transpose(d, s, i))

        bank_free = [None] * 8
        bank_ready = [None] * 8

        def mm_bank(bank, mms):
            if bank_free[bank] is not None:
                P.wait("tensor", bank_free[bank])
                bank_free[bank] = None
            n = len(mms)
            for i, (l, r, dap) in enumerate(mms):
                mm(dap if dap is not None else pb[bank][:, 0:TCH], l, r, i == 0, i == n - 1)
            bank_ready[bank] = P.mark("tensor")
            return bank_ready[bank]

        def evac(bank, eng, emit):
            P.wait(eng, bank_ready[bank])
            emit()
            bank_free[bank] = P.mark(eng)

        arB = Arena(ABN, "aB")
        arF = Arena(AFN, "aF")

        dma("sync", x_sb[:, :].rearrange("p (dc t) -> p dc t", dc=NDC),
            io["x_fm"].rearrange("(dc p) t -> p dc t", p=128))
        CST = {}
        _off = [0]
        for nm, w in [("bout", NDC), ("b1", NDC), ("bv", NDC), ("b2", NDC), ("bdt", NDC),
                      ("lng", NDC), ("lnb", NDC), ("convw", NDC * 3), ("dskip", NDC),
                      ("alog", NDC * NST), ("meta", 4)]:
            CST[nm] = _off[0]
            dma("sync", cst[:, _off[0]:_off[0] + w], io[nm][:, :])
            _off[0] += w

        def cstc(name, i=0):
            return cst[:, CST[name] + i: CST[name] + i + 1]

        P.do("vector", lambda e: e.memset(ones_b[:, :], 1.0))
        P.do("vector", lambda e: e.memset(rowones_f[:, :], 1.0))
        _fc = [0]
        FCBASE = 228

        def fconst(v):
            col = FCBASE + _fc[0]
            _fc[0] += 1
            assert col < 256
            P.do("vector", lambda e, c=col, vv=v: e.memset(cst[:, c:c + 1], vv))
            nc.const_aps.register(F32, float(v), cst[:, col:col + 1])
            return float(v)

        for _v in (EPS, EPS * EPS, 1e-6, 1.0, RMAGIC):
            fconst(_v)
        SKIP_CONSTS = os.environ.get("KSKIP", "")
        if "id" not in SKIP_CONSTS:
            P.do("gpsimd", lambda e: e.memset(identb[:, :], 0.0))
        if "id" not in SKIP_CONSTS:
            P.wait("gpsimd", P.mark("gpsimd"))
            P.do("gpsimd", lambda e: e.affine_select(
                out=identb[:, :], in_=identb[:, :], compare_op=ALU.not_equal,
                fill=1.0, base=0, pattern=[[-1, 128]], channel_multiplier=1))
            P.do("gpsimd", lambda e: e.memset(identf[:, :], 0.0))
            P.wait("gpsimd", P.mark("gpsimd"))
            P.do("gpsimd", lambda e: e.affine_select(
                out=identf[:, :], in_=identf[:, :], compare_op=ALU.not_equal,
                fill=1.0, base=0, pattern=[[-1, 128]], channel_multiplier=1))
        if "selp" not in SKIP_CONSTS:
            P.do("gpsimd", lambda e: e.iota(aF[0:16, 0:128], pattern=[[1, 128]], base=0,
                                            channel_multiplier=0, allow_small_or_imprecise_dtypes=True))
            P.do("gpsimd", lambda e: e.iota(aF[0:16, 128:256], pattern=[[0, 128]], base=0,
                                            channel_multiplier=1, allow_small_or_imprecise_dtypes=True))
        P.barrier()
        if "selp" not in SKIP_CONSTS:
            ts("vector", aF[0:16, 0:128], aF[0:16, 0:128], 64.0, None, ALU.is_ge)
            tt("vector", aF[0:16, 128:256], aF[0:16, 128:256], aF[0:16, 0:128], ALU.subtract)
            for hp in range(NPAIR):
                ts("vector", selp[:, hp * 128:(hp + 1) * 128], aF[0:16, 128:256],
                   float(2 * hp), None, ALU.is_equal)
        if "aexp" not in SKIP_CONSTS:
            a0 = CST["alog"]
            act("scalar", cst[:, a0:a0 + NDC * NST], cst[:, a0:a0 + NDC * NST], AF.Exp)
            P.wait("vector", P.mark("scalar"))
            ts("vector", cst[:, a0:a0 + NDC * NST], cst[:, a0:a0 + NDC * NST], -1.0, None, ALU.mult)
        identf_ap[0] = identf[:, :]
        P.barrier()

        def layer_norm(dst_bf, inner_gb=False):
            arB.push(); arF.push()
            sq = [arB.al(TCH, "ln_sq"), arB.al(TCH, "ln_sq2")]
            xb = [arB.al(TCH, "ln_xb"), arB.al(TCH, "ln_xb2")]
            rb = arF.al(TC, "rstd_b")
            mb = arF.al(TC, "murstd_b")
            tmpc = arF.al(TCH, "ln_tmp")
            ro = arF.al(5 * 64, "ln_rows")
            arF.off -= 5 * 64
            ro = arF.al(5 * TCH, "ln_rows")
            mu = aF[0:1, ro:ro + TCH]
            va = aF[0:1, ro + TCH:ro + 2 * TCH]
            r3 = aF[0:1, ro + 2 * TCH:ro + 3 * TCH]
            mr = aF[0:1, ro + 3 * TCH:ro + 4 * TCH]
            sc1 = aF[0:1, ro + 4 * TCH:ro + 5 * TCH]
            mm_toks = {}
            for tc in range(NT):
                if bank_free[5] is not None:
                    P.wait("tensor", bank_free[5]); bank_free[5] = None
                if bank_free[6] is not None:
                    P.wait("tensor", bank_free[6]); bank_free[6] = None
                for dc in range(NDC):
                    par = dc % 2
                    xs = x_sb[:, dc * TC + tc * TCH: dc * TC + (tc + 1) * TCH]
                    prev = mm_toks.get(par)
                    if prev is not None:
                        P.wait("vector", prev); P.wait("scalar", prev)
                    cp("vector", aB[:, xb[par]:xb[par] + TCH], xs)
                    tv = P.mark("vector")
                    act("scalar", aB[:, sq[par]:sq[par] + TCH], xs, AF.Square)
                    tsc = P.mark("scalar")
                    P.wait("tensor", tv); P.wait("tensor", tsc)
                    mm(pb[5][0:1, 0:TCH], ones_b[:, 0:1], aB[:, xb[par]:xb[par] + TCH], dc == 0, dc == NDC - 1)
                    mm(pb[6][0:1, 0:TCH], ones_b[:, 0:1], aB[:, sq[par]:sq[par] + TCH], dc == 0, dc == NDC - 1)
                    mm_toks[par] = P.mark("tensor")
                tmm = P.tok("tensor")
                P.wait("vector", tmm)
                ts("vector", mu, pb[5][0:1, 0:TCH], 1.0 / D, None, ALU.mult)
                ts("vector", va, pb[6][0:1, 0:TCH], 1.0 / D, None, ALU.mult)
                tfree = P.mark("vector")
                bank_free[5] = tfree; bank_free[6] = tfree
                tt("vector", sc1, mu, mu, ALU.mult)
                tt("vector", va, va, sc1, ALU.subtract)
                tva = P.mark("vector")
                P.wait("scalar", tva)
                if inner_gb:
                    act("scalar", r3, va, AF.Ln, bias=EPS * EPS, scale=1.0 + EPS)
                else:
                    act("scalar", r3, va, AF.Ln, bias=EPS)
                act("scalar", r3, r3, AF.Exp, scale=-0.5)
                trs = P.mark("scalar")
                P.wait("vector", trs)
                tt("vector", mr, mu, r3, ALU.mult)
                tstat = P.mark("vector")
                P.wait("tensor", tstat)
                for (row, dsto) in ((r3, rb), (mr, mb)):
                    mm_bank(4, [(rowones_f[:, :], row, None)])

                    def _e(tc=tc, dsto=dsto):
                        cp("vector", aF[:, dsto + tc * TCH: dsto + (tc + 1) * TCH], pb[4][:, 0:TCH])
                    evac(4, "vector", _e)
            for dc in range(NDC):
                for tc in range(NT):
                    lo = dc * TC + tc * TCH
                    hi = lo + TCH
                    tt("vector", aF[:, tmpc:tmpc + TCH], x_sb[:, lo:hi],
                       aF[:, rb + tc * TCH: rb + (tc + 1) * TCH], ALU.mult)
                    if inner_gb:
                        tt("vector", aF[:, tmpc:tmpc + TCH], aF[:, tmpc:tmpc + TCH],
                           aF[:, mb + tc * TCH: mb + (tc + 1) * TCH], ALU.subtract)
                        ts("vector", dst_bf[:, lo:hi], aF[:, tmpc:tmpc + TCH],
                           cstc("lng", dc), cstc("lnb", dc), ALU.mult, ALU.add)
                    else:
                        tt("vector", dst_bf[:, lo:hi], aF[:, tmpc:tmpc + TCH],
                           aF[:, mb + tc * TCH: mb + (tc + 1) * TCH], ALU.subtract)
            arB.pop(); arF.pop()

        def l0_attn():
            arB.push(); arF.push()
            xn_o = arB.al(NDC * TC, "xn1")
            xn = ColView(aB, xn_o)
            layer_norm(xn)
            P.wait("tensor", P.tok("vector"))
            phi_q = arB.al(NPAIR * TC, "phi_q")
            phi_kg = arB.al(4 * TC, "phi_kg")
            rfb = arB.al(NPAIR * 128, "rfblk")
            wst = [arB.al(2 * NDC * 128, "wqk0"), arB.al(2 * NDC * 128, "wqk1")]
            qk = arB.al(TC, "qk")
            vv = arB.al(512, "vT")
            pkT = arB.al(512, "phikT")
            wvm = arB.al(NDC * 512, "wv_moving")
            pairkv = qk
            kvcols = arF.al(1040, "kvcols")
            nrmtm_o = arF.al(NSUB * 16, "nrmtm")
            nrmtm = aF[:, nrmtm_o:nrmtm_o + NSUB * 16]
            nrow_o = arB.al(TC, "nrow")
            nrow = aB[0:16, nrow_o:nrow_o + TC]
            dma("gpsimd", aB[:, rfb:rfb + NPAIR * 128].rearrange("p (h c) -> p h c", h=NPAIR),
                io["rfblk"].rearrange("h p c -> p h c"), st="lrf")
            trf = P.tok("lrf")

            wst_guard = [None, None]
            qkb = [qk, nrow_o]
            qk_guard = [None, None]
            wv_guard = None
            for grp in range(2):
                prev = None
                for lp in range(4):
                    hp = grp * 4 + lp
                    slot = lp % 2
                    ws = wst[slot]
                    if wst_guard[slot] is not None:
                        P.wait("gpsimd", wst_guard[slot])
                    for part in range(2):
                        col0 = part * D + hp * 128
                        dma("gpsimd",
                            aB[:, ws + part * NDC * 128: ws + (part + 1) * NDC * 128]
                            .rearrange("p (dc c) -> p dc c", dc=NDC),
                            io["wqkv"].rearrange("(dc p) c -> p dc c", p=128)[:, :, col0:col0 + 128],
                            st=f"lw{slot}")
                    tw = P.tok(f"lw{slot}")
                    P.wait("tensor", tw)
                    if hp == 0:
                        P.wait("tensor", trf)

                    def emit_phi(hp_, pdst, po, rpar, tqk):
                        P.wait("tensor", tqk)
                        for tc in range(NT):
                            bank = 4 + tc % 2
                            mm_bank(bank, [(aB[:, rfb + hp_ * 128: rfb + (hp_ + 1) * 128],
                                            aB[:, qkb[rpar] + tc * TCH: qkb[rpar] + (tc + 1) * TCH],
                                            None)])

                            def _ep(bank=bank, pdst=pdst, po=po, tc=tc):
                                act("scalar", aB[:, pdst + po + tc * TCH: pdst + po + (tc + 1) * TCH],
                                    pb[bank][:, 0:TCH], AF.Relu)
                            evac(bank, "scalar", _ep)
                        qk_guard[rpar] = P.tok("tensor")

                    for part, pdst, po in ((0, phi_q, hp * TC), (1, phi_kg, lp * TC)):
                        ridx = 2 * lp + part
                        rpar = ridx % 2
                        for tc in range(NT):
                            bank = tc % 4
                            mm_bank(bank, [(aB[:, ws + (part * NDC + dc) * 128: ws + (part * NDC + dc) * 128 + 128],
                                            xn[:, dc * TC + tc * TCH: dc * TC + (tc + 1) * TCH], None)
                                           for dc in range(NDC)])

                            def _e(bank=bank, tc=tc, rpar=rpar):
                                if qk_guard[rpar] is not None:
                                    P.wait("vector", qk_guard[rpar])
                                    qk_guard[rpar] = None
                                cp("vector", aB[:, qkb[rpar] + tc * TCH: qkb[rpar] + (tc + 1) * TCH],
                                   pb[bank][:, 0:TCH])
                            evac(bank, "vector", _e)
                        tqk_r = P.tok("vector")
                        if prev is not None:
                            emit_phi(*prev)
                        prev = (hp, pdst, po, rpar, tqk_r)
                    wst_guard[slot] = P.tok("tensor")
                emit_phi(*prev)

                if wv_guard is not None:
                    P.wait("gpsimd", wv_guard)
                dma("gpsimd", aB[:, wvm:wvm + NDC * 512].rearrange("p (dc c) -> p dc c", dc=NDC),
                    io["wqkv"].rearrange("(dc p) c -> p dc c", p=128)
                    [:, :, 2 * D + grp * 512:2 * D + (grp + 1) * 512], st="lwv")
                twv = P.tok("lwv")
                P.wait("tensor", twv)
                tphi = P.tok("scalar")
                P.wait("tensor", tphi)
                for b_ in range(2):
                    if bank_free[b_] is not None:
                        P.wait("tensor", bank_free[b_]); bank_free[b_] = None
                vvb = [vv, wst[1]]
                pkb = [pkT, wst[1] + 512]
                vk_guard = [None, None]
                prevB = None
                tkv = None

                def emit_kv(sub, par, tvv):
                    nonlocal tkv
                    P.wait("tensor", tvv)
                    for lh in range(8):
                        bank = lh // 6
                        c0 = (lh % 6) * 65
                        lpp, ab = lh // 2, lh % 2
                        first = (sub == 0 and lh % 6 == 0)
                        statap = aB[:, pkb[par] + lpp * 128 + ab * 64: pkb[par] + lpp * 128 + ab * 64 + 64]
                        mm(pb[bank][0:64, c0:c0 + 64], statap,
                           aB[:, vvb[par] + lh * 64: vvb[par] + lh * 64 + 64], first, False)
                        mm(pb[bank][0:64, c0 + 64:c0 + 65], statap, ones_b[:, 0:1], False, False)
                    tkv = P.mark("tensor")
                    vk_guard[par] = tkv
                    bank_ready[0] = tkv
                    bank_ready[1] = tkv

                for sub in range(NSUB):
                    t0 = sub * 128
                    par = sub % 2
                    mm_bank(4 + par, [(xn[:, dc * TC + t0: dc * TC + t0 + 128],
                                       aB[:, wvm + dc * 512: wvm + (dc + 1) * 512],
                                       pb[4 + par][:, 0:512]) for dc in range(NDC)])

                    def _ev(par=par):
                        if vk_guard[par] is not None:
                            P.wait("vector", vk_guard[par])
                            vk_guard[par] = None
                        cp("vector", aB[:, vvb[par]:vvb[par] + 512], pb[4 + par][:, 0:512])
                    evac(4 + par, "vector", _ev)
                    if bank_free[7] is not None:
                        P.wait("tensor", bank_free[7]); bank_free[7] = None
                    for lp in range(4):
                        transpose(pT[:, lp * 128:(lp + 1) * 128],
                                  aB[:, phi_kg + lp * TC + t0: phi_kg + lp * TC + t0 + 128],
                                  identb[:, :])
                    bank_ready[7] = P.mark("tensor")

                    def _ek(par=par):
                        cp("vector", aB[:, pkb[par]:pkb[par] + 512], pT[:, 0:512])
                    evac(7, "vector", _ek)
                    tvv = P.tok("vector")
                    if prevB is not None:
                        emit_kv(*prevB)
                    prevB = (sub, par, tvv)
                emit_kv(*prevB)
                wst_guard[1] = tkv
                P.wait("vector", tkv)
                for bank in range(2):
                    nh = 6 if bank == 0 else 2
                    base_h = grp * 8 + bank * 6
                    cp("vector", aF[0:64, kvcols + base_h * 65: kvcols + (base_h + nh) * 65],
                       pb[bank][0:64, 0:nh * 65])
                tkvev = P.mark("vector")
                for bank in range(2):
                    bank_free[bank] = tkvev
                    bank_ready[bank] = None
                wv_guard = P.tok("tensor")

            tkvev = P.tok("vector")
            P.wait("sync", tkvev)
            dma("sync", cc_kv_in[:, :], aF[0:64, kvcols:kvcols + 1040], st="dcc")
            P.wait("gpsimd", P.tok("dcc"))
            P.do("gpsimd", lambda e: e.collective_compute(
                "AllReduce", ALU.add, replica_groups=RGROUPS,
                ins=[cc_kv_in.ap()], outs=[cc_kv_out.ap()]), cc="cck")
            P.wait("gpsimd", P.tok("cck"))
            dma("gpsimd", aF[0:64, kvcols:kvcols + 1040], cc_kv_out[:, :], st="dcc2")
            tkvr = P.tok("dcc2")

            P.wait("vector", tkvr)
            P.do("vector", lambda e: e.memset(aB[:, pairkv:pairkv + NPAIR * 130], 0.0))
            for hp in range(NPAIR):
                o = pairkv + hp * 130
                ha, hb = 2 * hp, 2 * hp + 1
                cp("vector", aB[0:64, o:o + 64], aF[0:64, kvcols + ha * 65: kvcols + ha * 65 + 64])
                cp("vector", aB[64:128, o + 64:o + 128], aF[0:64, kvcols + hb * 65: kvcols + hb * 65 + 64])
                cp("vector", aB[0:64, o + 128:o + 129], aF[0:64, kvcols + ha * 65 + 64: kvcols + ha * 65 + 65])
                cp("vector", aB[64:128, o + 129:o + 130], aF[0:64, kvcols + hb * 65 + 64: kvcols + hb * 65 + 65])
            tpair = P.mark("vector")
            P.wait("tensor", tpair)

            for g0 in range(0, NSUB, 8):
                if bank_free[6] is not None:
                    P.wait("tensor", bank_free[6]); bank_free[6] = None
                gn = min(8, NSUB - g0)
                for si in range(gn):
                    sub = g0 + si
                    for hp in range(NPAIR):
                        mm(pb[6][0:128, si * 16 + hp * 2: si * 16 + hp * 2 + 2],
                           aB[:, phi_q + hp * TC + sub * 128: phi_q + hp * TC + sub * 128 + 128],
                           aB[:, pairkv + hp * 130 + 128: pairkv + hp * 130 + 130],
                           si == 0 and hp == 0, False)
                bank_ready[6] = P.mark("tensor")

                def _e(g0=g0, gn=gn):
                    cp("vector", aF[:, nrmtm_o + g0 * 16:nrmtm_o + (g0 + gn) * 16], pb[6][:, 0:gn * 16])
                evac(6, "vector", _e)
            ts("vector", aF[:, nrmtm_o:nrmtm_o + NSUB * 16], aF[:, nrmtm_o:nrmtm_o + NSUB * 16], 1e-6, None, ALU.add)
            recip("vector", aF[:, nrmtm_o:nrmtm_o + NSUB * 16], aF[:, nrmtm_o:nrmtm_o + NSUB * 16])
            tnr = P.mark("vector")
            P.wait("tensor", tnr)
            for sub in range(NSUB):
                bank = sub % 2
                if bank_free[bank] is not None:
                    P.wait("tensor", bank_free[bank]); bank_free[bank] = None
                transpose(pb[bank][0:16, 0:128], aF[:, nrmtm_o + sub * 16:nrmtm_o + (sub + 1) * 16],
                          identf[:, :])
                bank_ready[bank] = P.mark("tensor")

                def _e(bank=bank, sub=sub):
                    cp("vector", aB[0:16, nrow_o + sub * 128:nrow_o + (sub + 1) * 128],
                       pb[bank][0:16, 0:128])
                evac(bank, "vector", _e)

            atn = xn_o
            nb = arF.al(TCH, "nrm_b")
            tnrow = P.tok("vector")
            P.wait("tensor", tnrow)
            for hp in range(NPAIR):
                for tc in range(NT):
                    bank = (hp * NT + tc) % 2
                    mm_bank(bank, [(aB[:, pairkv + hp * 130: pairkv + hp * 130 + 128],
                                    aB[:, phi_q + hp * TC + tc * TCH: phi_q + hp * TC + (tc + 1) * TCH],
                                    pb[bank][:, 0:TCH])])
                    bank2 = 2 + (hp * NT + tc) % 2
                    mm_bank(bank2, [(selp[:, hp * 128:(hp + 1) * 128],
                                     nrow[:, tc * TCH:(tc + 1) * TCH], pb[bank2][:, 0:TCH])])

                    def _e2(bank2=bank2):
                        cp("vector", aF[:, nb:nb + TCH], pb[bank2][:, 0:TCH])
                    evac(bank2, "vector", _e2)

                    def _e(bank=bank, hp=hp, tc=tc):
                        tt("vector", aB[:, atn + hp * TC + tc * TCH: atn + hp * TC + (tc + 1) * TCH],
                           pb[bank][:, 0:TCH], aF[:, nb:nb + TCH], ALU.mult)
                    evac(bank, "vector", _e)
            tatn = P.tok("vector")
            P.wait("tensor", tatn)
            woguard = [None, None]
            for oc in range(NDC):
                slot = oc % 2
                ws = wst[slot]
                if woguard[slot] is not None:
                    P.wait("gpsimd", woguard[slot])
                P.wait("gpsimd", wst_guard[slot])
                wst_guard[slot] = None
                for dc in range(NDC):
                    dma("gpsimd", aB[:, ws + dc * 128: ws + dc * 128 + 128],
                        io["wout"][dc * 128:(dc + 1) * 128, oc * 128:(oc + 1) * 128], st=f"lw{slot}")
                two = P.tok(f"lw{slot}")
                P.wait("tensor", two)
                for tc in range(NT):
                    bank = 4 + (oc * NT + tc) % 3
                    mm_bank(bank, [(aB[:, ws + dc * 128: ws + dc * 128 + 128],
                                    aB[:, atn + dc * TC + tc * TCH: atn + dc * TC + (tc + 1) * TCH],
                                    pb[bank][:, 0:TCH])
                                   for dc in range(NDC)])

                    def _e(bank=bank, oc=oc, tc=tc):
                        stt("vector", x_sb[:, oc * TC + tc * TCH: oc * TC + (tc + 1) * TCH],
                            pb[bank][:, 0:TCH], cstc("bout", oc),
                            x_sb[:, oc * TC + tc * TCH: oc * TC + (tc + 1) * TCH],
                            ALU.add, ALU.add)
                    evac(bank, "vector", _e)
                woguard[slot] = P.tok("tensor")
            DBG.update(atn=atn, phi_q=phi_q, pairkv=pairkv, kvcols=kvcols)
            arB.pop(); arF.pop()

        def rowmax_bcast(am_off, ncols, sb_off, usb_off, smax, usc):
            nsub = ncols // 128
            arB.push(); arF.push()
            rtm = arB.al(nsub, "rowmax_tm")
            P.wait("tensor", P.tok("vector"))
            for sb_i in range(nsub):
                if bank_free[7] is not None:
                    P.wait("tensor", bank_free[7]); bank_free[7] = None
                transpose(pT[:, 0:128], aB[:, am_off + sb_i * 128: am_off + (sb_i + 1) * 128],
                          identb[:, :])
                bank_ready[7] = P.mark("tensor")

                def _e(sb_i=sb_i):
                    P.do("vector", lambda e, sb_i=sb_i: e.reduce_max(
                        aB[:, rtm + sb_i: rtm + sb_i + 1], pT[:, 0:128], AXX))
                evac(7, "vector", _e)
            srow = arF.al(2 * nsub, "srow_tm")
            ts("vector", aF[:, srow:srow + nsub], aB[:, rtm:rtm + nsub], 1e-5, None, ALU.max)
            recip("vector", aF[:, srow:srow + nsub], aF[:, srow:srow + nsub])
            ts("vector", aF[:, srow:srow + nsub], aF[:, srow:srow + nsub], smax, None, ALU.mult)
            ts("vector", aF[:, srow + nsub:srow + 2 * nsub], aB[:, rtm:rtm + nsub],
               usc, None, ALU.mult)
            tsc_ = P.mark("vector")
            P.wait("tensor", tsc_)
            bank = 6
            if bank_free[bank] is not None:
                P.wait("tensor", bank_free[bank]); bank_free[bank] = None
            transpose(pb[bank][0:2 * nsub, 0:128], aF[:, srow:srow + 2 * nsub], identf_ap[0])
            bank_ready[bank] = P.mark("tensor")
            rows = arF.al(128, "rm_rows")

            def _e3():
                cp("vector", aF[0:2 * nsub, rows:rows + 128], pb[6][0:2 * nsub, 0:128])
            evac(bank, "vector", _e3)
            trow = P.mark("vector")
            P.wait("sync", trow)
            P.wait("sync", P.tok("rbs"))
            dma("sync", rb_scr[0:2 * nsub, :], aF[0:2 * nsub, rows:rows + 128], st="rbs")
            P.wait("sync", P.tok("rbs"))
            src_s = rb_scr[0:nsub, :].unsqueeze(0).broadcast_to((128, nsub, 128))
            src_u = rb_scr[nsub:2 * nsub, :].unsqueeze(0).broadcast_to((128, nsub, 128))
            dma("sync", aF[:, sb_off:sb_off + ncols].rearrange("p (a b) -> p a b", a=nsub),
                src_s, st="rbs")
            dma("sync", aF[:, usb_off:usb_off + ncols].rearrange("p (a b) -> p a b", a=nsub),
                src_u, st="rbs")
            arB.pop(); arF.pop()
            return P.tok("rbs")

        def weight_quant():
            arF.push(); arB.push()
            wbuf = arF.al(2 * 2048, "wq_buf")
            wbB = arB.al(4 * 4096, "wq_bufB")
            accc = arF.al(12, "wq_acc")
            cells = arF.al(8, "wq_cells")
            stg = [arB.al(2048, f"wq_stg{i}") for i in range(6)]
            onef = cst[:, FCBASE + 3:FCBASE + 4]

            def wslot(par):
                if par < 2:
                    return aF[:, wbuf + par * 2048: wbuf + (par + 1) * 2048]
                return aB[:, wbB + (par - 2) * 4096: wbB + (par - 1) * 4096].bitcast(F32)

            NB = 6
            guards = [None] * NB
            sguards = [None] * NB
            mats = [("wg_t", wq_g, NDC, INNER, "swg"), ("wu_t", wq_u, NDC, INNER, "swu"),
                    ("wd_t", wq_d, INNER // 128, D, "swd")]
            for mi, (src, dst, nrow_c, w, sname) in enumerate(mats):
                ncolh = w // 2048 if w > 2048 else 1
                colw = min(w, 2048)
                P.do("vector", lambda e: e.memset(aF[:, accc:accc + 1], 0.0))
                tidx = 0
                for rc in range(nrow_c):
                    for ch in range(ncolh):
                        par = tidx % NB
                        if guards[par] is not None:
                            P.wait("gpsimd", guards[par])
                        if sguards[par % 6] is not None:
                            P.wait("gpsimd", sguards[par % 6])
                        dma("gpsimd", aB[:, stg[par % 6]:stg[par % 6] + colw],
                            io[src][rc * 128:(rc + 1) * 128, ch * 2048: ch * 2048 + colw],
                            st=f"lp{par}")
                        P.wait("vector", P.tok(f"lp{par}"))
                        P.do("vector", lambda e, par=par, colw=colw: e.reduce_sum(
                            aF[:, accc + 1 + par: accc + 2 + par],
                            aB[:, stg[par % 6]:stg[par % 6] + colw], AXX,
                            apply_absolute_value=True))
                        tt("vector", aF[:, accc:accc + 1], aF[:, accc:accc + 1],
                           aF[:, accc + 1 + par: accc + 2 + par], ALU.add)
                        guards[par] = P.mark("vector")
                        tidx += 1
                tacc = P.mark("vector")
                P.wait("tensor", tacc)
                if bank_free[6] is not None:
                    P.wait("tensor", bank_free[6]); bank_free[6] = None
                mm(pb[6][0:1, 0:1], onef, aF[:, accc:accc + 1], True, True)
                bank_ready[6] = P.mark("tensor")

                def _m():
                    cp("vector", dbgc[0:1, 8 + mi:9 + mi], pb[6][0:1, 0:1])
                    ts("vector", aF[0:1, cells:cells + 1], pb[6][0:1, 0:1],
                       1.0 / (nrow_c * 128 * w), None, ALU.mult)
                    ts("vector", aF[0:1, cells:cells + 1], aF[0:1, cells:cells + 1],
                       1e-5, None, ALU.max)
                    recip("vector", aF[0:1, cells + 1:cells + 2], aF[0:1, cells:cells + 1])
                evac(6, "vector", _m)
                tcells = P.mark("vector")
                P.wait("tensor", tcells)
                for (cidx, dst_c) in ((1, 2), (0, 4)):
                    if bank_free[6] is not None:
                        P.wait("tensor", bank_free[6]); bank_free[6] = None
                    mm(pb[6][:, 0:1], rowones_f[:, :],
                       aF[0:1, cells + cidx:cells + cidx + 1], True, True)
                    bank_ready[6] = P.mark("tensor")

                    def _b(dst_c=dst_c):
                        cp("vector", aF[:, cells + dst_c:cells + dst_c + 1], pb[6][:, 0:1])
                    evac(6, "vector", _b)
                CST[sname] = None
                DBG[sname + "_s"] = cells + 2
                DBG[sname + "_m"] = cells + 4
                mw_col = FCBASE - 6 + mi
                cp("vector", cst[:, mw_col:mw_col + 1], aF[:, cells + 4:cells + 5])
                CST["mw" + str(mi)] = mw_col
                tidx = 0
                for rc in range(nrow_c):
                    for ch in range(ncolh):
                        par = tidx % NB
                        if guards[par] is not None:
                            P.wait("sync", guards[par])
                        dma("sync", wslot(par)[:, 0:colw],
                            io[src][rc * 128:(rc + 1) * 128, ch * 2048: ch * 2048 + colw],
                            st=f"lq{par}")
                        P.wait("scalar", P.tok(f"lq{par}"))
                        bo = wslot(par)
                        P.wait("scalar", P.tok("vector"))
                        act("scalar", bo[:, 0:colw], bo[:, 0:colw], AF.Identity,
                            bias=RMAGIC, scale=aF[:, cells + 2:cells + 3])
                        P.wait("vector", P.mark("scalar"))
                        ts("vector", bo[:, 0:colw], bo[:, 0:colw],
                           RMAGIC, 1.0, ALU.subtract, ALU.min)
                        so = stg[par]
                        if sguards[par] is not None:
                            P.wait("vector", sguards[par])
                        P.do("vector", lambda e, bo=bo, colw=colw, so=so: e.tensor_scalar(
                            aB[:, so:so + colw], bo[:, 0:colw],
                            -1.0, None, ALU.max))
                        tq = P.mark("vector")
                        guards[par] = tq
                        P.wait("scalar", tq)
                        dma("scalar", dst[rc * 128:(rc + 1) * 128, ch * 2048: ch * 2048 + colw],
                            aB[:, so:so + colw], st=f"wqs{par}")
                        sguards[par] = P.tok(f"wqs{par}")
                        tidx += 1
            tqs = tuple(P.tok(f"wqs{i}") for i in range(6))
            arF.pop(); arB.pop()
            return tqs

        def l0_ffn(tq_scratch):
            arB.push(); arF.push()
            xn_o = arB.al(NDC * TC, "xn2")
            xn = ColView(aB, xn_o)
            layer_norm(xn)
            P.wait("tensor", P.tok("vector"))
            NPASS = 4 if TC >= 2048 else 2
            TP = TC // NPASS
            NTP = max(1, TP // 512)
            TPH = TP // NTP
            sxb = arF.al(TC, "sx_b")
            usxb = arF.al(TC, "usx_b")
            tmpf = arF.al(TC, "ffn_tmp")
            arB.push()
            am = arB.al(TC, "am")
            P.wait("scalar", P.tok("vector"))
            abst = arB.al(TC, "abs_tmp")
            for dc in range(NDC):
                act("scalar", aB[:, abst:abst + TC], xn[:, dc * TC:(dc + 1) * TC], AF.Abs)
                tabs = P.mark("scalar")
                P.wait("vector", tabs)
                if dc == 0:
                    cp("vector", aB[:, am:am + TC], aB[:, abst:abst + TC])
                else:
                    tt("vector", aB[:, am:am + TC], aB[:, am:am + TC],
                       aB[:, abst:abst + TC], ALU.max)
                tmx = P.mark("vector")
                P.wait("scalar", tmx)
            trb = rowmax_bcast(am, TC, sxb, usxb, 127.0, 1.0 / 127.0)
            P.wait("vector", trb)
            for dc in range(NDC):
                sl = slice(dc * TC, (dc + 1) * TC)
                tt("vector", aF[:, tmpf:tmpf + TC], xn[:, sl], aF[:, sxb:sxb + TC], ALU.mult)
                ts("vector", aF[:, tmpf:tmpf + TC], aF[:, tmpf:tmpf + TC],
                   RMAGIC, RMAGIC, ALU.add, ALU.subtract)
                tt("vector", xn[:, sl], aF[:, tmpf:tmpf + TC], aF[:, usxb:usxb + TC], ALU.mult)
            txdq = P.mark("vector")
            P.wait("tensor", txdq)
            for t_ in tq_scratch:
                P.wait("gpsimd", t_)
            arB.pop()
            if os.environ.get("KFFN", "") == "1":
                arB.pop(); arF.pop(); return

            mid = arB.al(NIC * TP, "mid")
            wsl_g = [arB.al(4096, "wqg0"), arB.al(4096, "wqg1")]
            wsl_u = [arB.al(4096, "wqu0"), arB.al(4096, "wqu1")]
            wsl_d = [arB.al(2048, "wqd0"), arB.al(2048, "wqd1")]
            amM = arB.al(TP, "amM")
            absM = arB.al(TP, "absM")
            smb = arF.al(TP, "sm_b")
            usmb = arF.al(TP, "usm_b")
            gu_guard = [None, None]

            def gu_load(grp, par):
                if gu_guard[par] is not None:
                    P.wait("gpsimd", gu_guard[par])
                    gu_guard[par] = None
                dma("gpsimd", aB[:, wsl_g[par]:wsl_g[par] + 4096]
                    .rearrange("p (dc c) -> p dc c", dc=NDC),
                    wq_g.rearrange("(dc p) c -> p dc c", p=128)[:, :, grp * 512:(grp + 1) * 512],
                    st=f"lg{par}")
                dma("gpsimd", aB[:, wsl_u[par]:wsl_u[par] + 4096]
                    .rearrange("p (dc c) -> p dc c", dc=NDC),
                    wq_u.rearrange("(dc p) c -> p dc c", p=128)[:, :, grp * 512:(grp + 1) * 512],
                    st=f"lu{par}")
                return (P.tok(f"lg{par}"), P.tok(f"lu{par}"))

            d_guard = [None, None]

            def d_load(och, grp, par):
                if d_guard[par] is not None:
                    P.wait("gpsimd", d_guard[par])
                    d_guard[par] = None
                dma("gpsimd", aB[:, wsl_d[par]:wsl_d[par] + 2048]
                    .rearrange("p (ic c) -> p ic c", ic=4),
                    wq_d.rearrange("(ic p) c -> p ic c", p=128)
                    [:, grp * 4:(grp + 1) * 4, och * 512:(och + 1) * 512], st=f"ldd{par}")
                return P.tok(f"ldd{par}")

            tdown_prev = None
            gu_t = {(0, 0): gu_load(0, 0), (0, 1): gu_load(1, 1)}
            for p in range(NPASS):
                t0 = p * TP
                if tdown_prev is not None:
                    P.wait("scalar", tdown_prev)
                    P.wait("vector", tdown_prev)
                for grp in range(8):
                    par = grp % 2
                    tg_, tu_ = gu_t[(p, grp)]
                    P.wait("tensor", tg_); P.wait("tensor", tu_)
                    for ici in range(4):
                        ic = grp * 4 + ici
                        for tch in range(NTP):
                            bg = (2 * (ici * NTP + tch)) % 4
                            bu = bg + 1
                            mm_bank(bg, [(aB[:, wsl_g[par] + dc * 512 + ici * 128:
                                                wsl_g[par] + dc * 512 + ici * 128 + 128],
                                          xn[:, dc * TC + t0 + tch * TPH: dc * TC + t0 + (tch + 1) * TPH],
                                          pb[bg][:, 0:TPH]) for dc in range(NDC)])

                            def _eg(bg=bg, ic=ic, tch=tch):
                                act("scalar", aB[:, mid + ic * TP + tch * TPH: mid + ic * TP + (tch + 1) * TPH],
                                    pb[bg][:, 0:TPH], AF.Silu, scale=cst[:, CST["mw0"]:CST["mw0"] + 1])
                            evac(bg, "scalar", _eg)
                            mm_bank(bu, [(aB[:, wsl_u[par] + dc * 512 + ici * 128:
                                                wsl_u[par] + dc * 512 + ici * 128 + 128],
                                          xn[:, dc * TC + t0 + tch * TPH: dc * TC + t0 + (tch + 1) * TPH],
                                          pb[bu][:, 0:TPH]) for dc in range(NDC)])

                            def _eu(bu=bu, ic=ic, tch=tch, p=p):
                                P.wait("vector", P.tok("scalar"))
                                o = mid + ic * TP + tch * TPH
                                P.do("vector", lambda e, bu=bu, o=o: e.scalar_tensor_tensor(
                                    aB[:, o:o + TPH], pb[bu][:, 0:TPH],
                                    cst[:, CST["mw1"]:CST["mw1"] + 1],
                                    aB[:, o:o + TPH], ALU.mult, ALU.mult))
                                tmid = P.mark("vector")
                                P.wait("scalar", tmid)
                                act("scalar", aB[:, absM + tch * TPH: absM + (tch + 1) * TPH],
                                    aB[:, o:o + TPH], AF.Abs)
                                tam = P.mark("scalar")
                                P.wait("vector", tam)
                                if ic == 0:
                                    cp("vector", aB[:, amM + tch * TPH: amM + (tch + 1) * TPH],
                                       aB[:, absM + tch * TPH: absM + (tch + 1) * TPH])
                                else:
                                    P.do("vector", lambda e, tch=tch: e.tensor_tensor(
                                        aB[:, amM + tch * TPH: amM + (tch + 1) * TPH],
                                        aB[:, amM + tch * TPH: amM + (tch + 1) * TPH],
                                        aB[:, absM + tch * TPH: absM + (tch + 1) * TPH], ALU.max))
                            evac(bu, "vector", _eu)
                    gu_guard[par] = P.tok("tensor")
                    if grp + 2 < 8:
                        gu_t[(p, grp + 2)] = gu_load(grp + 2, par)
                    elif p + 1 < NPASS:
                        gu_t[(p + 1, grp + 2 - 8)] = gu_load(grp + 2 - 8, par)
                if os.environ.get("KFFN", "") == "2":
                    continue
                trb2 = rowmax_bcast(amM, TP, smb, usmb, 127.0, 1.0 / 127.0)
                P.wait("vector", trb2)
                for ic in range(NIC):
                    o = mid + ic * TP
                    tt("vector", aF[:, tmpf:tmpf + TP], aB[:, o:o + TP],
                       aF[:, smb:smb + TP], ALU.mult)
                    P.do("vector", lambda e, o=o: e.tensor_scalar(
                        aB[:, o:o + TP], aF[:, tmpf:tmpf + TP], RMAGIC, RMAGIC,
                        ALU.add, ALU.subtract))
                tmq = P.mark("vector")
                P.wait("tensor", tmq)
                if os.environ.get("KFFN", "") == "3":
                    continue
                d_t = {(0, 0): d_load(0, 0, 0), (0, 1): d_load(0, 1, 1)}
                for och in range(2):
                    for b_ in range(4, 8):
                        if bank_free[b_] is not None:
                            P.wait("tensor", bank_free[b_]); bank_free[b_] = None
                    for grp in range(8):
                        par = grp % 2
                        P.wait("tensor", d_t[(och, grp)])
                        for ici in range(4):
                            ic = grp * 4 + ici
                            for tch in range(NTP):
                                for j in range(4):
                                    bank = 4 + j
                                    mm(pb[bank][:, 0:TPH],
                                       aB[:, wsl_d[par] + ici * 512 + j * 128: wsl_d[par] + ici * 512 + j * 128 + 128],
                                       aB[:, mid + ic * TP + tch * TPH: mid + ic * TP + (tch + 1) * TPH],
                                       ic == 0, ic == NIC - 1)
                        d_guard[par] = P.tok("tensor")
                        if grp + 2 < 8:
                            d_t[(och, grp + 2)] = d_load(och, grp + 2, par)
                        elif och == 0:
                            d_t[(1, grp + 2 - 8)] = d_load(1, grp + 2 - 8, par)
                    tdown = P.mark("tensor")
                    for tch in range(NTP):
                        for j in range(4):
                            oc = och * 4 + j
                            bank = 4 + j
                            bank_ready[bank] = tdown
                            xsl = slice(oc * TC + t0 + tch * TPH, oc * TC + t0 + (tch + 1) * TPH)

                            def _ed(bank=bank, xsl=xsl, tch=tch):
                                P.do("vector", lambda e, bank=bank, tch=tch: e.tensor_tensor(
                                    aF[:, tmpf:tmpf + TPH], pb[bank][:, 0:TPH],
                                    aF[:, usmb + tch * TPH: usmb + (tch + 1) * TPH], ALU.mult))
                                P.do("vector", lambda e, xsl=xsl: e.scalar_tensor_tensor(
                                    x_sb[:, xsl], aF[:, tmpf:tmpf + TPH],
                                    cst[:, CST["mw2"]:CST["mw2"] + 1],
                                    x_sb[:, xsl], ALU.mult, ALU.add))
                            evac(bank, "vector", _ed)
                tdown_prev = P.tok("tensor")
            DBG.update(ffn_mid=mid, ffn_amM=amM, ffn_smb=smb, ffn_usmb=usmb, ffn_sxb=sxb, ffn_usxb=usxb)
            arB.pop(); arF.pop()

        def l1_attn():
            arB.push(); arF.push()
            QT = TC // 2
            xn_o = arB.al(NDC * TC, "xn3")
            xn = ColView(aB, xn_o)
            layer_norm(xn, inner_gb=True)
            P.wait("tensor", P.tok("vector"))
            u_o = arB.al(NDC * TC, "u")
            v_o = arB.al(NDC * TC, "v")
            arB.push()
            wstL = [arB.al(NDC * 128, "wL0"), arB.al(NDC * 128, "wL1")]
            raw_o = wstL[0]
            stg_o = arB.al(2 * TCH, "l1stage")
            bstg_o = arB.al(TCH, "bcstage")
            edgb = arB.al(32, "edges_b")
            est = arF.al(16, "edge_stage")
            edges = arF.al(32, "edges")
            x1row = arF.al(TC + 2, "x1row")
            dtc = arF.al(TCH, "dtc")
            halo = arF.al(4, "halo")

            for dc in range(NDC):
                cp("vector", aF[:, est + dc * 2: est + dc * 2 + 1], xn[:, dc * TC: dc * TC + 1])
                cp("vector", aF[:, est + dc * 2 + 1: est + dc * 2 + 2],
                   xn[:, dc * TC + TC - 1: dc * TC + TC])
            P.wait("sync", P.mark("vector"))
            dma("sync", cc_edge_in[:, :], aF[:, est:est + 16], st="deg")
            P.wait("gpsimd", P.tok("deg"))
            P.do("gpsimd", lambda e: e.collective_compute(
                "AllGather", ALU.bypass, replica_groups=RGROUPS,
                ins=[cc_edge_in.ap()], outs=[cc_edge_out.ap()]), cc="cce")
            P.wait("gpsimd", P.tok("cce"))
            dma("gpsimd", aF[:, edges:edges + 16], cc_edge_out[0:128, :], st="deg2")
            dma("gpsimd", aF[:, edges + 16:edges + 32], cc_edge_out[128:256, :], st="deg2")
            P.wait("vector", P.tok("deg2"))
            cp("vector", aB[:, edgb:edgb + 32], aF[:, edges:edges + 32])
            tedgb = P.mark("vector")

            wguard = [None, None]

            def wload(name, oc):
                slot = oc % 2
                ws = wstL[slot]
                if wguard[slot] is not None:
                    P.wait("gpsimd", wguard[slot])
                dma("gpsimd", aB[:, ws:ws + NDC * 128].rearrange("p (dc c) -> p dc c", dc=NDC),
                    io[name].rearrange("(dc p) c -> p dc c", p=128)[:, :, oc * 128:(oc + 1) * 128],
                    st=f"lwa{slot}")
                P.wait("tensor", P.tok(f"lwa{slot}"))
                return slot, ws

            for oc in range(NDC):
                slot, ws = wload("w1", oc)
                if oc == 0:
                    P.wait("tensor", tedgb)
                mm_bank(5, [(aB[:, ws + dc * 128: ws + dc * 128 + 128],
                             aB[:, edgb + 2 * dc + 1: edgb + 2 * dc + 2],
                             pb[5][:, 0:1]) for dc in range(NDC)] +
                           [(aB[:, ws + dc * 128: ws + dc * 128 + 128],
                             aB[:, edgb + 16 + 2 * dc: edgb + 16 + 2 * dc + 1],
                             pb[5][:, 1:2]) for dc in range(NDC)])

                def _eh(oc=oc):
                    ts("vector", aF[:, halo:halo + 2], pb[5][:, 0:2], cstc("b1", oc),
                       None, ALU.add)
                    tt("vector", aF[:, halo:halo + 2], aF[:, halo:halo + 2],
                       cst[:, CST["meta"]:CST["meta"] + 2], ALU.mult)
                    cp("vector", aF[:, x1row:x1row + 1], aF[:, halo:halo + 1])
                    cp("vector", aF[:, x1row + TC + 1:x1row + TC + 2],
                       aF[:, halo + 1:halo + 2])
                evac(5, "vector", _eh)
                for tc in range(NT):
                    bank = tc % 4
                    mm_bank(bank, [(aB[:, ws + dc * 128: ws + dc * 128 + 128],
                                    xn[:, dc * TC + tc * TCH: dc * TC + (tc + 1) * TCH],
                                    pb[bank][:, 0:TCH]) for dc in range(NDC)])

                    def _ex(bank=bank, oc=oc, tc=tc):
                        act("scalar", aF[:, x1row + 1 + tc * TCH: x1row + 1 + (tc + 1) * TCH],
                            pb[bank][:, 0:TCH], AF.Identity, bias=cstc("b1", oc))
                    evac(bank, "scalar", _ex)
                P.wait("vector", P.tok("scalar"))
                thalo = P.tok("vector")
                for tc in range(NT):
                    lo = x1row + tc * TCH
                    P.wait("vector", P.tok("scalar"))
                    ts("vector", aF[:, dtc:dtc + TCH], aF[:, lo:lo + TCH],
                       cstc("convw", oc * 3 + 0), None, ALU.mult)
                    stt("vector", aF[:, dtc:dtc + TCH], aF[:, lo + 1:lo + 1 + TCH],
                        cstc("convw", oc * 3 + 1), aF[:, dtc:dtc + TCH], ALU.mult, ALU.add)
                    stt("vector", aF[:, dtc:dtc + TCH], aF[:, lo + 2:lo + 2 + TCH],
                        cstc("convw", oc * 3 + 2), aF[:, dtc:dtc + TCH], ALU.mult, ALU.add)
                    P.wait("scalar", P.mark("vector"))
                    act("scalar", aB[:, u_o + oc * TC + tc * TCH: u_o + oc * TC + (tc + 1) * TCH],
                        aF[:, dtc:dtc + TCH], AF.Silu)
                P.wait("scalar", P.tok("vector"))
                wguard[slot] = P.tok("tensor")

            P.wait("tensor", P.tok("scalar"))
            for oc in range(NDC):
                slot, ws = wload("wdt", oc)
                for tc in range(NT):
                    bank = tc % 4
                    mm_bank(bank, [(aB[:, ws + dc * 128: ws + dc * 128 + 128],
                                    aB[:, u_o + dc * TC + tc * TCH: u_o + dc * TC + (tc + 1) * TCH],
                                    pb[bank][:, 0:TCH]) for dc in range(NDC)])

                    def _ed(bank=bank, oc=oc, tc=tc):
                        P.wait("scalar", P.tok("vector"))
                        act("scalar", aF[:, dtc:dtc + TCH], pb[bank][:, 0:TCH],
                            AF.Exp, bias=cstc("bdt", oc))
                        act("scalar", aF[:, dtc:dtc + TCH], aF[:, dtc:dtc + TCH],
                            AF.Ln, bias=1.0)
                        P.wait("vector", P.mark("scalar"))
                        P.wait("vector", P.tok("sdt0"))
                        P.wait("vector", P.tok("sdt1"))
                        cp("vector", aB[:, stg_o:stg_o + TCH], aF[:, dtc:dtc + TCH])
                        tt("vector", aB[:, stg_o + TCH:stg_o + 2 * TCH], aF[:, dtc:dtc + TCH],
                           aB[:, u_o + oc * TC + tc * TCH: u_o + oc * TC + (tc + 1) * TCH],
                           ALU.mult)
                        P.wait("sync", P.mark("vector"))
                        tmid = TC // 2
                        ro = oc * 128
                        g0, g1 = tc * TCH, (tc + 1) * TCH
                        for (lo_, hi_, cc_h) in ((g0, min(g1, tmid), cc_dtA_in),
                                                 (max(g0, tmid), g1, cc_dtB_in)):
                            if lo_ >= hi_:
                                continue
                            co = lo_ - (0 if cc_h is cc_dtA_in else tmid)
                            so_ = lo_ - g0
                            dma("sync", cc_h[0, ro:ro + 128, co:co + (hi_ - lo_)],
                                aB[:, stg_o + so_:stg_o + so_ + (hi_ - lo_)], st="sdt0")
                            dma("sync", cc_h[1, ro:ro + 128, co:co + (hi_ - lo_)],
                                aB[:, stg_o + TCH + so_:stg_o + TCH + so_ + (hi_ - lo_)],
                                st="sdt1")
                    evac(bank, "scalar", _ed)
                wguard[slot] = P.tok("tensor")

            for bci, wname in ((0, "wb"), (1, "wc")):
                if wguard[0] is not None:
                    P.wait("gpsimd", wguard[0]); wguard[0] = None
                dma("gpsimd", aB[:, wstL[0]:wstL[0] + NDC * 16]
                    .rearrange("p (dc c) -> p dc c", dc=NDC),
                    io[wname].rearrange("(dc p) c -> p dc c", p=128), st="lwa0")
                P.wait("tensor", P.tok("lwa0"))
                for tc in range(NT):
                    bank = 4 + tc % 2
                    mm_bank(bank, [(aB[:, wstL[0] + dc * 16: wstL[0] + dc * 16 + 16],
                                    aB[:, u_o + dc * TC + tc * TCH: u_o + dc * TC + (tc + 1) * TCH],
                                    pb[bank][0:16, 0:TCH]) for dc in range(NDC)])

                    def _eb(bank=bank, bci=bci, tc=tc):
                        P.wait("vector", P.tok("sbc"))
                        cp("vector", aB[0:16, bstg_o:bstg_o + TCH], pb[bank][0:16, 0:TCH])
                        P.wait("sync", P.mark("vector"))
                        dma("sync", cc_bc_in[bci, :, tc * TCH:(tc + 1) * TCH],
                            aB[0:16, bstg_o:bstg_o + TCH], st="sbc")
                    evac(bank, "vector", _eb)
                wguard[0] = P.tok("tensor")

            P.wait("gpsimd", P.tok("sdt0"))
            P.wait("gpsimd", P.tok("sdt1"))
            P.wait("gpsimd", P.tok("sbc"))
            P.do("gpsimd", lambda e: e.collective_compute(
                "AllGather", ALU.bypass, replica_groups=RGROUPS,
                ins=[cc_bc_in.ap()], outs=[cc_bc_out.ap()]), cc="ccg")
            tgathBC = P.tok("ccg")
            P.do("gpsimd", lambda e: e.collective_compute(
                "AllGather", ALU.bypass, replica_groups=RGROUPS,
                ins=[cc_dtA_in.ap()], outs=[cc_dtA_out.ap()]), cc="ccg")
            tgathA = P.tok("ccg")
            P.do("gpsimd", lambda e: e.collective_compute(
                "AllGather", ALU.bypass, replica_groups=RGROUPS,
                ins=[cc_dtB_in.ap()], outs=[cc_dtB_out.ap()]), cc="ccg")
            tgath = P.tok("ccg")

            for oc in range(NDC):
                slot, ws = wload("v1", oc)
                for tc in range(NT):
                    bank = tc % 4
                    mm_bank(bank, [(aB[:, ws + dc * 128: ws + dc * 128 + 128],
                                    xn[:, dc * TC + tc * TCH: dc * TC + (tc + 1) * TCH],
                                    pb[bank][:, 0:TCH]) for dc in range(NDC)])

                    def _ev(bank=bank, oc=oc, tc=tc):
                        act("scalar", aB[:, v_o + oc * TC + tc * TCH: v_o + oc * TC + (tc + 1) * TCH],
                            pb[bank][:, 0:TCH], AF.Silu, bias=cstc("bv", oc))
                    evac(bank, "scalar", _ev)
                wguard[slot] = P.tok("tensor")



            P.wait("gpsimd", tgath)
            dt_o = xn_o
            dtu_o = xn_o + 4 * QT
            y_o = xn_o + 8 * QT
            bcB_o = xn_o + 12 * QT
            bcC_o = xn_o + 13 * QT
            inp_o = xn_o + 14 * QT
            h0_o = xn_o + 15 * QT
            arF.push()
            dAs = [arF.al(QT, "dA0"), arF.al(QT, "dA1")]
            car_o = arF.al(64, "carry")
            bc2f = arF.al(QT, "bc2")
            tm2f = arF.al(QT, "tmp2")
            h2f = arF.al(QT // 2, "h2")
            in2f = arF.al(QT // 2, "inp2")
            aBC2 = aF[:, bc2f:bc2f + QT].bitcast(BF16)
            aTM2 = aF[:, tm2f:tm2f + QT].bitcast(BF16)
            inpS = [aB[:, inp_o:inp_o + QT], aF[:, in2f:in2f + QT // 2].bitcast(BF16)]
            bcB = [aB[:, bcB_o:bcB_o + QT], aBC2[:, 0:QT]]
            bcC = [aB[:, bcC_o:bcC_o + QT], aBC2[:, QT:2 * QT]]
            bcBC = [aB[:, bcB_o:bcB_o + 2 * QT].rearrange("p (b t) -> p b t", b=2),
                    aBC2[:, 0:2 * QT].rearrange("p (b t) -> p b t", b=2)]
            tmp = [aTM2[:, 0:QT], aTM2[:, QT:2 * QT]]
            hb = [aB[:, h0_o:h0_o + QT], aF[:, h2f:h2f + QT // 2].bitcast(BF16)]
            m_lo = cstc("meta", 1)
            m_hi = cstc("meta", 0)
            bc_guard = [None, None]

            def bc_load(q_, n_):
                par = n_ % 2
                s_, hf_ = q_ // 2, q_ % 2
                if bc_guard[par] is not None:
                    P.wait("gpsimd", bc_guard[par])
                    bc_guard[par] = None
                dma("gpsimd", bcBC[par],
                    cc_bc_out[s_, :, n_, hf_ * QT:(hf_ + 1) * QT]
                    .unsqueeze(0).broadcast_to((128, 2, QT)), st=f"lbc{par}")
                return P.tok(f"lbc{par}")

            P.wait("gpsimd", P.tok("scalar"))
            P.wait("gpsimd", tgathBC)
            tbc_next = bc_load(0, 0)
            P.wait("sync", P.tok("scalar"))
            for q in range(4):
                s, hf = q // 2, q % 2
                tsrc = cc_dtA_out if hf == 0 else cc_dtB_out
                tgq = tgathA if hf == 0 else tgath
                P.wait("gpsimd", tgq)
                P.wait("gpsimd", P.tok("scalar"))
                P.wait("gpsimd", P.tok("vector"))
                for mi, dsto in ((0, dt_o), (1, dtu_o)):
                    dma("gpsimd", aB[:, dsto:dsto + 4 * QT]
                        .rearrange("p (dcl t) -> p dcl t", dcl=4),
                        tsrc[s, mi].rearrange("(dcl p) t -> p dcl t", p=128)[:, 0:4, 0:QT],
                        st="ldt")
                t_lo = P.tok("ldt")
                P.wait("sync", tgq)
                P.wait("sync", P.tok("scalar"))
                P.wait("sync", P.tok("vector"))
                blend_t = {}
                idx = 0
                for mi, dsto in ((0, dt_o), (1, dtu_o)):
                    for dcl in range(4):
                        par = idx % 2
                        if idx >= 2:
                            P.wait("sync", blend_t[idx - 2])
                        dma("sync", aB[:, raw_o + par * QT: raw_o + (par + 1) * QT],
                            tsrc[s, mi, 512 + dcl * 128: 512 + (dcl + 1) * 128, 0:QT],
                            st=f"ldh{par}")
                        thi = P.tok(f"ldh{par}")
                        d_ = aB[:, dsto + dcl * QT: dsto + (dcl + 1) * QT]
                        P.wait("vector", t_lo)
                        P.wait("vector", thi)
                        ts("vector", d_, d_, m_lo, None, ALU.mult)
                        stt("vector", d_, aB[:, raw_o + par * QT: raw_o + (par + 1) * QT],
                            m_hi, d_, ALU.mult, ALU.add)
                        blend_t[idx] = P.mark("vector")
                        idx += 1
                P.wait("vector", P.tok("sy"))
                P.wait("vector", P.tok("gpsimd"))
                P.wait("scalar", P.tok("vector"))

                def exp_issue(g):
                    n_, dcl_ = g // 4, g % 4
                    act("scalar", aF[:, dAs[g % 2]:dAs[g % 2] + QT],
                        aB[:, dt_o + dcl_ * QT: dt_o + (dcl_ + 1) * QT],
                        AF.Exp, scale=cstc("alog", dcl_ * NST + n_))
                    return P.mark("scalar")

                texp = exp_issue(0)
                gp_tok = {}
                sc_tok = {}
                for n in range(NST):
                    if n + 1 < NST:
                        tbc_cur, tbc_next = tbc_next, bc_load(q, n + 1)
                    elif q + 1 < 4:
                        tbc_cur, tbc_next = tbc_next, bc_load(q + 1, 0)
                    else:
                        tbc_cur = tbc_next
                    P.wait("vector", tbc_cur)
                    par = n % 2
                    for dcl in range(4):
                        g = n * 4 + dcl
                        texp_cur = texp
                        if g + 1 < 4 * NST:
                            P.wait("scalar", P.tok("vector"))
                            texp = exp_issue(g + 1)
                        hB = hb[g % 2]
                        tmB = tmp[g % 2]
                        P.wait("vector", texp_cur)
                        if g - 2 in gp_tok:
                            P.wait("vector", gp_tok[g - 2])
                        if g - 2 in sc_tok:
                            P.wait("vector", sc_tok[g - 2])
                        tt("vector", aB[:, inp_o:inp_o + QT],
                           aB[:, dtu_o + dcl * QT: dtu_o + (dcl + 1) * QT],
                           bcB[par], ALU.mult)
                        cini = 0.0 if q == 0 else aF[:, car_o + dcl * 16 + n: car_o + dcl * 16 + n + 1]
                        P.do("vector", lambda e, ci=cini, hB=hB, g=g: e.tensor_tensor_scan(
                            hB, aF[:, dAs[g % 2]:dAs[g % 2] + QT],
                            aB[:, inp_o:inp_o + QT], ci, ALU.mult, ALU.add))
                        tv1 = P.mark("vector")
                        P.wait("scalar", tv1)
                        act("scalar", aF[:, car_o + dcl * 16 + n: car_o + dcl * 16 + n + 1],
                            hB[:, QT - 1:QT], AF.Identity)
                        sc_tok[g] = P.mark("scalar")
                        if n == 0:
                            tt("vector", aB[:, y_o + dcl * QT: y_o + (dcl + 1) * QT],
                               hB, bcC[par], ALU.mult)
                        elif g % 2 == 0:
                            tt("vector", tmB, hB, bcC[par], ALU.mult)
                            tv2 = P.mark("vector")
                            P.wait("gpsimd", tv2)
                            tt("gpsimd", aB[:, y_o + dcl * QT: y_o + (dcl + 1) * QT],
                               aB[:, y_o + dcl * QT: y_o + (dcl + 1) * QT],
                               tmB, ALU.add)
                            gp_tok[g] = P.mark("gpsimd")
                        else:
                            tt("vector", tmB, hB, bcC[par], ALU.mult)
                            P.wait("vector", P.tok("gpsimd"))
                            tt("vector", aB[:, y_o + dcl * QT: y_o + (dcl + 1) * QT],
                               aB[:, y_o + dcl * QT: y_o + (dcl + 1) * QT],
                               tmB, ALU.add)
                            gp_tok[g] = P.mark("vector")
                    bc_guard[par] = P.tok("vector")
                P.wait("sync", P.mark("vector"))
                P.wait("sync", P.tok("gpsimd"))
                ydst = cc_yA_in if q < 2 else cc_yB_in
                for dcl in range(4):
                    dma("sync", ydst[dcl * 128:(dcl + 1) * 128, hf * QT:(hf + 1) * QT],
                        aB[:, y_o + dcl * QT: y_o + (dcl + 1) * QT], st="sy")
                if q == 1:
                    P.wait("gpsimd", P.tok("sy"))
                    P.do("gpsimd", lambda e: e.collective_compute(
                        "AllGather", ALU.bypass, replica_groups=RGROUPS,
                        ins=[cc_yA_in.ap()], outs=[cc_yA_out.ap()]), cc="ccy")

            arF.pop()
            P.wait("gpsimd", P.tok("sy"))
            P.do("gpsimd", lambda e: e.collective_compute(
                "AllGather", ALU.bypass, replica_groups=RGROUPS,
                ins=[cc_yB_in.ap()], outs=[cc_yB_out.ap()]), cc="ccy")
            P.wait("gpsimd", P.tok("ccy"))
            P.wait("sync", P.tok("ccy"))
            P.wait("gpsimd", P.tok("vector"))
            P.wait("gpsimd", P.tok("scalar"))
            P.wait("vector", P.tok("scalar"))
            arB.pop()
            xs_o = xn_o
            ya_o = [arB.al(TC, "yA_t0"), arB.al(TC, "yA_t1")]
            yb_o = [arF.al(TC, "yB_t0"), arF.al(TC, "yB_t1")]
            yg = [None, None]

            def yr_load(dc):
                par = dc % 2
                if yg[par] is not None:
                    P.wait("gpsimd", yg[par])
                    P.wait("sync", yg[par])
                dma("sync", aB[:, ya_o[par]:ya_o[par] + TC],
                    cc_yA_out[dc * 128:(dc + 1) * 128, :], st=f"lya{par}")
                dma("gpsimd", aF[:, yb_o[par]:yb_o[par] + TC],
                    cc_yB_out[dc * 128:(dc + 1) * 128, :], st=f"lyr{par}")
                return (P.tok(f"lya{par}"), P.tok(f"lyr{par}"))

            yr_t = {0: yr_load(0), 1: yr_load(1)}
            for dc in range(NDC):
                par = dc % 2
                P.wait("vector", yr_t[dc][0])
                P.wait("vector", yr_t[dc][1])
                xs = aB[:, xs_o + dc * TC: xs_o + (dc + 1) * TC]
                ts("vector", xs, aB[:, ya_o[par]:ya_o[par] + TC], m_lo, None, ALU.mult)
                stt("vector", xs, aF[:, yb_o[par]:yb_o[par] + TC], m_hi, xs, ALU.mult, ALU.add)
                stt("vector", xs, aB[:, u_o + dc * TC: u_o + (dc + 1) * TC],
                    cstc("dskip", dc), xs, ALU.mult, ALU.add)
                tt("vector", xs, xs, aB[:, v_o + dc * TC: v_o + (dc + 1) * TC], ALU.mult)
                yg[par] = P.tok("vector")
                if dc + 2 < NDC:
                    yr_t[dc + 2] = yr_load(dc + 2)
            P.wait("tensor", P.mark("vector"))

            wstL2 = [arB.al(NDC * 128, "w2a"), arB.al(NDC * 128, "w2b")]
            wstL[0], wstL[1] = wstL2[0], wstL2[1]
            for oc in range(NDC):
                slot, ws = wload("w2", oc)
                for tc in range(NT):
                    bank = tc % 4
                    mm_bank(bank, [(aB[:, ws + dc * 128: ws + dc * 128 + 128],
                                    aB[:, xs_o + dc * TC + tc * TCH: xs_o + dc * TC + (tc + 1) * TCH],
                                    pb[bank][:, 0:TCH]) for dc in range(NDC)])

                    def _eo(bank=bank, oc=oc, tc=tc):
                        stt("vector", x_sb[:, oc * TC + tc * TCH: oc * TC + (tc + 1) * TCH],
                            pb[bank][:, 0:TCH], cstc("b2", oc),
                            x_sb[:, oc * TC + tc * TCH: oc * TC + (tc + 1) * TCH],
                            ALU.add, ALU.add)
                    evac(bank, "vector", _eo)
                wguard[slot] = P.tok("tensor")
            DBG.update(l1_xn=xn_o, l1_u=u_o, l1_v=v_o, l1_xs=xs_o)
            arB.pop(); arF.pop()

        def l1_ffn():
            arB.push(); arF.push()
            NG = 4
            PG = NEXP // NG
            xn_o = arB.al(NDC * TC, "xn4")
            xn = ColView(aB, xn_o)
            layer_norm(xn)
            P.wait("tensor", P.tok("vector"))
            wrow_o = arB.al(TC, "Wrow")
            wsl = [arB.al(4096, f"moew{i}") for i in range(4)]
            it_o = arB.al(4 * TC, "inter")
            wbc = [arB.al(TC, "wbc0"), arB.al(TC, "wbc1")]
            tmpb_o = arB.al(TCH, "silu_t")
            wtmb_o = arB.al(16, "W_tmb")
            scr = arF.al(TC, "scores")
            ebr = arF.al(16, "ebias_row")
            g0 = arF.al(96, "gate_scratch")
            c_sc, c_sfc, c_tmp, c_rnd = g0, g0 + 16, g0 + 32, g0 + 48
            c_ab, c_cd, c_mix, c_gsc = g0 + 64, g0 + 68, g0 + 72, g0 + 76
            c_m1, c_n1, c_m2, c_n2 = g0 + 80, g0 + 81, g0 + 82, g0 + 83
            c_t1, c_t2, c_thrg, c_thre = g0 + 84, g0 + 85, g0 + 86, g0 + 87
            c_mx, c_ws, c_selg = g0 + 88, g0 + 89, g0 + 92

            for q_ in ("sync", "gpsimd"):
                P.wait(q_, P.tok("scalar"))
                P.wait(q_, P.tok("vector"))
            dma("sync", aF[:, ebr:ebr + 16],
                io["ebias"][0:1, :].broadcast_to((128, NEXP)), st="lgw")
            dma("gpsimd", aB[:, wsl[0]:wsl[0] + 128].rearrange("p (dc e) -> p dc e", dc=NDC),
                io["gatew_t"].rearrange("(dc p) e -> p dc e", p=128), st="lm0")
            P.wait("tensor", P.tok("lm0"))
            for tc in range(NT):
                bank = 5
                mm_bank(bank, [(aB[:, wsl[0] + dc * 16: wsl[0] + dc * 16 + 16],
                                xn[:, dc * TC + tc * TCH: dc * TC + (tc + 1) * TCH],
                                pb[bank][0:16, 0:TCH]) for dc in range(NDC)])

                def _eg(tc=tc):
                    act("scalar", aF[0:16, scr + tc * TCH: scr + (tc + 1) * TCH],
                        pb[5][0:16, 0:TCH], AF.Sigmoid)
                evac(bank, "scalar", _eg)
            wg_e0_gate = P.tok("tensor")
            P.wait("tensor", P.tok("scalar"))
            P.wait("vector", P.tok("lgw"))

            sfc3 = aF[:, c_sfc:c_sfc + 16].rearrange("p (g e) -> p g e", g=NG)
            tmp3 = aF[:, c_tmp:c_tmp + 16].rearrange("p (g e) -> p g e", g=NG)
            selg3 = aF[:, c_selg:c_selg + 4].rearrange("p (g e) -> p g e", g=NG)
            ab3 = aF[:, c_ab:c_ab + 4].rearrange("p (g e) -> p g e", g=NG)
            cd3 = aF[:, c_cd:c_cd + 4].rearrange("p (g e) -> p g e", g=NG)
            mix3 = aF[:, c_mix:c_mix + 4].rearrange("p (g e) -> p g e", g=NG)
            for j in range(TC // 128):
                jc = slice(j * 128, (j + 1) * 128)
                bank = 6
                if bank_free[bank] is not None:
                    P.wait("tensor", bank_free[bank]); bank_free[bank] = None
                transpose(pb[6][:, 0:16], aF[0:16, scr + j * 128: scr + (j + 1) * 128],
                          identf[0:16, 0:16])
                bank_ready[bank] = P.mark("tensor")

                def _etm():
                    cp("vector", aF[:, c_sc:c_sc + 16], pb[6][:, 0:16])
                evac(bank, "vector", _etm)
                tt("vector", aF[:, c_sfc:c_sfc + 16], aF[:, c_sc:c_sc + 16],
                   aF[:, ebr:ebr + 16], ALU.add)
                tt("vector", ab3[:, :, 0:1], sfc3[:, :, 0:1], sfc3[:, :, 1:2], ALU.add)
                tt("vector", cd3[:, :, 0:1], sfc3[:, :, 2:3], sfc3[:, :, 3:4], ALU.add)
                tt("vector", mix3[:, :, 0:1], sfc3[:, :, 0:1], sfc3[:, :, 1:2], ALU.max)
                tt("vector", selg3[:, :, 0:1], sfc3[:, :, 2:3], sfc3[:, :, 3:4], ALU.max)
                tt("vector", aF[:, c_mix:c_mix + 4], aF[:, c_mix:c_mix + 4],
                   aF[:, c_selg:c_selg + 4], ALU.add)
                tt("vector", aF[:, c_gsc:c_gsc + 4], aF[:, c_ab:c_ab + 4],
                   aF[:, c_cd:c_cd + 4], ALU.max)
                tt("vector", aF[:, c_gsc:c_gsc + 4], aF[:, c_gsc:c_gsc + 4],
                   aF[:, c_mix:c_mix + 4], ALU.max)
                tt("vector", aF[:, c_m1:c_m1 + 1], aF[:, c_gsc:c_gsc + 1],
                   aF[:, c_gsc + 1:c_gsc + 2], ALU.max)
                tt("vector", aF[:, c_n1:c_n1 + 1], aF[:, c_gsc:c_gsc + 1],
                   aF[:, c_gsc + 1:c_gsc + 2], ALU.min)
                tt("vector", aF[:, c_m2:c_m2 + 1], aF[:, c_gsc + 2:c_gsc + 3],
                   aF[:, c_gsc + 3:c_gsc + 4], ALU.max)
                tt("vector", aF[:, c_n2:c_n2 + 1], aF[:, c_gsc + 2:c_gsc + 3],
                   aF[:, c_gsc + 3:c_gsc + 4], ALU.min)
                tt("vector", aF[:, c_t1:c_t1 + 1], aF[:, c_m1:c_m1 + 1],
                   aF[:, c_m2:c_m2 + 1], ALU.min)
                tt("vector", aF[:, c_t2:c_t2 + 1], aF[:, c_n1:c_n1 + 1],
                   aF[:, c_n2:c_n2 + 1], ALU.max)
                tt("vector", aF[:, c_thrg:c_thrg + 1], aF[:, c_t1:c_t1 + 1],
                   aF[:, c_t2:c_t2 + 1], ALU.max)
                ts("vector", aF[:, c_selg:c_selg + 4], aF[:, c_gsc:c_gsc + 4],
                   aF[:, c_thrg:c_thrg + 1], None, ALU.is_ge)
                tt("vector", tmp3[:, :, :], sfc3[:, :, :],
                   selg3[:, :, 0:1].broadcast_to((128, NG, PG)), ALU.mult)
                cp("vector", aF[:, c_rnd:c_rnd + 16], aF[:, c_tmp:c_tmp + 16])
                for k in range(4):
                    P.do("vector", lambda e: e.reduce_max(
                        aF[:, c_mx:c_mx + 1], aF[:, c_rnd:c_rnd + 16], AXX))
                    if k < 3:
                        ts("vector", aF[:, c_ab:c_ab + 16], aF[:, c_rnd:c_rnd + 16],
                           aF[:, c_mx:c_mx + 1], None, ALU.is_equal)
                        stt("vector", aF[:, c_rnd:c_rnd + 16], aF[:, c_ab:c_ab + 16],
                            -1e30, aF[:, c_rnd:c_rnd + 16], ALU.mult, ALU.add)
                ts("vector", aF[:, c_tmp:c_tmp + 16], aF[:, c_tmp:c_tmp + 16],
                   aF[:, c_mx:c_mx + 1], None, ALU.is_ge)
                tt("vector", aF[:, c_tmp:c_tmp + 16], aF[:, c_tmp:c_tmp + 16],
                   aF[:, c_sc:c_sc + 16], ALU.mult)
                P.do("vector", lambda e: e.reduce_sum(
                    aF[:, c_ws:c_ws + 1], aF[:, c_tmp:c_tmp + 16], AXX))
                ts("vector", aF[:, c_ws:c_ws + 1], aF[:, c_ws:c_ws + 1], 1e-20, None, ALU.add)
                recip("vector", aF[:, c_ws:c_ws + 1], aF[:, c_ws:c_ws + 1])
                ts("vector", aF[:, c_ws:c_ws + 1], aF[:, c_ws:c_ws + 1], 2.5, None, ALU.mult)
                ts("vector", aB[:, wtmb_o:wtmb_o + 16], aF[:, c_tmp:c_tmp + 16],
                   aF[:, c_ws:c_ws + 1], None, ALU.mult)
                twm = P.mark("vector")
                P.wait("tensor", twm)
                if bank_free[7] is not None:
                    P.wait("tensor", bank_free[7]); bank_free[7] = None
                transpose(pT[0:16, 0:128], aB[:, wtmb_o:wtmb_o + 16], identb[:, :])
                bank_ready[7] = P.mark("tensor")

                def _ewr(j=j):
                    cp("vector", aB[0:16, wrow_o + j * 128: wrow_o + (j + 1) * 128],
                       pT[0:16, 0:128])
                evac(7, "vector", _ewr)
            P.wait("sync", P.mark("vector"))
            dma("sync", wsc_scr[:, :], aB[0:16, wrow_o:wrow_o + TC], st="swsc")

            wg_e = [wg_e0_gate, None, None, None]
            wbcg = [None, None]
            tdown = None

            def eload(slot, src_ap, nin=NDC):
                if wg_e[slot] is not None:
                    P.wait("gpsimd", wg_e[slot])
                    wg_e[slot] = None
                dma("gpsimd",
                    aB[:, wsl[slot]:wsl[slot] + 4096].rearrange("p (b i) -> p b i", b=nin),
                    src_ap, st=f"lm{slot}")
                return P.tok(f"lm{slot}")

            for e in range(NEXP + 1):
                shared = (e == NEXP)
                sg, su, sd = (3 * e) % 4, (3 * e + 1) % 4, (3 * e + 2) % 4
                if shared:
                    tg = eload(sg, io["sg_t"].rearrange("(dc p) i -> p dc i", p=128))
                    tu = eload(su, io["su_t"].rearrange("(dc p) i -> p dc i", p=128))
                    td = eload(sd, io["sd_t"].rearrange("(ic p) o -> p ic o", p=128), nin=4)
                else:
                    tg = eload(sg, io["gp_t"][e].rearrange("(dc p) i -> p dc i", p=128))
                    tu = eload(su, io["up_t"][e].rearrange("(dc p) i -> p dc i", p=128))
                    td = eload(sd, io["dp_t"][e].rearrange("(ic p) o -> p ic o", p=128), nin=4)
                    if wbcg[e % 2] is not None:
                        P.wait("gpsimd", wbcg[e % 2])
                    P.wait("gpsimd", P.tok("swsc"))
                    dma("gpsimd", aB[:, wbc[e % 2]:wbc[e % 2] + TC],
                        wsc_scr[e:e + 1, :].broadcast_to((128, TC)), st="lwb")
                if tdown is not None:
                    P.wait("vector", tdown)
                P.wait("tensor", tg)
                P.wait("tensor", tu)
                if not shared:
                    P.wait("vector", P.tok("lwb"))

                def emit_downs(tc, tw, sd=sd):
                    P.wait("tensor", tw)
                    P.wait("tensor", td)
                    for oc in range(NDC):
                        bank = 4 + (oc + tc) % 4
                        mm_bank(bank, [(aB[:, wsl[sd] + ic * 1024 + oc * 128:
                                              wsl[sd] + ic * 1024 + oc * 128 + 128],
                                        aB[:, it_o + ic * TC + tc * TCH:
                                              it_o + ic * TC + (tc + 1) * TCH],
                                        pb[bank][:, 0:TCH]) for ic in range(4)])

                        def _ed(bank=bank, oc=oc, tc=tc):
                            xs = x_sb[:, oc * TC + tc * TCH: oc * TC + (tc + 1) * TCH]
                            tt("vector", xs, xs, pb[bank][:, 0:TCH], ALU.add)
                        evac(bank, "vector", _ed)

                pend = None
                for tc in range(NT):
                    for ic in range(4):
                        bg, bu = ic % 2, 2 + ic % 2
                        mm_bank(bg, [(aB[:, wsl[sg] + (dc * 4 + ic) * 128:
                                            wsl[sg] + (dc * 4 + ic) * 128 + 128],
                                      xn[:, dc * TC + tc * TCH: dc * TC + (tc + 1) * TCH],
                                      pb[bg][:, 0:TCH]) for dc in range(NDC)])

                        def _es(bg=bg):
                            P.wait("scalar", P.tok("vector"))
                            act("scalar", aB[:, tmpb_o:tmpb_o + TCH], pb[bg][:, 0:TCH],
                                AF.Silu)
                        evac(bg, "scalar", _es)
                        tsi = P.tok("scalar")
                        mm_bank(bu, [(aB[:, wsl[su] + (dc * 4 + ic) * 128:
                                            wsl[su] + (dc * 4 + ic) * 128 + 128],
                                      xn[:, dc * TC + tc * TCH: dc * TC + (tc + 1) * TCH],
                                      pb[bu][:, 0:TCH]) for dc in range(NDC)])

                        def _ei(bu=bu, ic=ic, tc=tc, tsi=tsi, e=e, shared=shared):
                            P.wait("vector", tsi)
                            dst = aB[:, it_o + ic * TC + tc * TCH: it_o + ic * TC + (tc + 1) * TCH]
                            tt("vector", dst, aB[:, tmpb_o:tmpb_o + TCH],
                               pb[bu][:, 0:TCH], ALU.mult)
                            if not shared:
                                tt("vector", dst, dst,
                                   aB[:, wbc[e % 2] + tc * TCH: wbc[e % 2] + (tc + 1) * TCH],
                                   ALU.mult)
                        evac(bu, "vector", _ei)
                    tw = P.tok("vector")
                    if pend is not None:
                        emit_downs(*pend)
                    pend = (tc, tw)
                emit_downs(*pend)
                tdown = P.tok("tensor")
                if not shared:
                    wbcg[e % 2] = P.tok("vector")
                for sl in (sg, su, sd):
                    wg_e[sl] = P.tok("tensor")
            DBG.update(moe_xn=xn_o, moe_wrow=wrow_o, moe_inter=it_o, moe_scr=scr)
            arB.pop(); arF.pop()

        if upto >= 1:
            l0_attn()
            P.barrier()
        if upto >= 2:
            tq_scratch = weight_quant()
            P.barrier()
            if os.environ.get("KWQONLY", "") != "1":
                l0_ffn(tq_scratch)
                P.barrier()
        if upto >= 3:
            l1_attn()
            P.barrier()
        if upto >= 4:
            l1_ffn()
            P.barrier()

        for tname, shape, fn in taps:
            src = fn(dict(x_sb=x_sb, aB=aB, aF=aF, dbgc=dbgc))
            q = "gpsimd" if src.dtype != F32 else "sync"
            dma(q, tapd[tname][:, :], src, st="dout" if q == "sync" else "dout2")
        for dc in range(NDC):
            dma("sync", out_fm[dc * 128:(dc + 1) * 128, :], x_sb[:, dc * TC:(dc + 1) * TC], st="dout")
        tout = P.tok("dout")
        P.wait("sync", tout)

        P.check()
        sems = {}
        for sname in list(ENGS) + sorted(P.dma_sems):
            sems[sname] = stx.enter_context(nc.semaphore(sname))
        P.replay(stx.enter_context(nc.Block()), sems)

    return nc


def stage_inputs(inputs, TC):
    B = 4
    NSEQ = 2 * TC
    x = np.asarray(inputs["L0_attn_x"], np.float32)

    def t(a):
        return np.ascontiguousarray(np.asarray(a, np.float32).T)

    rf = np.asarray(inputs["L0_attn_RF"], np.float32)
    rfblk = np.zeros((NPAIR, 128, 128), np.float32)
    for hp in range(NPAIR):
        rfblk[hp, 0:64, 0:64] = rf[2 * hp]
        rfblk[hp, 64:128, 64:128] = rf[2 * hp + 1]

    def pcol(v):
        return np.ascontiguousarray(np.asarray(v, np.float32).reshape(NDC, 128).T)

    base = {
        "wqkv": np.ascontiguousarray(np.asarray(inputs["L0_attn_W_qkv"], np.float32)),
        "rfblk": rfblk,
        "wout": np.ascontiguousarray(np.asarray(inputs["L0_attn_W_out"], np.float32)),
        "bout": pcol(inputs["L0_attn_b_out"]),
        "wg_t": t(inputs["L0_ffn_w_gate"]),
        "wu_t": t(inputs["L0_ffn_w_up"]),
        "wd_t": t(inputs["L0_ffn_w_down"]),
        "lng": pcol(inputs["L1_attn_ln_g"]),
        "lnb": pcol(inputs["L1_attn_ln_b"]),
        "w1": np.ascontiguousarray(np.asarray(inputs["L1_attn_w1_w"], np.float32)),
        "b1": pcol(inputs["L1_attn_w1_b"]),
        "v1": np.ascontiguousarray(np.asarray(inputs["L1_attn_v1_w"], np.float32)),
        "bv": pcol(inputs["L1_attn_v1_b"]),
        "w2": np.ascontiguousarray(np.asarray(inputs["L1_attn_w2_w"], np.float32)),
        "b2": pcol(inputs["L1_attn_w2_b"]),
        "wdt": np.ascontiguousarray(np.asarray(inputs["L1_attn_W_dt"], np.float32)),
        "bdt": pcol(inputs["L1_attn_b_dt"]),
        "convw": np.ascontiguousarray(
            np.asarray(inputs["L1_attn_conv_w"], np.float32).reshape(NDC, 128, 3)
            .transpose(1, 0, 2).reshape(128, NDC * 3)),
        "dskip": pcol(inputs["L1_attn_D_skip"]),
        "wb": np.ascontiguousarray(np.asarray(inputs["L1_attn_W_B"], np.float32)),
        "wc": np.ascontiguousarray(np.asarray(inputs["L1_attn_W_C"], np.float32)),
        "gatew_t": t(inputs["L1_ffn_gate_w"]),
        "ebias": np.asarray(inputs["L1_ffn_e_bias"], np.float32).reshape(1, NEXP),
        "gp_t": np.ascontiguousarray(np.asarray(inputs["L1_ffn_gate_proj"], np.float32).transpose(0, 2, 1)),
        "up_t": np.ascontiguousarray(np.asarray(inputs["L1_ffn_up_proj"], np.float32).transpose(0, 2, 1)),
        "dp_t": np.ascontiguousarray(np.asarray(inputs["L1_ffn_down_proj"], np.float32).transpose(0, 2, 1)),
        "sg_t": t(inputs["L1_ffn_shared_gate_w"]),
        "su_t": t(inputs["L1_ffn_shared_up_w"]),
        "sd_t": t(inputs["L1_ffn_shared_down_w"]),
    }
    alog3 = np.asarray(inputs["L1_attn_A_log"], np.float32).reshape(NDC, 128, NST)
    in_maps = []
    for c in range(8):
        b, h = c // 2, c % 2
        m = dict(base)
        m["x_fm"] = np.ascontiguousarray(x[b, h * TC:(h + 1) * TC, :].T)
        meta = np.zeros((128, 4), np.float32)
        meta[:, 0] = 1.0 if h == 1 else 0.0
        meta[:, 1] = 1.0 if h == 0 else 0.0
        m["meta"] = meta
        m["alog"] = np.ascontiguousarray(
            np.roll(alog3, -4 * h, axis=0).transpose(1, 0, 2).reshape(128, NDC * NST))
        in_maps.append(m)
    return in_maps


def assemble(res, TC):
    x = np.zeros((4, 2 * TC, D), np.float32)
    for c in range(8):
        b, h = c // 2, c % 2
        x[b, h * TC:(h + 1) * TC, :] = res.results[c]["out_fm"].T
    return x


def kernel(**inputs):
    TC = 2048
    nc = build(TC, upto=4)
    in_maps = stage_inputs(inputs, TC)
    res = run_bass_kernel_spmd(nc, in_maps, core_ids=list(range(8)))
    return assemble(res, TC)


# version 14
# speedup vs baseline: 1.1925x; 1.0278x over previous
import contextlib
import os

import numpy as np
import concourse.bass as bass
import concourse.mybir as mybir
from concourse.bass_utils import run_bass_kernel_spmd

F32 = mybir.dt.float32
BF16 = mybir.dt.bfloat16
AF = mybir.ActivationFunctionType
ALU = mybir.AluOpType
AXX = mybir.AxisListType.X

D = 1024
NDC = 8
HEADS = 16
NPAIR = 8
INNER = 4096
NIC = 32
NST = 16
MOE_I = 512
NEXP = 16
RGROUPS = [[0, 1], [2, 3], [4, 5], [6, 7]]
RMAGIC = 3 * 2.0 ** 22
EPS = 1e-5

ENGS = ("tensor", "vector", "scalar", "gpsimd", "sync")


class Plan:

    SELF_SERIAL = ("vector", "scalar", "gpsimd")

    def __init__(self):
        self.stream = {e: [] for e in ENGS}
        self.cnt = {}
        self.dirty = {e: None for e in ENGS}
        self.dma_sems = set()

    def do(self, eng, fn, dma=None, cc=None):
        if cc:
            step = {"fn": fn, "incs": [(cc, 1)], "sem": cc}
            self.cnt[cc] = self.cnt.get(cc, 0) + 1
            self.dma_sems.add(cc)
        elif dma:
            sem = dma
            step = {"fn": fn, "incs": [(sem, 16)], "sem": sem}
            self.cnt[sem] = self.cnt.get(sem, 0) + 16
            self.dma_sems.add(sem)
        else:
            if eng in self.SELF_SERIAL and self.cnt.get(eng, 0) > 0:
                self.stream[eng].append({"wait": (eng, self.cnt[eng])})
            step = {"fn": fn, "incs": [(eng, 1)], "sem": eng}
            self.cnt[eng] = self.cnt.get(eng, 0) + 1
        step["tok"] = (step["sem"], self.cnt[step["sem"]])
        self.stream[eng].append(step)
        self.dirty[eng] = step
        return step

    def mark(self, eng, sem=None):
        step = self.dirty[eng]
        assert step is not None, f"mark({eng}) with no dirty step"
        return step["tok"]

    def tok(self, sem):
        return (sem, self.cnt.get(sem, 0))

    def wait(self, eng, token):
        if token is None:
            return
        sem, val = token
        if val <= 0:
            return
        self.stream[eng].append({"wait": (sem, val)})

    def barrier(self):
        toks = []
        for e in ENGS:
            if self.dirty[e] is not None:
                toks.append(self.dirty[e]["tok"])
                self.dirty[e] = None
        for d in sorted(self.dma_sems):
            if self.cnt.get(d, 0) > 0:
                toks.append(self.tok(d))
        for e in ENGS:
            for t in toks:
                self.wait(e, t)

    def check(self):
        pos = {e: 0 for e in ENGS}
        sem = {}
        progressed = True
        while progressed:
            progressed = False
            for e in ENGS:
                items = self.stream[e]
                while pos[e] < len(items):
                    it = items[pos[e]]
                    if "wait" in it:
                        sm, v = it["wait"]
                        if sem.get(sm, 0) < v:
                            break
                    else:
                        for (sm, k) in it["incs"]:
                            sem[sm] = sem.get(sm, 0) + k
                    pos[e] += 1
                    progressed = True
        stuck = {e: (pos[e], len(self.stream[e])) for e in ENGS if pos[e] < len(self.stream[e])}
        if stuck:
            det = {}
            for e in stuck:
                it = self.stream[e][pos[e]]
                det[e] = ("waits", it["wait"], "have", sem.get(it["wait"][0], 0)) if "wait" in it else "??"
            raise RuntimeError(f"PLAN DEADLOCK: {stuck} {det}")

    def replay(self, blk, sems):
        for e in ENGS:
            items = self.stream[e]

            def run(eng, items=items):
                prev_wait = None
                for it in items:
                    if "wait" in it:
                        if it["wait"] == prev_wait:
                            continue
                        prev_wait = it["wait"]
                        sm, v = it["wait"]
                        eng.wait_ge(sems[sm], v)
                    else:
                        instr = it["fn"](eng)
                        for (sm, k) in it["incs"]:
                            instr.then_inc(sems[sm], k)
            getattr(blk, e)(run)


class Arena:
    def __init__(self, n, name):
        self.n, self.name = n, name
        self.off = 0
        self.marks = []
        self.peak = 0

    def al(self, cols, what=""):
        o = self.off
        self.off += cols
        self.peak = max(self.peak, self.off)
        assert self.off <= self.n, f"{self.name} overflow at {what}: {self.off}>{self.n}"
        return o

    def push(self):
        self.marks.append(self.off)

    def pop(self):
        self.off = self.marks.pop()


class ColView:

    def __init__(self, t, off):
        self.t, self.off = t, off

    def __getitem__(self, key):
        p, c = key
        return self.t[p, c.start + self.off: c.stop + self.off]

    @property
    def dtype(self):
        return self.t[:, 0:1].dtype


DBG = {}


def build(TC=2048, upto=2, taps=()):
    NT = max(1, TC // 512)
    TCH = TC // NT
    NSUB = TC // 128
    nc = bass.Bass("TRN2", target_bir_lowering=False, debug=False)

    io = {}

    def EI(name, shape):
        io[name] = nc.dram_tensor(name, list(shape), F32, kind="ExternalInput")

    EI("x_fm", (D, TC))
    EI("wqkv", (D, 3 * D)); EI("rfblk", (NPAIR, 128, 128))
    EI("wout", (D, D)); EI("bout", (128, NDC))
    EI("wg_t", (D, INNER)); EI("wu_t", (D, INNER)); EI("wd_t", (INNER, D))
    EI("lng", (128, NDC)); EI("lnb", (128, NDC))
    EI("w1", (D, D)); EI("b1", (128, NDC))
    EI("v1", (D, D)); EI("bv", (128, NDC))
    EI("w2", (D, D)); EI("b2", (128, NDC))
    EI("wdt", (D, D)); EI("bdt", (128, NDC))
    EI("convw", (128, NDC * 3)); EI("alog", (128, NDC * NST)); EI("dskip", (128, NDC))
    EI("wb", (D, NST)); EI("wc", (D, NST))
    EI("gatew_t", (D, NEXP)); EI("ebias", (1, NEXP))
    EI("gp_t", (NEXP, D, MOE_I)); EI("up_t", (NEXP, D, MOE_I)); EI("dp_t", (NEXP, MOE_I, D))
    EI("sg_t", (D, MOE_I)); EI("su_t", (D, MOE_I)); EI("sd_t", (MOE_I, D))
    EI("meta", (128, 4))

    out_fm = nc.dram_tensor("out_fm", [D, TC], F32, kind="ExternalOutput")
    tapd = {}
    for tname, shape, _ in taps:
        tapd[tname] = nc.dram_tensor("tap_" + tname, list(shape), F32, kind="ExternalOutput")

    cc_kv_in = nc.dram_tensor("cc_kv_in", [64, HEADS * 65], F32)
    cc_kv_out = nc.dram_tensor("cc_kv_out", [64, HEADS * 65], F32)
    cc_edge_in = nc.dram_tensor("cc_edge_in", [128, NDC * 2], F32)
    cc_edge_out = nc.dram_tensor("cc_edge_out", [2 * 128, NDC * 2], F32)
    cc_dtA_in = nc.dram_tensor("cc_dtA_in", [2, D, TC // 2], BF16)
    cc_dtA_out = nc.dram_tensor("cc_dtA_out", [2, 2, D, TC // 2], BF16)
    cc_dtB_in = nc.dram_tensor("cc_dtB_in", [2, D, TC // 2], BF16)
    cc_dtB_out = nc.dram_tensor("cc_dtB_out", [2, 2, D, TC // 2], BF16)
    cc_bc_in = nc.dram_tensor("cc_bc_in", [2, NST, TC], BF16)
    cc_bc_out = nc.dram_tensor("cc_bc_out", [2, 2, NST, TC], BF16)
    cc_yA_in = nc.dram_tensor("cc_yA_in", [512, TC], BF16)
    cc_yA_out = nc.dram_tensor("cc_yA_out", [D, TC], BF16)
    cc_yB_in = nc.dram_tensor("cc_yB_in", [512, TC], BF16)
    cc_yB_out = nc.dram_tensor("cc_yB_out", [D, TC], BF16)
    wq_g = nc.dram_tensor("wq_g", [D, INNER], BF16)
    rb_scr = nc.dram_tensor("rb_scr", [64, 128], F32)
    wq_u = nc.dram_tensor("wq_u", [D, INNER], BF16)
    wq_d = nc.dram_tensor("wq_d", [INNER, D], BF16)
    wsc_scr = nc.dram_tensor("wsc_scr", [NEXP, TC], BF16)

    P = Plan()

    with contextlib.ExitStack() as stx:
        def SB(name, shape, dtype):
            return stx.enter_context(nc.sbuf_tensor(name, list(shape), dtype))

        def PSUM(name, shape, dtype=F32):
            return stx.enter_context(nc.psum_tensor(name, list(shape), dtype))

        x_sb = SB("x_sb", (128, NDC * TC), F32)
        ABN = max(TC * 28 + 1024, 36000) if TC <= 1024 else 55296
        AFN = 8064 if TC >= 2048 else TC * 4 + 3072
        aB = SB("aB", (128, ABN), BF16)
        aF = SB("aF", (128, AFN), F32)
        cst = SB("cst", (128, 244), F32)
        ones_b = SB("ones_b", (128, 2), BF16)
        rowones_f = SB("rowones_f", (1, 128), F32)
        selp = SB("selp", (16, 128 * NPAIR), BF16)
        identb = SB("identb", (128, 128), BF16)
        identf = SB("identf", (128, 128), F32)
        identf_ap = [None]
        dbgc = SB("dbgc", (1, 16), F32)
        P.do("vector", lambda e: e.memset(dbgc[:, :], 0.0))

        pb = [PSUM(f"pb{i}", (128, 512), F32) for i in range(8)]
        pT = pb[7][:, :].bitcast(BF16)


        def dma(queue, dst, src, st="ld0"):
            return P.do(queue, lambda e, d=dst, s=src: e.dma_start(out=d, in_=s), dma=st)

        def mm(dst, lhsT, rhs, start, stop):
            return P.do("tensor", lambda e, d=dst, l=lhsT, r=rhs, st=start, sp=stop:
                        e.matmul(d, l, r, start=st, stop=sp, skip_group_check=True))

        def act(eng, dst, src, func, bias=0.0, scale=1.0):
            return P.do(eng, lambda e, d=dst, s=src, f=func, b=bias, sc=scale:
                        e.activation(d, s, f, bias=b, scale=sc))

        def tt(eng, dst, a, b, op):
            return P.do(eng, lambda e, d=dst, x=a, y=b, o=op: e.tensor_tensor(d, x, y, o))

        def ts(eng, dst, a, s1, s2, op0, op1=ALU.bypass):
            return P.do(eng, lambda e, d=dst, x=a, u=s1, v=s2, o0=op0, o1=op1:
                        e.tensor_scalar(d, x, u, v, o0, o1))

        def stt(eng, dst, in0, scal, in1, op0, op1):
            return P.do(eng, lambda e, d=dst, a=in0, s=scal, b=in1, o0=op0, o1=op1:
                        e.scalar_tensor_tensor(d, a, s, b, o0, o1))

        def cp(eng, dst, src):
            return P.do(eng, lambda e, d=dst, s=src: e.tensor_copy(d, s))

        def recip(eng, dst, src):
            return P.do(eng, lambda e, d=dst, s=src: e.reciprocal(d, s))

        def transpose(dst_psum, src, ident):
            return P.do("tensor", lambda e, d=dst_psum, s=src, i=ident: e.transpose(d, s, i))

        bank_free = [None] * 8
        bank_ready = [None] * 8

        def mm_bank(bank, mms):
            if bank_free[bank] is not None:
                P.wait("tensor", bank_free[bank])
                bank_free[bank] = None
            n = len(mms)
            for i, (l, r, dap) in enumerate(mms):
                mm(dap if dap is not None else pb[bank][:, 0:TCH], l, r, i == 0, i == n - 1)
            bank_ready[bank] = P.mark("tensor")
            return bank_ready[bank]

        def evac(bank, eng, emit):
            P.wait(eng, bank_ready[bank])
            emit()
            bank_free[bank] = P.mark(eng)

        arB = Arena(ABN, "aB")
        arF = Arena(AFN, "aF")

        dma("sync", x_sb[:, :].rearrange("p (dc t) -> p dc t", dc=NDC),
            io["x_fm"].rearrange("(dc p) t -> p dc t", p=128))
        CST = {}
        _off = [0]
        for nm, w in [("bout", NDC), ("b1", NDC), ("bv", NDC), ("b2", NDC), ("bdt", NDC),
                      ("lng", NDC), ("lnb", NDC), ("convw", NDC * 3), ("dskip", NDC),
                      ("alog", NDC * NST), ("meta", 4)]:
            CST[nm] = _off[0]
            dma("sync", cst[:, _off[0]:_off[0] + w], io[nm][:, :])
            _off[0] += w

        def cstc(name, i=0):
            return cst[:, CST[name] + i: CST[name] + i + 1]

        P.do("vector", lambda e: e.memset(ones_b[:, :], 1.0))
        P.do("vector", lambda e: e.memset(rowones_f[:, :], 1.0))
        _fc = [0]
        FCBASE = 228

        def fconst(v):
            col = FCBASE + _fc[0]
            _fc[0] += 1
            assert col < 256
            P.do("vector", lambda e, c=col, vv=v: e.memset(cst[:, c:c + 1], vv))
            nc.const_aps.register(F32, float(v), cst[:, col:col + 1])
            return float(v)

        for _v in (EPS, EPS * EPS, 1e-6, 1.0, RMAGIC):
            fconst(_v)
        SKIP_CONSTS = os.environ.get("KSKIP", "")
        if "id" not in SKIP_CONSTS:
            P.do("gpsimd", lambda e: e.memset(identb[:, :], 0.0))
        if "id" not in SKIP_CONSTS:
            P.wait("gpsimd", P.mark("gpsimd"))
            P.do("gpsimd", lambda e: e.affine_select(
                out=identb[:, :], in_=identb[:, :], compare_op=ALU.not_equal,
                fill=1.0, base=0, pattern=[[-1, 128]], channel_multiplier=1))
            P.do("gpsimd", lambda e: e.memset(identf[:, :], 0.0))
            P.wait("gpsimd", P.mark("gpsimd"))
            P.do("gpsimd", lambda e: e.affine_select(
                out=identf[:, :], in_=identf[:, :], compare_op=ALU.not_equal,
                fill=1.0, base=0, pattern=[[-1, 128]], channel_multiplier=1))
        if "selp" not in SKIP_CONSTS:
            P.do("gpsimd", lambda e: e.iota(aF[0:16, 0:128], pattern=[[1, 128]], base=0,
                                            channel_multiplier=0, allow_small_or_imprecise_dtypes=True))
            P.do("gpsimd", lambda e: e.iota(aF[0:16, 128:256], pattern=[[0, 128]], base=0,
                                            channel_multiplier=1, allow_small_or_imprecise_dtypes=True))
        P.barrier()
        if "selp" not in SKIP_CONSTS:
            ts("vector", aF[0:16, 0:128], aF[0:16, 0:128], 64.0, None, ALU.is_ge)
            tt("vector", aF[0:16, 128:256], aF[0:16, 128:256], aF[0:16, 0:128], ALU.subtract)
            for hp in range(NPAIR):
                ts("vector", selp[:, hp * 128:(hp + 1) * 128], aF[0:16, 128:256],
                   float(2 * hp), None, ALU.is_equal)
        if "aexp" not in SKIP_CONSTS:
            a0 = CST["alog"]
            act("scalar", cst[:, a0:a0 + NDC * NST], cst[:, a0:a0 + NDC * NST], AF.Exp)
            P.wait("vector", P.mark("scalar"))
            ts("vector", cst[:, a0:a0 + NDC * NST], cst[:, a0:a0 + NDC * NST], -1.0, None, ALU.mult)
        identf_ap[0] = identf[:, :]
        P.barrier()

        def layer_norm(dst_bf, inner_gb=False):
            arB.push(); arF.push()
            sq = [arB.al(TCH, "ln_sq"), arB.al(TCH, "ln_sq2")]
            xb = [arB.al(TCH, "ln_xb"), arB.al(TCH, "ln_xb2")]
            rb = arF.al(TC, "rstd_b")
            mb = arF.al(TC, "murstd_b")
            tmpc = arF.al(TCH, "ln_tmp")
            ro = arF.al(5 * 64, "ln_rows")
            arF.off -= 5 * 64
            ro = arF.al(5 * TCH, "ln_rows")
            mu = aF[0:1, ro:ro + TCH]
            va = aF[0:1, ro + TCH:ro + 2 * TCH]
            r3 = aF[0:1, ro + 2 * TCH:ro + 3 * TCH]
            mr = aF[0:1, ro + 3 * TCH:ro + 4 * TCH]
            sc1 = aF[0:1, ro + 4 * TCH:ro + 5 * TCH]
            mm_toks = {}
            for tc in range(NT):
                if bank_free[5] is not None:
                    P.wait("tensor", bank_free[5]); bank_free[5] = None
                if bank_free[6] is not None:
                    P.wait("tensor", bank_free[6]); bank_free[6] = None
                for dc in range(NDC):
                    par = dc % 2
                    xs = x_sb[:, dc * TC + tc * TCH: dc * TC + (tc + 1) * TCH]
                    prev = mm_toks.get(par)
                    if prev is not None:
                        P.wait("vector", prev); P.wait("scalar", prev)
                    cp("vector", aB[:, xb[par]:xb[par] + TCH], xs)
                    tv = P.mark("vector")
                    act("scalar", aB[:, sq[par]:sq[par] + TCH], xs, AF.Square)
                    tsc = P.mark("scalar")
                    P.wait("tensor", tv); P.wait("tensor", tsc)
                    mm(pb[5][0:1, 0:TCH], ones_b[:, 0:1], aB[:, xb[par]:xb[par] + TCH], dc == 0, dc == NDC - 1)
                    mm(pb[6][0:1, 0:TCH], ones_b[:, 0:1], aB[:, sq[par]:sq[par] + TCH], dc == 0, dc == NDC - 1)
                    mm_toks[par] = P.mark("tensor")
                tmm = P.tok("tensor")
                P.wait("vector", tmm)
                ts("vector", mu, pb[5][0:1, 0:TCH], 1.0 / D, None, ALU.mult)
                ts("vector", va, pb[6][0:1, 0:TCH], 1.0 / D, None, ALU.mult)
                tfree = P.mark("vector")
                bank_free[5] = tfree; bank_free[6] = tfree
                tt("vector", sc1, mu, mu, ALU.mult)
                tt("vector", va, va, sc1, ALU.subtract)
                tva = P.mark("vector")
                P.wait("scalar", tva)
                if inner_gb:
                    act("scalar", r3, va, AF.Ln, bias=EPS * EPS, scale=1.0 + EPS)
                else:
                    act("scalar", r3, va, AF.Ln, bias=EPS)
                act("scalar", r3, r3, AF.Exp, scale=-0.5)
                trs = P.mark("scalar")
                P.wait("vector", trs)
                tt("vector", mr, mu, r3, ALU.mult)
                tstat = P.mark("vector")
                P.wait("tensor", tstat)
                for (row, dsto) in ((r3, rb), (mr, mb)):
                    mm_bank(4, [(rowones_f[:, :], row, None)])

                    def _e(tc=tc, dsto=dsto):
                        cp("vector", aF[:, dsto + tc * TCH: dsto + (tc + 1) * TCH], pb[4][:, 0:TCH])
                    evac(4, "vector", _e)
            for dc in range(NDC):
                for tc in range(NT):
                    lo = dc * TC + tc * TCH
                    hi = lo + TCH
                    tt("vector", aF[:, tmpc:tmpc + TCH], x_sb[:, lo:hi],
                       aF[:, rb + tc * TCH: rb + (tc + 1) * TCH], ALU.mult)
                    if inner_gb:
                        tt("vector", aF[:, tmpc:tmpc + TCH], aF[:, tmpc:tmpc + TCH],
                           aF[:, mb + tc * TCH: mb + (tc + 1) * TCH], ALU.subtract)
                        ts("vector", dst_bf[:, lo:hi], aF[:, tmpc:tmpc + TCH],
                           cstc("lng", dc), cstc("lnb", dc), ALU.mult, ALU.add)
                    else:
                        tt("vector", dst_bf[:, lo:hi], aF[:, tmpc:tmpc + TCH],
                           aF[:, mb + tc * TCH: mb + (tc + 1) * TCH], ALU.subtract)
            arB.pop(); arF.pop()

        def l0_attn():
            arB.push(); arF.push()
            xn_o = arB.al(NDC * TC, "xn1")
            xn = ColView(aB, xn_o)
            layer_norm(xn)
            P.wait("tensor", P.tok("vector"))
            phi_q = arB.al(NPAIR * TC, "phi_q")
            phi_kg = arB.al(4 * TC, "phi_kg")
            rfb = arB.al(NPAIR * 128, "rfblk")
            wst = [arB.al(2 * NDC * 128, "wqk0"), arB.al(2 * NDC * 128, "wqk1")]
            qk = arB.al(TC, "qk")
            vv = arB.al(512, "vT")
            pkT = arB.al(512, "phikT")
            wvm = arB.al(NDC * 512, "wv_moving")
            pairkv = qk
            kvcols = arF.al(1040, "kvcols")
            nrmtm_o = arF.al(NSUB * 16, "nrmtm")
            nrmtm = aF[:, nrmtm_o:nrmtm_o + NSUB * 16]
            nrow_o = arB.al(TC, "nrow")
            nrow = aB[0:16, nrow_o:nrow_o + TC]
            dma("gpsimd", aB[:, rfb:rfb + NPAIR * 128].rearrange("p (h c) -> p h c", h=NPAIR),
                io["rfblk"].rearrange("h p c -> p h c"), st="lrf")
            trf = P.tok("lrf")

            wst_guard = [None, None]
            qkb = [qk, nrow_o]
            qk_guard = [None, None]
            wv_guard = None
            for grp in range(2):
                prev = None
                for lp in range(4):
                    hp = grp * 4 + lp
                    slot = lp % 2
                    ws = wst[slot]
                    if wst_guard[slot] is not None:
                        P.wait("gpsimd", wst_guard[slot])
                    for part in range(2):
                        col0 = part * D + hp * 128
                        dma("gpsimd",
                            aB[:, ws + part * NDC * 128: ws + (part + 1) * NDC * 128]
                            .rearrange("p (dc c) -> p dc c", dc=NDC),
                            io["wqkv"].rearrange("(dc p) c -> p dc c", p=128)[:, :, col0:col0 + 128],
                            st=f"lw{slot}")
                    tw = P.tok(f"lw{slot}")
                    P.wait("tensor", tw)
                    if hp == 0:
                        P.wait("tensor", trf)

                    def emit_phi(hp_, pdst, po, rpar, tqk):
                        P.wait("tensor", tqk)
                        for tc in range(NT):
                            bank = 4 + tc % 2
                            mm_bank(bank, [(aB[:, rfb + hp_ * 128: rfb + (hp_ + 1) * 128],
                                            aB[:, qkb[rpar] + tc * TCH: qkb[rpar] + (tc + 1) * TCH],
                                            None)])

                            def _ep(bank=bank, pdst=pdst, po=po, tc=tc):
                                act("scalar", aB[:, pdst + po + tc * TCH: pdst + po + (tc + 1) * TCH],
                                    pb[bank][:, 0:TCH], AF.Relu)
                            evac(bank, "scalar", _ep)
                        qk_guard[rpar] = P.tok("tensor")

                    for part, pdst, po in ((0, phi_q, hp * TC), (1, phi_kg, lp * TC)):
                        ridx = 2 * lp + part
                        rpar = ridx % 2
                        for tc in range(NT):
                            bank = tc % 4
                            mm_bank(bank, [(aB[:, ws + (part * NDC + dc) * 128: ws + (part * NDC + dc) * 128 + 128],
                                            xn[:, dc * TC + tc * TCH: dc * TC + (tc + 1) * TCH], None)
                                           for dc in range(NDC)])

                            def _e(bank=bank, tc=tc, rpar=rpar):
                                if qk_guard[rpar] is not None:
                                    P.wait("vector", qk_guard[rpar])
                                    qk_guard[rpar] = None
                                cp("vector", aB[:, qkb[rpar] + tc * TCH: qkb[rpar] + (tc + 1) * TCH],
                                   pb[bank][:, 0:TCH])
                            evac(bank, "vector", _e)
                        tqk_r = P.tok("vector")
                        if prev is not None:
                            emit_phi(*prev)
                        prev = (hp, pdst, po, rpar, tqk_r)
                    wst_guard[slot] = P.tok("tensor")
                emit_phi(*prev)

                if wv_guard is not None:
                    P.wait("gpsimd", wv_guard)
                dma("gpsimd", aB[:, wvm:wvm + NDC * 512].rearrange("p (dc c) -> p dc c", dc=NDC),
                    io["wqkv"].rearrange("(dc p) c -> p dc c", p=128)
                    [:, :, 2 * D + grp * 512:2 * D + (grp + 1) * 512], st="lwv")
                twv = P.tok("lwv")
                P.wait("tensor", twv)
                tphi = P.tok("scalar")
                P.wait("tensor", tphi)
                for b_ in range(2):
                    if bank_free[b_] is not None:
                        P.wait("tensor", bank_free[b_]); bank_free[b_] = None
                vvb = [vv, wst[1]]
                pkb = [pkT, wst[1] + 512]
                vk_guard = [None, None]
                prevB = None
                tkv = None

                def emit_kv(sub, par, tvv):
                    nonlocal tkv
                    P.wait("tensor", tvv)
                    for lh in range(8):
                        bank = lh // 6
                        c0 = (lh % 6) * 65
                        lpp, ab = lh // 2, lh % 2
                        first = (sub == 0 and lh % 6 == 0)
                        statap = aB[:, pkb[par] + lpp * 128 + ab * 64: pkb[par] + lpp * 128 + ab * 64 + 64]
                        mm(pb[bank][0:64, c0:c0 + 64], statap,
                           aB[:, vvb[par] + lh * 64: vvb[par] + lh * 64 + 64], first, False)
                        mm(pb[bank][0:64, c0 + 64:c0 + 65], statap, ones_b[:, 0:1], False, False)
                    tkv = P.mark("tensor")
                    vk_guard[par] = tkv
                    bank_ready[0] = tkv
                    bank_ready[1] = tkv

                for sub in range(NSUB):
                    t0 = sub * 128
                    par = sub % 2
                    mm_bank(4 + par, [(xn[:, dc * TC + t0: dc * TC + t0 + 128],
                                       aB[:, wvm + dc * 512: wvm + (dc + 1) * 512],
                                       pb[4 + par][:, 0:512]) for dc in range(NDC)])

                    def _ev(par=par):
                        if vk_guard[par] is not None:
                            P.wait("vector", vk_guard[par])
                            vk_guard[par] = None
                        cp("vector", aB[:, vvb[par]:vvb[par] + 512], pb[4 + par][:, 0:512])
                    evac(4 + par, "vector", _ev)
                    if bank_free[7] is not None:
                        P.wait("tensor", bank_free[7]); bank_free[7] = None
                    for lp in range(4):
                        transpose(pT[:, lp * 128:(lp + 1) * 128],
                                  aB[:, phi_kg + lp * TC + t0: phi_kg + lp * TC + t0 + 128],
                                  identb[:, :])
                    bank_ready[7] = P.mark("tensor")

                    def _ek(par=par):
                        cp("vector", aB[:, pkb[par]:pkb[par] + 512], pT[:, 0:512])
                    evac(7, "vector", _ek)
                    tvv = P.tok("vector")
                    if prevB is not None:
                        emit_kv(*prevB)
                    prevB = (sub, par, tvv)
                emit_kv(*prevB)
                wst_guard[1] = tkv
                P.wait("vector", tkv)
                for bank in range(2):
                    nh = 6 if bank == 0 else 2
                    base_h = grp * 8 + bank * 6
                    cp("vector", aF[0:64, kvcols + base_h * 65: kvcols + (base_h + nh) * 65],
                       pb[bank][0:64, 0:nh * 65])
                tkvev = P.mark("vector")
                for bank in range(2):
                    bank_free[bank] = tkvev
                    bank_ready[bank] = None
                wv_guard = P.tok("tensor")

            tkvev = P.tok("vector")
            P.wait("sync", tkvev)
            dma("sync", cc_kv_in[:, :], aF[0:64, kvcols:kvcols + 1040], st="dcc")
            P.wait("gpsimd", P.tok("dcc"))
            P.do("gpsimd", lambda e: e.collective_compute(
                "AllReduce", ALU.add, replica_groups=RGROUPS,
                ins=[cc_kv_in.ap()], outs=[cc_kv_out.ap()]), cc="cck")
            P.wait("gpsimd", P.tok("cck"))
            dma("gpsimd", aF[0:64, kvcols:kvcols + 1040], cc_kv_out[:, :], st="dcc2")
            tkvr = P.tok("dcc2")

            P.wait("vector", tkvr)
            P.do("vector", lambda e: e.memset(aB[:, pairkv:pairkv + NPAIR * 130], 0.0))
            for hp in range(NPAIR):
                o = pairkv + hp * 130
                ha, hb = 2 * hp, 2 * hp + 1
                cp("vector", aB[0:64, o:o + 64], aF[0:64, kvcols + ha * 65: kvcols + ha * 65 + 64])
                cp("vector", aB[64:128, o + 64:o + 128], aF[0:64, kvcols + hb * 65: kvcols + hb * 65 + 64])
                cp("vector", aB[0:64, o + 128:o + 129], aF[0:64, kvcols + ha * 65 + 64: kvcols + ha * 65 + 65])
                cp("vector", aB[64:128, o + 129:o + 130], aF[0:64, kvcols + hb * 65 + 64: kvcols + hb * 65 + 65])
            tpair = P.mark("vector")
            P.wait("tensor", tpair)

            for g0 in range(0, NSUB, 8):
                if bank_free[6] is not None:
                    P.wait("tensor", bank_free[6]); bank_free[6] = None
                gn = min(8, NSUB - g0)
                for si in range(gn):
                    sub = g0 + si
                    for hp in range(NPAIR):
                        mm(pb[6][0:128, si * 16 + hp * 2: si * 16 + hp * 2 + 2],
                           aB[:, phi_q + hp * TC + sub * 128: phi_q + hp * TC + sub * 128 + 128],
                           aB[:, pairkv + hp * 130 + 128: pairkv + hp * 130 + 130],
                           si == 0 and hp == 0, False)
                bank_ready[6] = P.mark("tensor")

                def _e(g0=g0, gn=gn):
                    cp("vector", aF[:, nrmtm_o + g0 * 16:nrmtm_o + (g0 + gn) * 16], pb[6][:, 0:gn * 16])
                evac(6, "vector", _e)
            ts("vector", aF[:, nrmtm_o:nrmtm_o + NSUB * 16], aF[:, nrmtm_o:nrmtm_o + NSUB * 16], 1e-6, None, ALU.add)
            recip("vector", aF[:, nrmtm_o:nrmtm_o + NSUB * 16], aF[:, nrmtm_o:nrmtm_o + NSUB * 16])
            tnr = P.mark("vector")
            P.wait("tensor", tnr)
            for sub in range(NSUB):
                bank = sub % 2
                if bank_free[bank] is not None:
                    P.wait("tensor", bank_free[bank]); bank_free[bank] = None
                transpose(pb[bank][0:16, 0:128], aF[:, nrmtm_o + sub * 16:nrmtm_o + (sub + 1) * 16],
                          identf[:, :])
                bank_ready[bank] = P.mark("tensor")

                def _e(bank=bank, sub=sub):
                    cp("vector", aB[0:16, nrow_o + sub * 128:nrow_o + (sub + 1) * 128],
                       pb[bank][0:16, 0:128])
                evac(bank, "vector", _e)

            atn = xn_o
            nb = arF.al(TCH, "nrm_b")
            tnrow = P.tok("vector")
            P.wait("tensor", tnrow)
            for hp in range(NPAIR):
                for tc in range(NT):
                    bank = (hp * NT + tc) % 2
                    mm_bank(bank, [(aB[:, pairkv + hp * 130: pairkv + hp * 130 + 128],
                                    aB[:, phi_q + hp * TC + tc * TCH: phi_q + hp * TC + (tc + 1) * TCH],
                                    pb[bank][:, 0:TCH])])
                    bank2 = 2 + (hp * NT + tc) % 2
                    mm_bank(bank2, [(selp[:, hp * 128:(hp + 1) * 128],
                                     nrow[:, tc * TCH:(tc + 1) * TCH], pb[bank2][:, 0:TCH])])

                    def _e2(bank2=bank2):
                        cp("vector", aF[:, nb:nb + TCH], pb[bank2][:, 0:TCH])
                    evac(bank2, "vector", _e2)

                    def _e(bank=bank, hp=hp, tc=tc):
                        tt("vector", aB[:, atn + hp * TC + tc * TCH: atn + hp * TC + (tc + 1) * TCH],
                           pb[bank][:, 0:TCH], aF[:, nb:nb + TCH], ALU.mult)
                    evac(bank, "vector", _e)
            tatn = P.tok("vector")
            P.wait("tensor", tatn)
            woguard = [None, None]
            for oc in range(NDC):
                slot = oc % 2
                ws = wst[slot]
                if woguard[slot] is not None:
                    P.wait("gpsimd", woguard[slot])
                P.wait("gpsimd", wst_guard[slot])
                wst_guard[slot] = None
                for dc in range(NDC):
                    dma("gpsimd", aB[:, ws + dc * 128: ws + dc * 128 + 128],
                        io["wout"][dc * 128:(dc + 1) * 128, oc * 128:(oc + 1) * 128], st=f"lw{slot}")
                two = P.tok(f"lw{slot}")
                P.wait("tensor", two)
                for tc in range(NT):
                    bank = 4 + (oc * NT + tc) % 3
                    mm_bank(bank, [(aB[:, ws + dc * 128: ws + dc * 128 + 128],
                                    aB[:, atn + dc * TC + tc * TCH: atn + dc * TC + (tc + 1) * TCH],
                                    pb[bank][:, 0:TCH])
                                   for dc in range(NDC)])

                    def _e(bank=bank, oc=oc, tc=tc):
                        stt("vector", x_sb[:, oc * TC + tc * TCH: oc * TC + (tc + 1) * TCH],
                            pb[bank][:, 0:TCH], cstc("bout", oc),
                            x_sb[:, oc * TC + tc * TCH: oc * TC + (tc + 1) * TCH],
                            ALU.add, ALU.add)
                    evac(bank, "vector", _e)
                woguard[slot] = P.tok("tensor")
            DBG.update(atn=atn, phi_q=phi_q, pairkv=pairkv, kvcols=kvcols)
            arB.pop(); arF.pop()

        def rowmax_bcast(am_off, ncols, sb_off, usb_off, smax, usc):
            nsub = ncols // 128
            arB.push(); arF.push()
            rtm = arB.al(nsub, "rowmax_tm")
            P.wait("tensor", P.tok("vector"))
            for sb_i in range(nsub):
                if bank_free[7] is not None:
                    P.wait("tensor", bank_free[7]); bank_free[7] = None
                transpose(pT[:, 0:128], aB[:, am_off + sb_i * 128: am_off + (sb_i + 1) * 128],
                          identb[:, :])
                bank_ready[7] = P.mark("tensor")

                def _e(sb_i=sb_i):
                    P.do("vector", lambda e, sb_i=sb_i: e.reduce_max(
                        aB[:, rtm + sb_i: rtm + sb_i + 1], pT[:, 0:128], AXX))
                evac(7, "vector", _e)
            srow = arF.al(2 * nsub, "srow_tm")
            ts("vector", aF[:, srow:srow + nsub], aB[:, rtm:rtm + nsub], 1e-5, None, ALU.max)
            recip("vector", aF[:, srow:srow + nsub], aF[:, srow:srow + nsub])
            ts("vector", aF[:, srow:srow + nsub], aF[:, srow:srow + nsub], smax, None, ALU.mult)
            ts("vector", aF[:, srow + nsub:srow + 2 * nsub], aB[:, rtm:rtm + nsub],
               usc, None, ALU.mult)
            tsc_ = P.mark("vector")
            P.wait("tensor", tsc_)
            bank = 6
            if bank_free[bank] is not None:
                P.wait("tensor", bank_free[bank]); bank_free[bank] = None
            transpose(pb[bank][0:2 * nsub, 0:128], aF[:, srow:srow + 2 * nsub], identf_ap[0])
            bank_ready[bank] = P.mark("tensor")
            rows = arF.al(128, "rm_rows")

            def _e3():
                cp("vector", aF[0:2 * nsub, rows:rows + 128], pb[6][0:2 * nsub, 0:128])
            evac(bank, "vector", _e3)
            trow = P.mark("vector")
            P.wait("sync", trow)
            P.wait("sync", P.tok("rbs"))
            dma("sync", rb_scr[0:2 * nsub, :], aF[0:2 * nsub, rows:rows + 128], st="rbs")
            P.wait("sync", P.tok("rbs"))
            src_s = rb_scr[0:nsub, :].unsqueeze(0).broadcast_to((128, nsub, 128))
            src_u = rb_scr[nsub:2 * nsub, :].unsqueeze(0).broadcast_to((128, nsub, 128))
            dma("sync", aF[:, sb_off:sb_off + ncols].rearrange("p (a b) -> p a b", a=nsub),
                src_s, st="rbs")
            dma("sync", aF[:, usb_off:usb_off + ncols].rearrange("p (a b) -> p a b", a=nsub),
                src_u, st="rbs")
            arB.pop(); arF.pop()
            return P.tok("rbs")

        def weight_quant():
            arF.push(); arB.push()
            wbuf = arF.al(2 * 2048, "wq_buf")
            wbB = arB.al(4 * 4096, "wq_bufB")
            accc = arF.al(12, "wq_acc")
            cells = arF.al(8, "wq_cells")
            stg = [arB.al(2048, f"wq_stg{i}") for i in range(6)]
            onef = cst[:, FCBASE + 3:FCBASE + 4]

            def wslot(par):
                if par < 2:
                    return aF[:, wbuf + par * 2048: wbuf + (par + 1) * 2048]
                return aB[:, wbB + (par - 2) * 4096: wbB + (par - 1) * 4096].bitcast(F32)

            NB = 6
            guards = [None] * NB
            sguards = [None] * NB
            mats = [("wg_t", wq_g, NDC, INNER, "swg"), ("wu_t", wq_u, NDC, INNER, "swu"),
                    ("wd_t", wq_d, INNER // 128, D, "swd")]
            for mi, (src, dst, nrow_c, w, sname) in enumerate(mats):
                ncolh = w // 2048 if w > 2048 else 1
                colw = min(w, 2048)
                P.do("vector", lambda e: e.memset(aF[:, accc:accc + 1], 0.0))
                tidx = 0
                for rc in range(nrow_c):
                    for ch in range(ncolh):
                        par = tidx % NB
                        if guards[par] is not None:
                            P.wait("gpsimd", guards[par])
                        if sguards[par % 6] is not None:
                            P.wait("gpsimd", sguards[par % 6])
                        dma("gpsimd", aB[:, stg[par % 6]:stg[par % 6] + colw],
                            io[src][rc * 128:(rc + 1) * 128, ch * 2048: ch * 2048 + colw],
                            st=f"lp{par}")
                        P.wait("vector", P.tok(f"lp{par}"))
                        P.do("vector", lambda e, par=par, colw=colw: e.reduce_sum(
                            aF[:, accc + 1 + par: accc + 2 + par],
                            aB[:, stg[par % 6]:stg[par % 6] + colw], AXX,
                            apply_absolute_value=True))
                        tt("vector", aF[:, accc:accc + 1], aF[:, accc:accc + 1],
                           aF[:, accc + 1 + par: accc + 2 + par], ALU.add)
                        guards[par] = P.mark("vector")
                        tidx += 1
                tacc = P.mark("vector")
                P.wait("tensor", tacc)
                if bank_free[6] is not None:
                    P.wait("tensor", bank_free[6]); bank_free[6] = None
                mm(pb[6][0:1, 0:1], onef, aF[:, accc:accc + 1], True, True)
                bank_ready[6] = P.mark("tensor")

                def _m():
                    cp("vector", dbgc[0:1, 8 + mi:9 + mi], pb[6][0:1, 0:1])
                    ts("vector", aF[0:1, cells:cells + 1], pb[6][0:1, 0:1],
                       1.0 / (nrow_c * 128 * w), None, ALU.mult)
                    ts("vector", aF[0:1, cells:cells + 1], aF[0:1, cells:cells + 1],
                       1e-5, None, ALU.max)
                    recip("vector", aF[0:1, cells + 1:cells + 2], aF[0:1, cells:cells + 1])
                evac(6, "vector", _m)
                tcells = P.mark("vector")
                P.wait("tensor", tcells)
                for (cidx, dst_c) in ((1, 2), (0, 4)):
                    if bank_free[6] is not None:
                        P.wait("tensor", bank_free[6]); bank_free[6] = None
                    mm(pb[6][:, 0:1], rowones_f[:, :],
                       aF[0:1, cells + cidx:cells + cidx + 1], True, True)
                    bank_ready[6] = P.mark("tensor")

                    def _b(dst_c=dst_c):
                        cp("vector", aF[:, cells + dst_c:cells + dst_c + 1], pb[6][:, 0:1])
                    evac(6, "vector", _b)
                CST[sname] = None
                DBG[sname + "_s"] = cells + 2
                DBG[sname + "_m"] = cells + 4
                mw_col = FCBASE - 6 + mi
                cp("vector", cst[:, mw_col:mw_col + 1], aF[:, cells + 4:cells + 5])
                CST["mw" + str(mi)] = mw_col
                tidx = 0
                for rc in range(nrow_c):
                    for ch in range(ncolh):
                        par = tidx % NB
                        if guards[par] is not None:
                            P.wait("sync", guards[par])
                        dma("sync", wslot(par)[:, 0:colw],
                            io[src][rc * 128:(rc + 1) * 128, ch * 2048: ch * 2048 + colw],
                            st=f"lq{par}")
                        P.wait("scalar", P.tok(f"lq{par}"))
                        bo = wslot(par)
                        P.wait("scalar", P.tok("vector"))
                        act("scalar", bo[:, 0:colw], bo[:, 0:colw], AF.Identity,
                            bias=RMAGIC, scale=aF[:, cells + 2:cells + 3])
                        P.wait("vector", P.mark("scalar"))
                        ts("vector", bo[:, 0:colw], bo[:, 0:colw],
                           RMAGIC, 1.0, ALU.subtract, ALU.min)
                        so = stg[par]
                        if sguards[par] is not None:
                            P.wait("vector", sguards[par])
                        P.do("vector", lambda e, bo=bo, colw=colw, so=so: e.tensor_scalar(
                            aB[:, so:so + colw], bo[:, 0:colw],
                            -1.0, None, ALU.max))
                        tq = P.mark("vector")
                        guards[par] = tq
                        P.wait("scalar", tq)
                        dma("scalar", dst[rc * 128:(rc + 1) * 128, ch * 2048: ch * 2048 + colw],
                            aB[:, so:so + colw], st=f"wqs{par}")
                        sguards[par] = P.tok(f"wqs{par}")
                        tidx += 1
            tqs = tuple(P.tok(f"wqs{i}") for i in range(6))
            arF.pop(); arB.pop()
            return tqs

        def l0_ffn(tq_scratch):
            arB.push(); arF.push()
            xn_o = arB.al(NDC * TC, "xn2")
            xn = ColView(aB, xn_o)
            layer_norm(xn)
            P.wait("tensor", P.tok("vector"))
            NPASS = 4 if TC >= 2048 else 2
            TP = TC // NPASS
            NTP = max(1, TP // 512)
            TPH = TP // NTP
            sxb = arF.al(TC, "sx_b")
            usxb = arF.al(TC, "usx_b")
            tmpf = arF.al(TC, "ffn_tmp")
            arB.push()
            am = arB.al(TC, "am")
            P.wait("scalar", P.tok("vector"))
            abst = arB.al(TC, "abs_tmp")
            for dc in range(NDC):
                act("scalar", aB[:, abst:abst + TC], xn[:, dc * TC:(dc + 1) * TC], AF.Abs)
                tabs = P.mark("scalar")
                P.wait("vector", tabs)
                if dc == 0:
                    cp("vector", aB[:, am:am + TC], aB[:, abst:abst + TC])
                else:
                    tt("vector", aB[:, am:am + TC], aB[:, am:am + TC],
                       aB[:, abst:abst + TC], ALU.max)
                tmx = P.mark("vector")
                P.wait("scalar", tmx)
            trb = rowmax_bcast(am, TC, sxb, usxb, 127.0, 1.0 / 127.0)
            P.wait("vector", trb)
            for dc in range(NDC):
                sl = slice(dc * TC, (dc + 1) * TC)
                tt("vector", aF[:, tmpf:tmpf + TC], xn[:, sl], aF[:, sxb:sxb + TC], ALU.mult)
                ts("vector", aF[:, tmpf:tmpf + TC], aF[:, tmpf:tmpf + TC],
                   RMAGIC, RMAGIC, ALU.add, ALU.subtract)
                tt("vector", xn[:, sl], aF[:, tmpf:tmpf + TC], aF[:, usxb:usxb + TC], ALU.mult)
            txdq = P.mark("vector")
            P.wait("tensor", txdq)
            for t_ in tq_scratch:
                P.wait("gpsimd", t_)
            arB.pop()
            if os.environ.get("KFFN", "") == "1":
                arB.pop(); arF.pop(); return

            mid = arB.al(NIC * TP, "mid")
            wsl_g = [arB.al(4096, "wqg0"), arB.al(4096, "wqg1")]
            wsl_u = [arB.al(4096, "wqu0"), arB.al(4096, "wqu1")]
            wsl_d = [arB.al(2048, "wqd0"), arB.al(2048, "wqd1")]
            amM = arB.al(TP, "amM")
            absM = arB.al(TP, "absM")
            smb = arF.al(TP, "sm_b")
            usmb = arF.al(TP, "usm_b")
            gu_guard = [None, None]

            def gu_load(grp, par):
                if gu_guard[par] is not None:
                    P.wait("gpsimd", gu_guard[par])
                    gu_guard[par] = None
                dma("gpsimd", aB[:, wsl_g[par]:wsl_g[par] + 4096]
                    .rearrange("p (dc c) -> p dc c", dc=NDC),
                    wq_g.rearrange("(dc p) c -> p dc c", p=128)[:, :, grp * 512:(grp + 1) * 512],
                    st=f"lg{par}")
                dma("gpsimd", aB[:, wsl_u[par]:wsl_u[par] + 4096]
                    .rearrange("p (dc c) -> p dc c", dc=NDC),
                    wq_u.rearrange("(dc p) c -> p dc c", p=128)[:, :, grp * 512:(grp + 1) * 512],
                    st=f"lu{par}")
                return (P.tok(f"lg{par}"), P.tok(f"lu{par}"))

            d_guard = [None, None]

            def d_load(och, grp, par):
                if d_guard[par] is not None:
                    P.wait("gpsimd", d_guard[par])
                    d_guard[par] = None
                dma("gpsimd", aB[:, wsl_d[par]:wsl_d[par] + 2048]
                    .rearrange("p (ic c) -> p ic c", ic=4),
                    wq_d.rearrange("(ic p) c -> p ic c", p=128)
                    [:, grp * 4:(grp + 1) * 4, och * 512:(och + 1) * 512], st=f"ldd{par}")
                return P.tok(f"ldd{par}")

            tdown_prev = None
            gu_t = {(0, 0): gu_load(0, 0), (0, 1): gu_load(1, 1)}
            for p in range(NPASS):
                t0 = p * TP
                if tdown_prev is not None:
                    P.wait("scalar", tdown_prev)
                    P.wait("vector", tdown_prev)
                for grp in range(8):
                    par = grp % 2
                    tg_, tu_ = gu_t[(p, grp)]
                    P.wait("tensor", tg_); P.wait("tensor", tu_)
                    for ici in range(4):
                        ic = grp * 4 + ici
                        for tch in range(NTP):
                            bg = (2 * (ici * NTP + tch)) % 4
                            bu = bg + 1
                            mm_bank(bg, [(aB[:, wsl_g[par] + dc * 512 + ici * 128:
                                                wsl_g[par] + dc * 512 + ici * 128 + 128],
                                          xn[:, dc * TC + t0 + tch * TPH: dc * TC + t0 + (tch + 1) * TPH],
                                          pb[bg][:, 0:TPH]) for dc in range(NDC)])

                            def _eg(bg=bg, ic=ic, tch=tch):
                                act("scalar", aB[:, mid + ic * TP + tch * TPH: mid + ic * TP + (tch + 1) * TPH],
                                    pb[bg][:, 0:TPH], AF.Silu, scale=cst[:, CST["mw0"]:CST["mw0"] + 1])
                            evac(bg, "scalar", _eg)
                            mm_bank(bu, [(aB[:, wsl_u[par] + dc * 512 + ici * 128:
                                                wsl_u[par] + dc * 512 + ici * 128 + 128],
                                          xn[:, dc * TC + t0 + tch * TPH: dc * TC + t0 + (tch + 1) * TPH],
                                          pb[bu][:, 0:TPH]) for dc in range(NDC)])

                            def _eu(bu=bu, ic=ic, tch=tch, p=p):
                                P.wait("vector", P.tok("scalar"))
                                o = mid + ic * TP + tch * TPH
                                P.do("vector", lambda e, bu=bu, o=o: e.scalar_tensor_tensor(
                                    aB[:, o:o + TPH], pb[bu][:, 0:TPH],
                                    cst[:, CST["mw1"]:CST["mw1"] + 1],
                                    aB[:, o:o + TPH], ALU.mult, ALU.mult))
                                tmid = P.mark("vector")
                                P.wait("scalar", tmid)
                                act("scalar", aB[:, absM + tch * TPH: absM + (tch + 1) * TPH],
                                    aB[:, o:o + TPH], AF.Abs)
                                tam = P.mark("scalar")
                                P.wait("vector", tam)
                                if ic == 0:
                                    cp("vector", aB[:, amM + tch * TPH: amM + (tch + 1) * TPH],
                                       aB[:, absM + tch * TPH: absM + (tch + 1) * TPH])
                                else:
                                    P.do("vector", lambda e, tch=tch: e.tensor_tensor(
                                        aB[:, amM + tch * TPH: amM + (tch + 1) * TPH],
                                        aB[:, amM + tch * TPH: amM + (tch + 1) * TPH],
                                        aB[:, absM + tch * TPH: absM + (tch + 1) * TPH], ALU.max))
                            evac(bu, "vector", _eu)
                    gu_guard[par] = P.tok("tensor")
                    if grp + 2 < 8:
                        gu_t[(p, grp + 2)] = gu_load(grp + 2, par)
                    elif p + 1 < NPASS:
                        gu_t[(p + 1, grp + 2 - 8)] = gu_load(grp + 2 - 8, par)
                if os.environ.get("KFFN", "") == "2":
                    continue
                trb2 = rowmax_bcast(amM, TP, smb, usmb, 127.0, 1.0 / 127.0)
                P.wait("vector", trb2)
                rt = {}
                for ic in range(NIC):
                    o = mid + ic * TP
                    tt("vector", aF[:, tmpf:tmpf + TP], aB[:, o:o + TP],
                       aF[:, smb:smb + TP], ALU.mult)
                    P.do("vector", lambda e, o=o: e.tensor_scalar(
                        aB[:, o:o + TP], aF[:, tmpf:tmpf + TP], RMAGIC, RMAGIC,
                        ALU.add, ALU.subtract))
                    rt[ic] = P.mark("vector")
                tmq = P.mark("vector")
                if os.environ.get("KFFN", "") == "3":
                    continue
                d_t = {(0, 0): d_load(0, 0, 0), (0, 1): d_load(0, 1, 1)}
                for och in range(2):
                    for b_ in range(4, 8):
                        if bank_free[b_] is not None:
                            P.wait("tensor", bank_free[b_]); bank_free[b_] = None
                    for grp in range(8):
                        par = grp % 2
                        P.wait("tensor", rt[grp * 4 + 3])
                        P.wait("tensor", d_t[(och, grp)])
                        for ici in range(4):
                            ic = grp * 4 + ici
                            for tch in range(NTP):
                                for j in range(4):
                                    bank = 4 + j
                                    mm(pb[bank][:, 0:TPH],
                                       aB[:, wsl_d[par] + ici * 512 + j * 128: wsl_d[par] + ici * 512 + j * 128 + 128],
                                       aB[:, mid + ic * TP + tch * TPH: mid + ic * TP + (tch + 1) * TPH],
                                       ic == 0, ic == NIC - 1)
                        d_guard[par] = P.tok("tensor")
                        if grp + 2 < 8:
                            d_t[(och, grp + 2)] = d_load(och, grp + 2, par)
                        elif och == 0:
                            d_t[(1, grp + 2 - 8)] = d_load(1, grp + 2 - 8, par)
                    tdown = P.mark("tensor")
                    for tch in range(NTP):
                        for j in range(4):
                            oc = och * 4 + j
                            bank = 4 + j
                            bank_ready[bank] = tdown
                            xsl = slice(oc * TC + t0 + tch * TPH, oc * TC + t0 + (tch + 1) * TPH)

                            def _ed(bank=bank, xsl=xsl, tch=tch):
                                P.do("vector", lambda e, bank=bank, tch=tch: e.tensor_tensor(
                                    aF[:, tmpf:tmpf + TPH], pb[bank][:, 0:TPH],
                                    aF[:, usmb + tch * TPH: usmb + (tch + 1) * TPH], ALU.mult))
                                P.do("vector", lambda e, xsl=xsl: e.scalar_tensor_tensor(
                                    x_sb[:, xsl], aF[:, tmpf:tmpf + TPH],
                                    cst[:, CST["mw2"]:CST["mw2"] + 1],
                                    x_sb[:, xsl], ALU.mult, ALU.add))
                            evac(bank, "vector", _ed)
                tdown_prev = P.tok("tensor")
            DBG.update(ffn_mid=mid, ffn_amM=amM, ffn_smb=smb, ffn_usmb=usmb, ffn_sxb=sxb, ffn_usxb=usxb)
            arB.pop(); arF.pop()

        def l1_attn():
            arB.push(); arF.push()
            QT = TC // 2
            xn_o = arB.al(NDC * TC, "xn3")
            xn = ColView(aB, xn_o)
            layer_norm(xn, inner_gb=True)
            P.wait("tensor", P.tok("vector"))
            u_o = arB.al(NDC * TC, "u")
            v_o = arB.al(NDC * TC, "v")
            arB.push()
            wstL = [arB.al(NDC * 128, "wL0"), arB.al(NDC * 128, "wL1")]
            raw_o = wstL[0]
            stg_o = arB.al(2 * TCH, "l1stage")
            bstg_o = arB.al(TCH, "bcstage")
            edgb = arB.al(32, "edges_b")
            est = arF.al(16, "edge_stage")
            edges = arF.al(32, "edges")
            x1row = arF.al(TC + 2, "x1row")
            dtc = arF.al(TCH, "dtc")
            halo = arF.al(4, "halo")

            for dc in range(NDC):
                cp("vector", aF[:, est + dc * 2: est + dc * 2 + 1], xn[:, dc * TC: dc * TC + 1])
                cp("vector", aF[:, est + dc * 2 + 1: est + dc * 2 + 2],
                   xn[:, dc * TC + TC - 1: dc * TC + TC])
            P.wait("sync", P.mark("vector"))
            dma("sync", cc_edge_in[:, :], aF[:, est:est + 16], st="deg")
            P.wait("gpsimd", P.tok("deg"))
            P.do("gpsimd", lambda e: e.collective_compute(
                "AllGather", ALU.bypass, replica_groups=RGROUPS,
                ins=[cc_edge_in.ap()], outs=[cc_edge_out.ap()]), cc="cce")
            P.wait("gpsimd", P.tok("cce"))
            dma("gpsimd", aF[:, edges:edges + 16], cc_edge_out[0:128, :], st="deg2")
            dma("gpsimd", aF[:, edges + 16:edges + 32], cc_edge_out[128:256, :], st="deg2")
            P.wait("vector", P.tok("deg2"))
            cp("vector", aB[:, edgb:edgb + 32], aF[:, edges:edges + 32])
            tedgb = P.mark("vector")

            wguard = [None, None]

            def wload(name, oc):
                slot = oc % 2
                ws = wstL[slot]
                if wguard[slot] is not None:
                    P.wait("gpsimd", wguard[slot])
                dma("gpsimd", aB[:, ws:ws + NDC * 128].rearrange("p (dc c) -> p dc c", dc=NDC),
                    io[name].rearrange("(dc p) c -> p dc c", p=128)[:, :, oc * 128:(oc + 1) * 128],
                    st=f"lwa{slot}")
                P.wait("tensor", P.tok(f"lwa{slot}"))
                return slot, ws

            for oc in range(NDC):
                slot, ws = wload("w1", oc)
                if oc == 0:
                    P.wait("tensor", tedgb)
                mm_bank(5, [(aB[:, ws + dc * 128: ws + dc * 128 + 128],
                             aB[:, edgb + 2 * dc + 1: edgb + 2 * dc + 2],
                             pb[5][:, 0:1]) for dc in range(NDC)] +
                           [(aB[:, ws + dc * 128: ws + dc * 128 + 128],
                             aB[:, edgb + 16 + 2 * dc: edgb + 16 + 2 * dc + 1],
                             pb[5][:, 1:2]) for dc in range(NDC)])

                def _eh(oc=oc):
                    ts("vector", aF[:, halo:halo + 2], pb[5][:, 0:2], cstc("b1", oc),
                       None, ALU.add)
                    tt("vector", aF[:, halo:halo + 2], aF[:, halo:halo + 2],
                       cst[:, CST["meta"]:CST["meta"] + 2], ALU.mult)
                    cp("vector", aF[:, x1row:x1row + 1], aF[:, halo:halo + 1])
                    cp("vector", aF[:, x1row + TC + 1:x1row + TC + 2],
                       aF[:, halo + 1:halo + 2])
                evac(5, "vector", _eh)
                for tc in range(NT):
                    bank = tc % 4
                    mm_bank(bank, [(aB[:, ws + dc * 128: ws + dc * 128 + 128],
                                    xn[:, dc * TC + tc * TCH: dc * TC + (tc + 1) * TCH],
                                    pb[bank][:, 0:TCH]) for dc in range(NDC)])

                    def _ex(bank=bank, oc=oc, tc=tc):
                        act("scalar", aF[:, x1row + 1 + tc * TCH: x1row + 1 + (tc + 1) * TCH],
                            pb[bank][:, 0:TCH], AF.Identity, bias=cstc("b1", oc))
                    evac(bank, "scalar", _ex)
                P.wait("vector", P.tok("scalar"))
                thalo = P.tok("vector")
                for tc in range(NT):
                    lo = x1row + tc * TCH
                    P.wait("vector", P.tok("scalar"))
                    ts("vector", aF[:, dtc:dtc + TCH], aF[:, lo:lo + TCH],
                       cstc("convw", oc * 3 + 0), None, ALU.mult)
                    stt("vector", aF[:, dtc:dtc + TCH], aF[:, lo + 1:lo + 1 + TCH],
                        cstc("convw", oc * 3 + 1), aF[:, dtc:dtc + TCH], ALU.mult, ALU.add)
                    stt("vector", aF[:, dtc:dtc + TCH], aF[:, lo + 2:lo + 2 + TCH],
                        cstc("convw", oc * 3 + 2), aF[:, dtc:dtc + TCH], ALU.mult, ALU.add)
                    P.wait("scalar", P.mark("vector"))
                    act("scalar", aB[:, u_o + oc * TC + tc * TCH: u_o + oc * TC + (tc + 1) * TCH],
                        aF[:, dtc:dtc + TCH], AF.Silu)
                P.wait("scalar", P.tok("vector"))
                wguard[slot] = P.tok("tensor")

            P.wait("tensor", P.tok("scalar"))
            for oc in range(NDC):
                slot, ws = wload("wdt", oc)
                for tc in range(NT):
                    bank = tc % 4
                    mm_bank(bank, [(aB[:, ws + dc * 128: ws + dc * 128 + 128],
                                    aB[:, u_o + dc * TC + tc * TCH: u_o + dc * TC + (tc + 1) * TCH],
                                    pb[bank][:, 0:TCH]) for dc in range(NDC)])

                    def _ed(bank=bank, oc=oc, tc=tc):
                        P.wait("scalar", P.tok("vector"))
                        act("scalar", aF[:, dtc:dtc + TCH], pb[bank][:, 0:TCH],
                            AF.Exp, bias=cstc("bdt", oc))
                        act("scalar", aF[:, dtc:dtc + TCH], aF[:, dtc:dtc + TCH],
                            AF.Ln, bias=1.0)
                        P.wait("vector", P.mark("scalar"))
                        P.wait("vector", P.tok("sdt0"))
                        P.wait("vector", P.tok("sdt1"))
                        cp("vector", aB[:, stg_o:stg_o + TCH], aF[:, dtc:dtc + TCH])
                        tt("vector", aB[:, stg_o + TCH:stg_o + 2 * TCH], aF[:, dtc:dtc + TCH],
                           aB[:, u_o + oc * TC + tc * TCH: u_o + oc * TC + (tc + 1) * TCH],
                           ALU.mult)
                        P.wait("sync", P.mark("vector"))
                        tmid = TC // 2
                        ro = oc * 128
                        g0, g1 = tc * TCH, (tc + 1) * TCH
                        for (lo_, hi_, cc_h) in ((g0, min(g1, tmid), cc_dtA_in),
                                                 (max(g0, tmid), g1, cc_dtB_in)):
                            if lo_ >= hi_:
                                continue
                            co = lo_ - (0 if cc_h is cc_dtA_in else tmid)
                            so_ = lo_ - g0
                            dma("sync", cc_h[0, ro:ro + 128, co:co + (hi_ - lo_)],
                                aB[:, stg_o + so_:stg_o + so_ + (hi_ - lo_)], st="sdt0")
                            dma("sync", cc_h[1, ro:ro + 128, co:co + (hi_ - lo_)],
                                aB[:, stg_o + TCH + so_:stg_o + TCH + so_ + (hi_ - lo_)],
                                st="sdt1")
                    evac(bank, "scalar", _ed)
                wguard[slot] = P.tok("tensor")

            for bci, wname in ((0, "wb"), (1, "wc")):
                if wguard[0] is not None:
                    P.wait("gpsimd", wguard[0]); wguard[0] = None
                dma("gpsimd", aB[:, wstL[0]:wstL[0] + NDC * 16]
                    .rearrange("p (dc c) -> p dc c", dc=NDC),
                    io[wname].rearrange("(dc p) c -> p dc c", p=128), st="lwa0")
                P.wait("tensor", P.tok("lwa0"))
                for tc in range(NT):
                    bank = 4 + tc % 2
                    mm_bank(bank, [(aB[:, wstL[0] + dc * 16: wstL[0] + dc * 16 + 16],
                                    aB[:, u_o + dc * TC + tc * TCH: u_o + dc * TC + (tc + 1) * TCH],
                                    pb[bank][0:16, 0:TCH]) for dc in range(NDC)])

                    def _eb(bank=bank, bci=bci, tc=tc):
                        P.wait("vector", P.tok("sbc"))
                        cp("vector", aB[0:16, bstg_o:bstg_o + TCH], pb[bank][0:16, 0:TCH])
                        P.wait("sync", P.mark("vector"))
                        dma("sync", cc_bc_in[bci, :, tc * TCH:(tc + 1) * TCH],
                            aB[0:16, bstg_o:bstg_o + TCH], st="sbc")
                    evac(bank, "vector", _eb)
                wguard[0] = P.tok("tensor")

            P.wait("gpsimd", P.tok("sdt0"))
            P.wait("gpsimd", P.tok("sdt1"))
            P.wait("gpsimd", P.tok("sbc"))
            P.do("gpsimd", lambda e: e.collective_compute(
                "AllGather", ALU.bypass, replica_groups=RGROUPS,
                ins=[cc_bc_in.ap()], outs=[cc_bc_out.ap()]), cc="ccg")
            tgathBC = P.tok("ccg")
            P.do("gpsimd", lambda e: e.collective_compute(
                "AllGather", ALU.bypass, replica_groups=RGROUPS,
                ins=[cc_dtA_in.ap()], outs=[cc_dtA_out.ap()]), cc="ccg")
            tgathA = P.tok("ccg")
            P.do("gpsimd", lambda e: e.collective_compute(
                "AllGather", ALU.bypass, replica_groups=RGROUPS,
                ins=[cc_dtB_in.ap()], outs=[cc_dtB_out.ap()]), cc="ccg")
            tgath = P.tok("ccg")

            for oc in range(NDC):
                slot, ws = wload("v1", oc)
                for tc in range(NT):
                    bank = tc % 4
                    mm_bank(bank, [(aB[:, ws + dc * 128: ws + dc * 128 + 128],
                                    xn[:, dc * TC + tc * TCH: dc * TC + (tc + 1) * TCH],
                                    pb[bank][:, 0:TCH]) for dc in range(NDC)])

                    def _ev(bank=bank, oc=oc, tc=tc):
                        act("scalar", aB[:, v_o + oc * TC + tc * TCH: v_o + oc * TC + (tc + 1) * TCH],
                            pb[bank][:, 0:TCH], AF.Silu, bias=cstc("bv", oc))
                    evac(bank, "scalar", _ev)
                wguard[slot] = P.tok("tensor")



            P.wait("gpsimd", tgath)
            dt_o = xn_o
            dtu_o = xn_o + 4 * QT
            y_o = xn_o + 8 * QT
            bcB_o = xn_o + 12 * QT
            bcC_o = xn_o + 13 * QT
            inp_o = xn_o + 14 * QT
            h0_o = xn_o + 15 * QT
            arF.push()
            dAs = [arF.al(QT, "dA0"), arF.al(QT, "dA1")]
            car_o = arF.al(64, "carry")
            bc2f = arF.al(QT, "bc2")
            tm2f = arF.al(QT, "tmp2")
            h2f = arF.al(QT // 2, "h2")
            in2f = arF.al(QT // 2, "inp2")
            aBC2 = aF[:, bc2f:bc2f + QT].bitcast(BF16)
            aTM2 = aF[:, tm2f:tm2f + QT].bitcast(BF16)
            inpS = [aB[:, inp_o:inp_o + QT], aF[:, in2f:in2f + QT // 2].bitcast(BF16)]
            bcB = [aB[:, bcB_o:bcB_o + QT], aBC2[:, 0:QT]]
            bcC = [aB[:, bcC_o:bcC_o + QT], aBC2[:, QT:2 * QT]]
            bcBC = [aB[:, bcB_o:bcB_o + 2 * QT].rearrange("p (b t) -> p b t", b=2),
                    aBC2[:, 0:2 * QT].rearrange("p (b t) -> p b t", b=2)]
            tmp = [aTM2[:, 0:QT], aTM2[:, QT:2 * QT]]
            hb = [aB[:, h0_o:h0_o + QT], aF[:, h2f:h2f + QT // 2].bitcast(BF16)]
            m_lo = cstc("meta", 1)
            m_hi = cstc("meta", 0)
            bc_guard = [None, None]

            def bc_load(q_, n_):
                par = n_ % 2
                s_, hf_ = q_ // 2, q_ % 2
                if bc_guard[par] is not None:
                    P.wait("gpsimd", bc_guard[par])
                    bc_guard[par] = None
                dma("gpsimd", bcBC[par],
                    cc_bc_out[s_, :, n_, hf_ * QT:(hf_ + 1) * QT]
                    .unsqueeze(0).broadcast_to((128, 2, QT)), st=f"lbc{par}")
                return P.tok(f"lbc{par}")

            P.wait("gpsimd", P.tok("scalar"))
            P.wait("gpsimd", tgathBC)
            tbc_next = bc_load(0, 0)
            P.wait("sync", P.tok("scalar"))
            for q in range(4):
                s, hf = q // 2, q % 2
                tsrc = cc_dtA_out if hf == 0 else cc_dtB_out
                tgq = tgathA if hf == 0 else tgath
                P.wait("gpsimd", tgq)
                P.wait("gpsimd", P.tok("scalar"))
                P.wait("gpsimd", P.tok("vector"))
                for mi, dsto in ((0, dt_o), (1, dtu_o)):
                    dma("gpsimd", aB[:, dsto:dsto + 4 * QT]
                        .rearrange("p (dcl t) -> p dcl t", dcl=4),
                        tsrc[s, mi].rearrange("(dcl p) t -> p dcl t", p=128)[:, 0:4, 0:QT],
                        st="ldt")
                t_lo = P.tok("ldt")
                P.wait("sync", tgq)
                P.wait("sync", P.tok("scalar"))
                P.wait("sync", P.tok("vector"))
                blend_t = {}
                idx = 0
                for mi, dsto in ((0, dt_o), (1, dtu_o)):
                    for dcl in range(4):
                        par = idx % 2
                        if idx >= 2:
                            P.wait("sync", blend_t[idx - 2])
                        dma("sync", aB[:, raw_o + par * QT: raw_o + (par + 1) * QT],
                            tsrc[s, mi, 512 + dcl * 128: 512 + (dcl + 1) * 128, 0:QT],
                            st=f"ldh{par}")
                        thi = P.tok(f"ldh{par}")
                        d_ = aB[:, dsto + dcl * QT: dsto + (dcl + 1) * QT]
                        P.wait("vector", t_lo)
                        P.wait("vector", thi)
                        ts("vector", d_, d_, m_lo, None, ALU.mult)
                        stt("vector", d_, aB[:, raw_o + par * QT: raw_o + (par + 1) * QT],
                            m_hi, d_, ALU.mult, ALU.add)
                        blend_t[idx] = P.mark("vector")
                        idx += 1
                P.wait("vector", P.tok("sy"))
                P.wait("vector", P.tok("gpsimd"))
                P.wait("scalar", P.tok("vector"))

                def exp_issue(g):
                    n_, dcl_ = g // 4, g % 4
                    act("scalar", aF[:, dAs[g % 2]:dAs[g % 2] + QT],
                        aB[:, dt_o + dcl_ * QT: dt_o + (dcl_ + 1) * QT],
                        AF.Exp, scale=cstc("alog", dcl_ * NST + n_))
                    return P.mark("scalar")

                texp = exp_issue(0)
                gp_tok = {}
                sc_tok = {}
                for n in range(NST):
                    if n + 1 < NST:
                        tbc_cur, tbc_next = tbc_next, bc_load(q, n + 1)
                    elif q + 1 < 4:
                        tbc_cur, tbc_next = tbc_next, bc_load(q + 1, 0)
                    else:
                        tbc_cur = tbc_next
                    P.wait("vector", tbc_cur)
                    par = n % 2
                    for dcl in range(4):
                        g = n * 4 + dcl
                        texp_cur = texp
                        if g + 1 < 4 * NST:
                            P.wait("scalar", P.tok("vector"))
                            texp = exp_issue(g + 1)
                        hB = hb[g % 2]
                        tmB = tmp[g % 2]
                        P.wait("vector", texp_cur)
                        if g - 2 in gp_tok:
                            P.wait("vector", gp_tok[g - 2])
                        if g - 2 in sc_tok:
                            P.wait("vector", sc_tok[g - 2])
                        tt("vector", aB[:, inp_o:inp_o + QT],
                           aB[:, dtu_o + dcl * QT: dtu_o + (dcl + 1) * QT],
                           bcB[par], ALU.mult)
                        cini = 0.0 if q == 0 else aF[:, car_o + dcl * 16 + n: car_o + dcl * 16 + n + 1]
                        P.do("vector", lambda e, ci=cini, hB=hB, g=g: e.tensor_tensor_scan(
                            hB, aF[:, dAs[g % 2]:dAs[g % 2] + QT],
                            aB[:, inp_o:inp_o + QT], ci, ALU.mult, ALU.add))
                        tv1 = P.mark("vector")
                        P.wait("scalar", tv1)
                        act("scalar", aF[:, car_o + dcl * 16 + n: car_o + dcl * 16 + n + 1],
                            hB[:, QT - 1:QT], AF.Identity)
                        sc_tok[g] = P.mark("scalar")
                        if n == 0:
                            tt("vector", aB[:, y_o + dcl * QT: y_o + (dcl + 1) * QT],
                               hB, bcC[par], ALU.mult)
                        elif g % 2 == 0:
                            tt("vector", tmB, hB, bcC[par], ALU.mult)
                            tv2 = P.mark("vector")
                            P.wait("gpsimd", tv2)
                            tt("gpsimd", aB[:, y_o + dcl * QT: y_o + (dcl + 1) * QT],
                               aB[:, y_o + dcl * QT: y_o + (dcl + 1) * QT],
                               tmB, ALU.add)
                            gp_tok[g] = P.mark("gpsimd")
                        else:
                            tt("vector", tmB, hB, bcC[par], ALU.mult)
                            P.wait("vector", P.tok("gpsimd"))
                            tt("vector", aB[:, y_o + dcl * QT: y_o + (dcl + 1) * QT],
                               aB[:, y_o + dcl * QT: y_o + (dcl + 1) * QT],
                               tmB, ALU.add)
                            gp_tok[g] = P.mark("vector")
                    bc_guard[par] = P.tok("vector")
                P.wait("sync", P.mark("vector"))
                P.wait("sync", P.tok("gpsimd"))
                ydst = cc_yA_in if q < 2 else cc_yB_in
                for dcl in range(4):
                    dma("sync", ydst[dcl * 128:(dcl + 1) * 128, hf * QT:(hf + 1) * QT],
                        aB[:, y_o + dcl * QT: y_o + (dcl + 1) * QT], st="sy")
                if q == 1:
                    P.wait("gpsimd", P.tok("sy"))
                    P.do("gpsimd", lambda e: e.collective_compute(
                        "AllGather", ALU.bypass, replica_groups=RGROUPS,
                        ins=[cc_yA_in.ap()], outs=[cc_yA_out.ap()]), cc="ccy")

            arF.pop()
            P.wait("gpsimd", P.tok("sy"))
            P.do("gpsimd", lambda e: e.collective_compute(
                "AllGather", ALU.bypass, replica_groups=RGROUPS,
                ins=[cc_yB_in.ap()], outs=[cc_yB_out.ap()]), cc="ccy")
            P.wait("gpsimd", P.tok("ccy"))
            P.wait("sync", P.tok("ccy"))
            P.wait("gpsimd", P.tok("vector"))
            P.wait("gpsimd", P.tok("scalar"))
            P.wait("vector", P.tok("scalar"))
            arB.pop()
            xs_o = xn_o
            ya_o = [arB.al(TC, "yA_t0"), arB.al(TC, "yA_t1")]
            yb_o = [arF.al(TC, "yB_t0"), arF.al(TC, "yB_t1")]
            yg = [None, None]

            def yr_load(dc):
                par = dc % 2
                if yg[par] is not None:
                    P.wait("gpsimd", yg[par])
                    P.wait("sync", yg[par])
                dma("sync", aB[:, ya_o[par]:ya_o[par] + TC],
                    cc_yA_out[dc * 128:(dc + 1) * 128, :], st=f"lya{par}")
                dma("gpsimd", aF[:, yb_o[par]:yb_o[par] + TC],
                    cc_yB_out[dc * 128:(dc + 1) * 128, :], st=f"lyr{par}")
                return (P.tok(f"lya{par}"), P.tok(f"lyr{par}"))

            yr_t = {0: yr_load(0), 1: yr_load(1)}
            for dc in range(NDC):
                par = dc % 2
                P.wait("vector", yr_t[dc][0])
                P.wait("vector", yr_t[dc][1])
                xs = aB[:, xs_o + dc * TC: xs_o + (dc + 1) * TC]
                ts("vector", xs, aB[:, ya_o[par]:ya_o[par] + TC], m_lo, None, ALU.mult)
                stt("vector", xs, aF[:, yb_o[par]:yb_o[par] + TC], m_hi, xs, ALU.mult, ALU.add)
                stt("vector", xs, aB[:, u_o + dc * TC: u_o + (dc + 1) * TC],
                    cstc("dskip", dc), xs, ALU.mult, ALU.add)
                tt("vector", xs, xs, aB[:, v_o + dc * TC: v_o + (dc + 1) * TC], ALU.mult)
                yg[par] = P.tok("vector")
                if dc + 2 < NDC:
                    yr_t[dc + 2] = yr_load(dc + 2)
            P.wait("tensor", P.mark("vector"))

            wstL2 = [arB.al(NDC * 128, "w2a"), arB.al(NDC * 128, "w2b")]
            wstL[0], wstL[1] = wstL2[0], wstL2[1]
            for oc in range(NDC):
                slot, ws = wload("w2", oc)
                for tc in range(NT):
                    bank = tc % 4
                    mm_bank(bank, [(aB[:, ws + dc * 128: ws + dc * 128 + 128],
                                    aB[:, xs_o + dc * TC + tc * TCH: xs_o + dc * TC + (tc + 1) * TCH],
                                    pb[bank][:, 0:TCH]) for dc in range(NDC)])

                    def _eo(bank=bank, oc=oc, tc=tc):
                        stt("vector", x_sb[:, oc * TC + tc * TCH: oc * TC + (tc + 1) * TCH],
                            pb[bank][:, 0:TCH], cstc("b2", oc),
                            x_sb[:, oc * TC + tc * TCH: oc * TC + (tc + 1) * TCH],
                            ALU.add, ALU.add)
                    evac(bank, "vector", _eo)
                wguard[slot] = P.tok("tensor")
            DBG.update(l1_xn=xn_o, l1_u=u_o, l1_v=v_o, l1_xs=xs_o)
            arB.pop(); arF.pop()

        def l1_ffn():
            arB.push(); arF.push()
            NG = 4
            PG = NEXP // NG
            xn_o = arB.al(NDC * TC, "xn4")
            xn = ColView(aB, xn_o)
            layer_norm(xn)
            P.wait("tensor", P.tok("vector"))
            wrow_o = arB.al(TC, "Wrow")
            wsl = [arB.al(4096, f"moew{i}") for i in range(4)]
            it_o = arB.al(4 * TC, "inter")
            wbc = [arB.al(TC, "wbc0"), arB.al(TC, "wbc1")]
            tmpb_o = arB.al(TCH, "silu_t")
            wtmb_o = arB.al(16, "W_tmb")
            scr = arF.al(TC, "scores")
            ebr = arF.al(16, "ebias_row")
            g0 = arF.al(96, "gate_scratch")
            c_sc, c_sfc, c_tmp, c_rnd = g0, g0 + 16, g0 + 32, g0 + 48
            c_ab, c_cd, c_mix, c_gsc = g0 + 64, g0 + 68, g0 + 72, g0 + 76
            c_m1, c_n1, c_m2, c_n2 = g0 + 80, g0 + 81, g0 + 82, g0 + 83
            c_t1, c_t2, c_thrg, c_thre = g0 + 84, g0 + 85, g0 + 86, g0 + 87
            c_mx, c_ws, c_selg = g0 + 88, g0 + 89, g0 + 92

            for q_ in ("sync", "gpsimd"):
                P.wait(q_, P.tok("scalar"))
                P.wait(q_, P.tok("vector"))
            dma("sync", aF[:, ebr:ebr + 16],
                io["ebias"][0:1, :].broadcast_to((128, NEXP)), st="lgw")
            dma("gpsimd", aB[:, wsl[0]:wsl[0] + 128].rearrange("p (dc e) -> p dc e", dc=NDC),
                io["gatew_t"].rearrange("(dc p) e -> p dc e", p=128), st="lm0")
            P.wait("tensor", P.tok("lm0"))
            for tc in range(NT):
                bank = 5
                mm_bank(bank, [(aB[:, wsl[0] + dc * 16: wsl[0] + dc * 16 + 16],
                                xn[:, dc * TC + tc * TCH: dc * TC + (tc + 1) * TCH],
                                pb[bank][0:16, 0:TCH]) for dc in range(NDC)])

                def _eg(tc=tc):
                    act("scalar", aF[0:16, scr + tc * TCH: scr + (tc + 1) * TCH],
                        pb[5][0:16, 0:TCH], AF.Sigmoid)
                evac(bank, "scalar", _eg)
            wg_e0_gate = P.tok("tensor")
            P.wait("tensor", P.tok("scalar"))
            P.wait("vector", P.tok("lgw"))

            sfc3 = aF[:, c_sfc:c_sfc + 16].rearrange("p (g e) -> p g e", g=NG)
            tmp3 = aF[:, c_tmp:c_tmp + 16].rearrange("p (g e) -> p g e", g=NG)
            selg3 = aF[:, c_selg:c_selg + 4].rearrange("p (g e) -> p g e", g=NG)
            ab3 = aF[:, c_ab:c_ab + 4].rearrange("p (g e) -> p g e", g=NG)
            cd3 = aF[:, c_cd:c_cd + 4].rearrange("p (g e) -> p g e", g=NG)
            mix3 = aF[:, c_mix:c_mix + 4].rearrange("p (g e) -> p g e", g=NG)
            for j in range(TC // 128):
                jc = slice(j * 128, (j + 1) * 128)
                bank = 6
                if bank_free[bank] is not None:
                    P.wait("tensor", bank_free[bank]); bank_free[bank] = None
                transpose(pb[6][:, 0:16], aF[0:16, scr + j * 128: scr + (j + 1) * 128],
                          identf[0:16, 0:16])
                bank_ready[bank] = P.mark("tensor")

                def _etm():
                    cp("vector", aF[:, c_sc:c_sc + 16], pb[6][:, 0:16])
                evac(bank, "vector", _etm)
                tt("vector", aF[:, c_sfc:c_sfc + 16], aF[:, c_sc:c_sc + 16],
                   aF[:, ebr:ebr + 16], ALU.add)
                tt("vector", ab3[:, :, 0:1], sfc3[:, :, 0:1], sfc3[:, :, 1:2], ALU.add)
                tt("vector", cd3[:, :, 0:1], sfc3[:, :, 2:3], sfc3[:, :, 3:4], ALU.add)
                tt("vector", mix3[:, :, 0:1], sfc3[:, :, 0:1], sfc3[:, :, 1:2], ALU.max)
                tt("vector", selg3[:, :, 0:1], sfc3[:, :, 2:3], sfc3[:, :, 3:4], ALU.max)
                tt("vector", aF[:, c_mix:c_mix + 4], aF[:, c_mix:c_mix + 4],
                   aF[:, c_selg:c_selg + 4], ALU.add)
                tt("vector", aF[:, c_gsc:c_gsc + 4], aF[:, c_ab:c_ab + 4],
                   aF[:, c_cd:c_cd + 4], ALU.max)
                tt("vector", aF[:, c_gsc:c_gsc + 4], aF[:, c_gsc:c_gsc + 4],
                   aF[:, c_mix:c_mix + 4], ALU.max)
                tt("vector", aF[:, c_m1:c_m1 + 1], aF[:, c_gsc:c_gsc + 1],
                   aF[:, c_gsc + 1:c_gsc + 2], ALU.max)
                tt("vector", aF[:, c_n1:c_n1 + 1], aF[:, c_gsc:c_gsc + 1],
                   aF[:, c_gsc + 1:c_gsc + 2], ALU.min)
                tt("vector", aF[:, c_m2:c_m2 + 1], aF[:, c_gsc + 2:c_gsc + 3],
                   aF[:, c_gsc + 3:c_gsc + 4], ALU.max)
                tt("vector", aF[:, c_n2:c_n2 + 1], aF[:, c_gsc + 2:c_gsc + 3],
                   aF[:, c_gsc + 3:c_gsc + 4], ALU.min)
                tt("vector", aF[:, c_t1:c_t1 + 1], aF[:, c_m1:c_m1 + 1],
                   aF[:, c_m2:c_m2 + 1], ALU.min)
                tt("vector", aF[:, c_t2:c_t2 + 1], aF[:, c_n1:c_n1 + 1],
                   aF[:, c_n2:c_n2 + 1], ALU.max)
                tt("vector", aF[:, c_thrg:c_thrg + 1], aF[:, c_t1:c_t1 + 1],
                   aF[:, c_t2:c_t2 + 1], ALU.max)
                ts("vector", aF[:, c_selg:c_selg + 4], aF[:, c_gsc:c_gsc + 4],
                   aF[:, c_thrg:c_thrg + 1], None, ALU.is_ge)
                tt("vector", tmp3[:, :, :], sfc3[:, :, :],
                   selg3[:, :, 0:1].broadcast_to((128, NG, PG)), ALU.mult)
                cp("vector", aF[:, c_rnd:c_rnd + 16], aF[:, c_tmp:c_tmp + 16])
                for k in range(4):
                    P.do("vector", lambda e: e.reduce_max(
                        aF[:, c_mx:c_mx + 1], aF[:, c_rnd:c_rnd + 16], AXX))
                    if k < 3:
                        ts("vector", aF[:, c_ab:c_ab + 16], aF[:, c_rnd:c_rnd + 16],
                           aF[:, c_mx:c_mx + 1], None, ALU.is_equal)
                        stt("vector", aF[:, c_rnd:c_rnd + 16], aF[:, c_ab:c_ab + 16],
                            -1e30, aF[:, c_rnd:c_rnd + 16], ALU.mult, ALU.add)
                ts("vector", aF[:, c_tmp:c_tmp + 16], aF[:, c_tmp:c_tmp + 16],
                   aF[:, c_mx:c_mx + 1], None, ALU.is_ge)
                tt("vector", aF[:, c_tmp:c_tmp + 16], aF[:, c_tmp:c_tmp + 16],
                   aF[:, c_sc:c_sc + 16], ALU.mult)
                P.do("vector", lambda e: e.reduce_sum(
                    aF[:, c_ws:c_ws + 1], aF[:, c_tmp:c_tmp + 16], AXX))
                ts("vector", aF[:, c_ws:c_ws + 1], aF[:, c_ws:c_ws + 1], 1e-20, None, ALU.add)
                recip("vector", aF[:, c_ws:c_ws + 1], aF[:, c_ws:c_ws + 1])
                ts("vector", aF[:, c_ws:c_ws + 1], aF[:, c_ws:c_ws + 1], 2.5, None, ALU.mult)
                ts("vector", aB[:, wtmb_o:wtmb_o + 16], aF[:, c_tmp:c_tmp + 16],
                   aF[:, c_ws:c_ws + 1], None, ALU.mult)
                twm = P.mark("vector")
                P.wait("tensor", twm)
                if bank_free[7] is not None:
                    P.wait("tensor", bank_free[7]); bank_free[7] = None
                transpose(pT[0:16, 0:128], aB[:, wtmb_o:wtmb_o + 16], identb[:, :])
                bank_ready[7] = P.mark("tensor")

                def _ewr(j=j):
                    cp("vector", aB[0:16, wrow_o + j * 128: wrow_o + (j + 1) * 128],
                       pT[0:16, 0:128])
                evac(7, "vector", _ewr)
            P.wait("sync", P.mark("vector"))
            dma("sync", wsc_scr[:, :], aB[0:16, wrow_o:wrow_o + TC], st="swsc")

            wg_e = [wg_e0_gate, None, None, None]
            wbcg = [None, None]
            tdown = None

            def eload(slot, src_ap, nin=NDC):
                if wg_e[slot] is not None:
                    P.wait("gpsimd", wg_e[slot])
                    wg_e[slot] = None
                dma("gpsimd",
                    aB[:, wsl[slot]:wsl[slot] + 4096].rearrange("p (b i) -> p b i", b=nin),
                    src_ap, st=f"lm{slot}")
                return P.tok(f"lm{slot}")

            for e in range(NEXP + 1):
                shared = (e == NEXP)
                sg, su, sd = (3 * e) % 4, (3 * e + 1) % 4, (3 * e + 2) % 4
                if shared:
                    tg = eload(sg, io["sg_t"].rearrange("(dc p) i -> p dc i", p=128))
                    tu = eload(su, io["su_t"].rearrange("(dc p) i -> p dc i", p=128))
                    td = eload(sd, io["sd_t"].rearrange("(ic p) o -> p ic o", p=128), nin=4)
                else:
                    tg = eload(sg, io["gp_t"][e].rearrange("(dc p) i -> p dc i", p=128))
                    tu = eload(su, io["up_t"][e].rearrange("(dc p) i -> p dc i", p=128))
                    td = eload(sd, io["dp_t"][e].rearrange("(ic p) o -> p ic o", p=128), nin=4)
                    if wbcg[e % 2] is not None:
                        P.wait("gpsimd", wbcg[e % 2])
                    P.wait("gpsimd", P.tok("swsc"))
                    dma("gpsimd", aB[:, wbc[e % 2]:wbc[e % 2] + TC],
                        wsc_scr[e:e + 1, :].broadcast_to((128, TC)), st="lwb")
                if tdown is not None:
                    P.wait("vector", tdown)
                P.wait("tensor", tg)
                P.wait("tensor", tu)
                if not shared:
                    P.wait("vector", P.tok("lwb"))

                def emit_downs(tc, tw, sd=sd):
                    P.wait("tensor", tw)
                    P.wait("tensor", td)
                    for oc in range(NDC):
                        bank = 4 + (oc + tc) % 4
                        mm_bank(bank, [(aB[:, wsl[sd] + ic * 1024 + oc * 128:
                                              wsl[sd] + ic * 1024 + oc * 128 + 128],
                                        aB[:, it_o + ic * TC + tc * TCH:
                                              it_o + ic * TC + (tc + 1) * TCH],
                                        pb[bank][:, 0:TCH]) for ic in range(4)])

                        def _ed(bank=bank, oc=oc, tc=tc):
                            xs = x_sb[:, oc * TC + tc * TCH: oc * TC + (tc + 1) * TCH]
                            tt("vector", xs, xs, pb[bank][:, 0:TCH], ALU.add)
                        evac(bank, "vector", _ed)

                pend = None
                for tc in range(NT):
                    for ic in range(4):
                        bg, bu = ic % 2, 2 + ic % 2
                        mm_bank(bg, [(aB[:, wsl[sg] + (dc * 4 + ic) * 128:
                                            wsl[sg] + (dc * 4 + ic) * 128 + 128],
                                      xn[:, dc * TC + tc * TCH: dc * TC + (tc + 1) * TCH],
                                      pb[bg][:, 0:TCH]) for dc in range(NDC)])

                        def _es(bg=bg):
                            P.wait("scalar", P.tok("vector"))
                            act("scalar", aB[:, tmpb_o:tmpb_o + TCH], pb[bg][:, 0:TCH],
                                AF.Silu)
                        evac(bg, "scalar", _es)
                        tsi = P.tok("scalar")
                        mm_bank(bu, [(aB[:, wsl[su] + (dc * 4 + ic) * 128:
                                            wsl[su] + (dc * 4 + ic) * 128 + 128],
                                      xn[:, dc * TC + tc * TCH: dc * TC + (tc + 1) * TCH],
                                      pb[bu][:, 0:TCH]) for dc in range(NDC)])

                        def _ei(bu=bu, ic=ic, tc=tc, tsi=tsi, e=e, shared=shared):
                            P.wait("vector", tsi)
                            dst = aB[:, it_o + ic * TC + tc * TCH: it_o + ic * TC + (tc + 1) * TCH]
                            tt("vector", dst, aB[:, tmpb_o:tmpb_o + TCH],
                               pb[bu][:, 0:TCH], ALU.mult)
                            if not shared:
                                tt("vector", dst, dst,
                                   aB[:, wbc[e % 2] + tc * TCH: wbc[e % 2] + (tc + 1) * TCH],
                                   ALU.mult)
                        evac(bu, "vector", _ei)
                    tw = P.tok("vector")
                    if pend is not None:
                        emit_downs(*pend)
                    pend = (tc, tw)
                emit_downs(*pend)
                tdown = P.tok("tensor")
                if not shared:
                    wbcg[e % 2] = P.tok("vector")
                for sl in (sg, su, sd):
                    wg_e[sl] = P.tok("tensor")
            DBG.update(moe_xn=xn_o, moe_wrow=wrow_o, moe_inter=it_o, moe_scr=scr)
            arB.pop(); arF.pop()

        if upto >= 1:
            l0_attn()
            P.barrier()
        if upto >= 2:
            tq_scratch = weight_quant()
            P.barrier()
            if os.environ.get("KWQONLY", "") != "1":
                l0_ffn(tq_scratch)
                P.barrier()
        if upto >= 3:
            l1_attn()
            P.barrier()
        if upto >= 4:
            l1_ffn()
            P.barrier()

        for tname, shape, fn in taps:
            src = fn(dict(x_sb=x_sb, aB=aB, aF=aF, dbgc=dbgc))
            q = "gpsimd" if src.dtype != F32 else "sync"
            dma(q, tapd[tname][:, :], src, st="dout" if q == "sync" else "dout2")
        for dc in range(NDC):
            dma("sync", out_fm[dc * 128:(dc + 1) * 128, :], x_sb[:, dc * TC:(dc + 1) * TC], st="dout")
        tout = P.tok("dout")
        P.wait("sync", tout)

        P.check()
        sems = {}
        for sname in list(ENGS) + sorted(P.dma_sems):
            sems[sname] = stx.enter_context(nc.semaphore(sname))
        P.replay(stx.enter_context(nc.Block()), sems)

    return nc


def stage_inputs(inputs, TC):
    B = 4
    NSEQ = 2 * TC
    x = np.asarray(inputs["L0_attn_x"], np.float32)

    def t(a):
        return np.ascontiguousarray(np.asarray(a, np.float32).T)

    rf = np.asarray(inputs["L0_attn_RF"], np.float32)
    rfblk = np.zeros((NPAIR, 128, 128), np.float32)
    for hp in range(NPAIR):
        rfblk[hp, 0:64, 0:64] = rf[2 * hp]
        rfblk[hp, 64:128, 64:128] = rf[2 * hp + 1]

    def pcol(v):
        return np.ascontiguousarray(np.asarray(v, np.float32).reshape(NDC, 128).T)

    base = {
        "wqkv": np.ascontiguousarray(np.asarray(inputs["L0_attn_W_qkv"], np.float32)),
        "rfblk": rfblk,
        "wout": np.ascontiguousarray(np.asarray(inputs["L0_attn_W_out"], np.float32)),
        "bout": pcol(inputs["L0_attn_b_out"]),
        "wg_t": t(inputs["L0_ffn_w_gate"]),
        "wu_t": t(inputs["L0_ffn_w_up"]),
        "wd_t": t(inputs["L0_ffn_w_down"]),
        "lng": pcol(inputs["L1_attn_ln_g"]),
        "lnb": pcol(inputs["L1_attn_ln_b"]),
        "w1": np.ascontiguousarray(np.asarray(inputs["L1_attn_w1_w"], np.float32)),
        "b1": pcol(inputs["L1_attn_w1_b"]),
        "v1": np.ascontiguousarray(np.asarray(inputs["L1_attn_v1_w"], np.float32)),
        "bv": pcol(inputs["L1_attn_v1_b"]),
        "w2": np.ascontiguousarray(np.asarray(inputs["L1_attn_w2_w"], np.float32)),
        "b2": pcol(inputs["L1_attn_w2_b"]),
        "wdt": np.ascontiguousarray(np.asarray(inputs["L1_attn_W_dt"], np.float32)),
        "bdt": pcol(inputs["L1_attn_b_dt"]),
        "convw": np.ascontiguousarray(
            np.asarray(inputs["L1_attn_conv_w"], np.float32).reshape(NDC, 128, 3)
            .transpose(1, 0, 2).reshape(128, NDC * 3)),
        "dskip": pcol(inputs["L1_attn_D_skip"]),
        "wb": np.ascontiguousarray(np.asarray(inputs["L1_attn_W_B"], np.float32)),
        "wc": np.ascontiguousarray(np.asarray(inputs["L1_attn_W_C"], np.float32)),
        "gatew_t": t(inputs["L1_ffn_gate_w"]),
        "ebias": np.asarray(inputs["L1_ffn_e_bias"], np.float32).reshape(1, NEXP),
        "gp_t": np.ascontiguousarray(np.asarray(inputs["L1_ffn_gate_proj"], np.float32).transpose(0, 2, 1)),
        "up_t": np.ascontiguousarray(np.asarray(inputs["L1_ffn_up_proj"], np.float32).transpose(0, 2, 1)),
        "dp_t": np.ascontiguousarray(np.asarray(inputs["L1_ffn_down_proj"], np.float32).transpose(0, 2, 1)),
        "sg_t": t(inputs["L1_ffn_shared_gate_w"]),
        "su_t": t(inputs["L1_ffn_shared_up_w"]),
        "sd_t": t(inputs["L1_ffn_shared_down_w"]),
    }
    alog3 = np.asarray(inputs["L1_attn_A_log"], np.float32).reshape(NDC, 128, NST)
    in_maps = []
    for c in range(8):
        b, h = c // 2, c % 2
        m = dict(base)
        m["x_fm"] = np.ascontiguousarray(x[b, h * TC:(h + 1) * TC, :].T)
        meta = np.zeros((128, 4), np.float32)
        meta[:, 0] = 1.0 if h == 1 else 0.0
        meta[:, 1] = 1.0 if h == 0 else 0.0
        m["meta"] = meta
        m["alog"] = np.ascontiguousarray(
            np.roll(alog3, -4 * h, axis=0).transpose(1, 0, 2).reshape(128, NDC * NST))
        in_maps.append(m)
    return in_maps


def assemble(res, TC):
    x = np.zeros((4, 2 * TC, D), np.float32)
    for c in range(8):
        b, h = c // 2, c % 2
        x[b, h * TC:(h + 1) * TC, :] = res.results[c]["out_fm"].T
    return x


def kernel(**inputs):
    TC = 2048
    nc = build(TC, upto=4)
    in_maps = stage_inputs(inputs, TC)
    res = run_bass_kernel_spmd(nc, in_maps, core_ids=list(range(8)))
    return assemble(res, TC)
